# version 25
# baseline (speedup 1.0000x reference)
import contextlib
import numpy as np
import concourse.bass as bass
import concourse.mybir as mybir
from concourse.bass_utils import run_bass_kernel_spmd

F32 = mybir.dt.float32
I32 = mybir.dt.int32
AF = mybir.ActivationFunctionType
ALU = mybir.AluOpType
AX = mybir.AxisListType

H, CH, NST, D = 16, 64, 64, 1024
HID, MOE, NEXP, EXPH = 2048, 64, 8, 1024
NCORE = 8
T_CORE = 2048
WARM = 512
T_TILE = 512
L = 128
CPT = T_TILE // L
N_WARM_TILES = WARM // T_TILE
N_MAIN_TILES = T_CORE // T_TILE
GRP = 512
GPT = T_TILE // GRP
DEBUG = False


class PB:

    ENGS = ("t", "v", "s", "g", "y")

    def __init__(self, nc, es):
        self.nc = nc
        self.prog = {e: [] for e in self.ENGS}
        self.sem = {k: es.enter_context(nc.semaphore("sem_" + k))
                    for k in ("t", "v", "s", "g", "A", "B")}
        self.cnt = {k: 0 for k in ("t", "v", "s", "g", "A", "B")}
        self.ldwf_ins = []

    def _skey(self, eng):
        return {"t": "t", "v": "v", "s": "s", "g": "g"}[eng]

    def emit(self, eng, build, waits=(), dma=None, inc=True, ldwf=False):
        sem, cnt = self.sem, self.cnt
        wl = [(k, c) for (k, c) in waits if c is not None and c > 0]
        if dma is None:
            key, amt = self._skey(eng), 1
        else:
            key, amt = dma, 16
        if inc:
            cnt[key] += amt
            target = cnt[key]
        else:
            target = None

        def fn(e, build=build, wl=wl, key=key, inc=inc, ldwf=ldwf, self_=self):
            for (k, c) in wl:
                e.wait_ge(sem[k], c)
            ins = build(e)
            if inc:
                ins.then_inc(sem[key], 1 if key not in ("A", "B") else 16)
            if ldwf:
                self_.ldwf_ins.append(ins.ins)
        self.prog[eng].append(fn)
        return (key, target) if inc else None

    def wait_only(self, eng, waits):
        sem = self.sem
        wl = [(k, c) for (k, c) in waits if c > 0]
        if not wl:
            return

        def fn(e, wl=wl):
            for (k, c) in wl:
                e.wait_ge(sem[k], c)
        self.prog[eng].append(fn)

    def barrier(self):
        snap = dict(self.cnt)
        for eng in ("t", "v", "s", "g", "y"):
            self.wait_only(eng, [(k, snap[k]) for k in ("t", "v", "s", "g", "A", "B")])


class Bank:
    def __init__(self, tensor):
        self.t = tensor
        self.wr = None
        self.rd = {}


def mk_mm(pb):
    def mm(bank, out_ap, lhsT, rhs, start, stop, waits=(), tp=None):
        w = [x for x in waits if x is not None]
        if start:
            w += list(bank.rd.items())
            if bank.wr is not None:
                w.append(bank.wr)

        def b(e, out_ap=out_ap, lhsT=lhsT, rhs=rhs, start=start, stop=stop, tp=tp):
            return e.matmul(out_ap, lhsT, rhs, start=start, stop=stop,
                            tile_position=tp)
        h = pb.emit("t", b, waits=w, inc=stop)
        if stop:
            bank.wr = h
            bank.rd = {}
        return h
    return mm


def mk_tr(pb):
    def tr(bank, out_ap, in_ap, iden_ap, waits=()):
        w = list(waits) + list(bank.rd.items())
        if bank.wr is not None:
            w.append(bank.wr)

        def b(e, out_ap=out_ap, in_ap=in_ap, iden_ap=iden_ap):
            return e.transpose(out_ap, in_ap, iden_ap)
        h = pb.emit("t", b, waits=w)
        bank.wr = h
        bank.rd = {}
        return h
    return tr


def rd_bank(pb, eng, bank, build, waits=()):
    w = list(waits)
    if bank.wr is not None:
        w.append(bank.wr)
    h = pb.emit(eng, build, waits=w)
    bank.rd[h[0]] = h[1]
    return h


def build_nc():
    nc = bass.Bass("TRN2", target_bir_lowering=False, debug=False)
    din = {}
    for name, shape in [
        ("xin", [WARM + T_CORE, D]), ("wmask", [128, 1]),
        ("wblk", [8, 128, 512]), ("astat", [8, 128, 16]),
        ("bxbcg", [128, 32]), ("bacol", [16, 1]),
        ("woutT", [D, D]), ("boutc", [128, 8]),
        ("encw", [D, HID]), ("encbc", [128, 16]),
        ("s2cd", [HID, 128]), ("s2cbc", [128, 1]),
        ("rw1", [64, 64]), ("rb1c", [64, 1]), ("rw2", [64, 8]),
        ("rb2rep", [128, 8]),
        ("ew1p", [4, 128, EXPH]), ("eb1c", [128, 64]),
        ("ew2s", [NEXP, EXPH, MOE]), ("eb2m", [8, 64]),
        ("c2sr", [128, HID]), ("c2sbc", [128, 16]),
        ("decw", [HID, D]), ("decbc", [128, 8]),
        ("lngrep", [128, D]), ("lnbrep", [128, D]),
        ("iden", [128, 128]), ("tripos", [128, 128]), ("tripost", [128, 128]),
        ("iota8", [128, 8]), ("big8", [128, 8]), ("neg1t", [128, 8]),
        ("ones16", [16, 128]),
    ]:
        din[name] = nc.dram_tensor(name, shape, F32, kind="ExternalInput").ap()
    dout = nc.dram_tensor("out", [T_CORE, D], F32, kind="ExternalOutput").ap()
    BF16 = mybir.dt.bfloat16
    scr = {}
    for nm, shape in [("ench", [D, HID]), ("encl", [D, HID]),
                      ("dech", [HID, D]), ("decl", [HID, D]),
                      ("wouth", [D, D]), ("woutl", [D, D]),
                      ("ew1h", [4, 128, EXPH]), ("ew1l", [4, 128, EXPH])]:
        scr[nm] = nc.dram_tensor("scr_" + nm, shape, BF16, kind="Internal").ap()
    dbg = {}
    if DEBUG:
        for nm, shp in [("h0", [128, 512]), ("xn", [128, 4096]),
                        ("proj", [128, 4096]), ("crow", [16, 128]),
                        ("cols", [128, 64]), ("ytm", [128, 1024]),
                        ("h1", [128, 512]), ("asb", [16, 128]),
                        ("stm", [128, 4096]), ("cont", [128, 512]),
                        ("spk", [128, 4096]), ("comb", [128, 32]),
                        ("moe", [128, 256]), ("mfd", [128, 512]),
                        ("rat", [128, 4096]), ("dec", [128, 2048]),
                        ("xtm", [128, 2048]), ("ygf", [128, 2048]),
                        ("lncols", [128, 8]), ("xb", [128, 1024]),
                        ("eu", [128, 8]), ("logit", [128, 8]), ("msk", [128, 16])]:
            dbg[nm] = nc.dram_tensor("dbg_" + nm, shp, F32, kind="ExternalOutput").ap()

    es = contextlib.ExitStack()
    sb = {}

    def S(name, shape, dt=F32):
        sb[name] = es.enter_context(nc.sbuf_tensor("sb_" + name, shape, dt))
        return sb[name]

    BF = mybir.dt.bfloat16
    S("wblkh", [128, 8 * 512], BF)
    S("wblkl", [128, 8 * 512], BF)
    S("astath", [128, 8 * 16], BF)
    S("astatl", [128, 8 * 16], BF)
    S("bxbcg", [128, 32]); S("bacol", [16, 1])
    S("encbc", [128, 16]); S("s2ch", [128, 16 * 128], BF); S("s2cl", [128, 16 * 128], BF)
    S("s2cbc", [128, 1])
    S("rw1", [64, 64]); S("rb1c", [64, 1]); S("rw2", [64, 8])
    S("rb2rep", [128, 8]); S("eb1c", [128, 64]); S("eb2m", [8, 64])
    S("c2srh", [128, HID], BF); S("c2srl", [128, HID], BF); S("c2sbc", [128, 16])
    S("decbc", [128, 8]); S("boutc", [128, 8])
    S("lngrep", [128, D]); S("lnbrep", [128, D])
    S("iden", [128, 128]); S("tripos", [128, 128]); S("tripost", [128, 128])
    S("iota8", [128, 8]); S("big8", [128, 8]); S("neg1t", [128, 8])
    S("ones16", [16, 128]); S("wmask", [128, 1])
    S("wstrb", [128, 32768], BF)
    S("xnfmh", [128, 8 * T_TILE], BF)
    S("xnfml", [128, 8 * T_TILE], BF)
    S("contb", [128, 2 * T_TILE], BF)
    S("stm", [128, CPT * D])
    S("htst", [128, 512])
    S("tr", [128, 13440])
    S("asb", [16, L]); S("sp16", [16, L]); S("crow", [16, L])
    S("frow", [16, L]); S("erow", [16, L]); S("dfree", [16, 128])
    S("cT", [128, 16]); S("fT", [128, 16]); S("eT", [128, 16]); S("dT", [128, 16])
    S("mucol", [128, 2]); S("s2col", [128, 2]); S("m2col", [128, 2])
    S("varcol", [128, 2]); S("rstdcol", [128, 2])
    S("bnst", [128, 24]); S("bnag", [128, 16])
    S("rhsb", [64, GRP])
    S("combm", [128, CPT * 8], I32)
    S("comb", [128, CPT * 8])
    S("combT", [8, CPT * 128])
    S("m1c", [128, 1]); S("m2c", [128, 1]); S("i1c", [128, 1]); S("i2c", [128, 1])
    S("dnc", [128, 1]); S("w1c", [128, 1]); S("w2c", [128, 1])
    S("logit", [128, 8]); S("eu", [128, 8]); S("eu2", [128, 8]); S("selb", [128, 8])
    S("clast", [16, 1]); S("dcol", [16, 1])

    pb = PB(nc, es)
    mm = mk_mm(pb)
    tr = mk_tr(pb)

    def mm3(bank, out_ap, sth, stl, mvh, mvl, start, stop, waits=()):
        w = [x for x in waits if x is not None]
        if start:
            w += list(bank.rd.items())
            if bank.wr is not None:
                w.append(bank.wr)
        if sth.base_partition() != 0:
            pb.emit("t", lambda e, out_ap=out_ap, sth=sth, mvh=mvh, start=start: e.matmul(
                out_ap, sth, mvh, start=start, stop=False), waits=w, inc=False)
            pb.emit("t", lambda e, out_ap=out_ap, sth=sth, mvl=mvl: e.matmul(
                out_ap, sth, mvl, start=False, stop=False), inc=False)
            h = pb.emit("t", lambda e, out_ap=out_ap, stl=stl, mvh=mvh, stop=stop: e.matmul(
                out_ap, stl, mvh, start=False, stop=stop), inc=stop)
            if stop:
                bank.wr = h
                bank.rd = {}
            return h
        pb.emit("t", lambda e, sth=sth: e.ldweights(sth), waits=w, inc=False)
        pb.emit("t", lambda e, out_ap=out_ap, sth=sth, mvh=mvh, start=start: e.matmul(
            out_ap, sth, mvh, start=start, stop=False), inc=False, ldwf=True)
        pb.emit("t", lambda e, out_ap=out_ap, sth=sth, mvl=mvl: e.matmul(
            out_ap, sth, mvl, start=False, stop=False), inc=False, ldwf=True)
        pb.emit("t", lambda e, stl=stl: e.ldweights(stl), inc=False)
        h = pb.emit("t", lambda e, out_ap=out_ap, stl=stl, mvh=mvh, stop=stop: e.matmul(
            out_ap, stl, mvh, start=False, stop=stop), inc=stop, ldwf=True)
        if stop:
            bank.wr = h
            bank.rd = {}
        return h
    banks = [Bank(es.enter_context(nc.psum_tensor("pb%d" % i, [128, 512], F32)))
             for i in range(8)]

    TRv = sb["tr"]
    A_XB, A_X2, A_XN = 0, 1024, 2048
    B_PROJ = 0
    B_XT = 4096
    B_YTM = 6144
    B_YG = 7168
    B_RT, B_GB, B_MT = 11264, 11392, 11520
    B_YT2, B_BW = 11648, 11712
    B_WSC = 11264
    C_RS2 = 11840
    C_SPB = 12352
    C_CONT = 12864
    C_EH = 0
    C_MOE = 9216
    C_MFD = 9472
    C_MTMP = 9984
    C_RAT = 0
    C_RS = 8192
    C_DTM = 8704
    C_G, C_SG, C_XR, C_OB = 0, 1024, 2048, 3072
    C_EO = 13056

    iden = sb["iden"]

    def evac(eng, bank, psrc, dst, func=None, bias=None, scale=1.0, waits=()):
        if eng == "s":
            def b(e, dst=dst, psrc=psrc, func=func, bias=bias, scale=scale):
                return e.activation(out=dst, in_=psrc,
                                    func=(func or AF.Identity),
                                    bias=(0.0 if bias is None else bias),
                                    scale=scale)
        else:
            def b(e, dst=dst, psrc=psrc):
                return e.tensor_copy(dst, psrc)
        return rd_bank(pb, eng, bank, b, waits=waits)

    ld = {}
    init_loads = []
    for name in ["bxbcg", "bacol", "encbc", "s2cbc", "rw1", "rb1c", "rw2",
                 "rb2rep", "eb1c", "eb2m", "c2sbc", "decbc", "boutc",
                 "lngrep", "lnbrep", "iden", "tripos", "tripost", "iota8", "big8",
                 "neg1t", "ones16", "wmask"]:
        init_loads.append((name, sb[name], [(din[name][:, :], sb[name][:, :])]))

    for name, _sbuf, pairs in init_loads:
        for (src, dst) in pairs:
            h = pb.emit("y", lambda e, dst=dst, src=src: e.dma_start(out=dst, in_=src),
                        dma="A")
        ld[name] = h
    init_done = pb.cnt["A"]
    pb.barrier()
    pb.emit("v", lambda e: e.memset(sb["htst"][:, :], 0.0))
    pb.emit("v", lambda e: e.memset(TRv[:, B_BW:B_BW + 128], 0.0))
    pb.barrier()

    wsb = sb["wstrb"]
    wsb_f32 = wsb[:, :].bitcast(F32)

    def split_to(dst_h, dst_l, f32_ap, hload):
        h1 = pb.emit("s", lambda e, dst_h=dst_h, f32_ap=f32_ap: e.activation(
            out=dst_h, in_=f32_ap, func=AF.Identity), waits=[hload])
        h2 = pb.emit("g", lambda e, dst_l=dst_l, f32_ap=f32_ap, dst_h=dst_h: e.tensor_sub(
            dst_l, f32_ap, dst_h), waits=[h1])
        return h2

    ph0 = {}
    for p in range(8):
        pp = p % 2
        of = pp * 1024
        wd = [h for h in (ph0.get(pp),) if h]
        hl = pb.emit("y", lambda e, p=p, of=of: e.dma_start(out=TRv[:, of:of + 512], in_=din["wblk"][p, :, :]),
                     waits=wd, dma="A")
        hs = split_to(sb["wblkh"][:, p * 512:(p + 1) * 512], sb["wblkl"][:, p * 512:(p + 1) * 512],
                      TRv[:, of:of + 512], hl)
        hl2 = pb.emit("y", lambda e, p=p, of=of: e.dma_start(out=TRv[:, of + 512:of + 528], in_=din["astat"][p, :, :]),
                      waits=[hs], dma="A")
        hs2 = split_to(sb["astath"][:, p * 16:(p + 1) * 16], sb["astatl"][:, p * 16:(p + 1) * 16],
                       TRv[:, of + 512:of + 528], hl2)
        ph0[pp] = hs2
    ph1 = {}
    for k in range(16):
        kp = k % 2
        of = kp * 128
        wd = [h for h in (ph1.get(kp),) if h]
        hl = pb.emit("y", lambda e, k=k, of=of: e.dma_start(out=TRv[:, 2048 + of:2048 + of + 128],
                                                            in_=din["s2cd"][k * 128:(k + 1) * 128, :]),
                     waits=wd, dma="A")
        hs = split_to(sb["s2ch"][:, k * 128:(k + 1) * 128], sb["s2cl"][:, k * 128:(k + 1) * 128],
                      TRv[:, 2048 + of:2048 + of + 128], hl)
        ph1[kp] = hs
    hl = pb.emit("y", lambda e: e.dma_start(out=TRv[:, 2432:4480], in_=din["c2sr"][:, :]), dma="A")
    hs = split_to(sb["c2srh"][:, :], sb["c2srl"][:, :], TRv[:, 2432:4480], hl)
    big_specs = [("enc", din["encw"], scr["ench"], scr["encl"], 8, 2048),
                 ("wout", din["woutT"], scr["wouth"], scr["woutl"], 8, 1024),
                 ("dec", din["decw"], scr["dech"], scr["decl"], 16, 1024)]
    jobs = []
    for (kind, dsrc, dh, dl, nchunk, cw) in big_specs:
        for k in range(nchunk):
            jobs.append((kind, dsrc[k * 128:(k + 1) * 128, :], dh[k * 128:(k + 1) * 128, :],
                         dl[k * 128:(k + 1) * 128, :], cw))
    for q in range(4):
        jobs.append(("ew1", din["ew1p"][q, :, :], scr["ew1h"][q, :, :], scr["ew1l"][q, :, :], 1024))
    store_h = {}
    split_ph = {}
    split_pos = [0]

    def emit_split_jobs(n):
        i0 = split_pos[0]
        for i in range(i0, min(i0 + n, len(jobs))):
            kind, jsrc, jdh, jdl, cw = jobs[i]
            p = i % 2
            trof = 8192 + p * 2048
            wbof = p * 4096
            wd = [h for h in (split_ph.get(("sp", p)),) if h]
            hl = pb.emit("y", lambda e, jsrc=jsrc, trof=trof, cw=cw: e.dma_start(
                out=wsb_f32[:, trof:trof + cw], in_=jsrc), waits=wd, dma="A")
            ws = [hl] + [h for h in (split_ph.get(("st", p)),) if h]
            h1 = pb.emit("s", lambda e, wbof=wbof, trof=trof, cw=cw: e.activation(
                out=wsb[:, wbof:wbof + cw], in_=wsb_f32[:, trof:trof + cw],
                func=AF.Identity), waits=ws)
            h2 = pb.emit("g", lambda e, wbof=wbof, trof=trof, cw=cw: e.tensor_sub(
                wsb[:, wbof + cw:wbof + 2 * cw], wsb_f32[:, trof:trof + cw],
                wsb[:, wbof:wbof + cw]), waits=[h1])
            split_ph[("sp", p)] = h2
            pb.emit("g", lambda e, jdh=jdh, wbof=wbof, cw=cw: e.dma_start(
                out=jdh, in_=wsb[:, wbof:wbof + cw]), waits=[h2], dma="B")
            hs2 = pb.emit("g", lambda e, jdl=jdl, wbof=wbof, cw=cw: e.dma_start(
                out=jdl, in_=wsb[:, wbof + cw:wbof + 2 * cw]), waits=[h2], dma="B")
            split_ph[("st", p)] = hs2
            store_h[kind] = hs2
        split_pos[0] = min(i0 + n, len(jobs))
    pb.wait_only("y", [h for h in (ph0.get(0), ph0.get(1), ph1.get(0), ph1.get(1), hs) if h])
    pb.barrier()

    def load_wout():
        h = None
        for k in range(8):
            pb.emit("y", lambda e, k=k: e.dma_start(
                out=wsb[:, k * 1024:(k + 1) * 1024],
                in_=scr["wouth"][k * 128:(k + 1) * 128, :]), dma="A",
                waits=([store_h["wout"]] if (k == 0 and "wout" in store_h) else []))
            h = pb.emit("y", lambda e, k=k: e.dma_start(
                out=wsb[:, 8192 + k * 1024:8192 + (k + 1) * 1024],
                in_=scr["woutl"][k * 128:(k + 1) * 128, :]), dma="A")
        return h

    def load_enc():
        h = None
        for k in range(8):
            pb.emit("y", lambda e, k=k: e.dma_start(
                out=wsb[:, k * 2048:(k + 1) * 2048],
                in_=scr["ench"][k * 128:(k + 1) * 128, :]), dma="A",
                waits=([store_h["enc"]] if (k == 0 and "enc" in store_h) else []))
            h = pb.emit("y", lambda e, k=k: e.dma_start(
                out=wsb[:, 16384 + k * 2048:16384 + (k + 1) * 2048],
                in_=scr["encl"][k * 128:(k + 1) * 128, :]), dma="A")
        return h

    def load_experts():
        for q in range(4):
            pb.emit("y", lambda e, q=q: e.dma_start(
                out=wsb[:, q * 1024:(q + 1) * 1024], in_=scr["ew1h"][q, :, :]), dma="A",
                waits=([store_h["ew1"]] if (q == 0 and "ew1" in store_h) else []))
            pb.emit("y", lambda e, q=q: e.dma_start(
                out=wsb[:, 4096 + q * 1024:4096 + (q + 1) * 1024], in_=scr["ew1l"][q, :, :]), dma="A")
        h = None
        for ee in range(NEXP):
            for k in range(8):
                h = pb.emit("y", lambda e, ee=ee, k=k: e.dma_start(
                    out=wsb_f32[:, 4096 + ee * 512 + k * 64: 4096 + ee * 512 + (k + 1) * 64],
                    in_=din["ew2s"][ee, k * 128:(k + 1) * 128, :]), dma="A")
        return h

    def load_dec():
        h = None
        for k in range(16):
            pb.emit("y", lambda e, k=k: e.dma_start(
                out=wsb[:, k * 1024:(k + 1) * 1024],
                in_=scr["dech"][k * 128:(k + 1) * 128, :]), dma="A",
                waits=([store_h["dec"]] if (k == 0 and "dec" in store_h) else []))
            h = pb.emit("y", lambda e, k=k: e.dma_start(
                out=wsb[:, 16384 + k * 1024:16384 + (k + 1) * 1024],
                in_=scr["decl"][k * 128:(k + 1) * 128, :]), dma="A")
        return h

    def phase_A_chunk(xin_row0, c, prev, base=0, bksel=(0, 1), xn_guard=None):
        lastev = None
        if True:
            cp = c % 2
            XB, XN = base + cp * 2048, base + cp * 2048 + 1024
            mcol = sb["mucol"][:, cp:cp + 1]
            vcol = sb["varcol"][:, cp:cp + 1]
            rcol = sb["rstdcol"][:, cp:cp + 1]
            bnst = sb["bnst"][:, cp * 12:(cp + 1) * 12]
            ag = sb["bnag"][:, cp * 8:(cp + 1) * 8]
            w_dma = [h for h in (prev.get(("xn", cp)),) if h]
            hdma = pb.emit("y", lambda e, XB=XB, c=c: e.dma_start(
                out=TRv[:, XB:XB + 1024],
                in_=din["xin"][xin_row0 + c * L: xin_row0 + (c + 1) * L, :]),
                waits=w_dma, dma="A")
            pb.emit("v", lambda e, XB=XB, bnst=bnst: e.bn_stats(bnst[:, 0:6], TRv[:, XB:XB + 512]),
                    waits=[hdma])
            pb.emit("v", lambda e, XB=XB, bnst=bnst: e.bn_stats(bnst[:, 6:12], TRv[:, XB + 512:XB + 1024]))
            pb.emit("v", lambda e, ag=ag, bnst=bnst: e.bn_aggr(ag[:, 0:2], bnst[:, 0:6]))
            ha2 = pb.emit("v", lambda e, ag=ag, bnst=bnst: e.bn_aggr(ag[:, 2:4], bnst[:, 6:12]))
            hsm = pb.emit("v", lambda e, ag=ag: e.tensor_add(ag[:, 4:6], ag[:, 0:2], ag[:, 2:4]),
                          waits=[ha2])
            hdf = pb.emit("v", lambda e, ag=ag: e.tensor_sub(ag[:, 6:7], ag[:, 0:1], ag[:, 2:3]),
                          waits=[ha2])
            hmu = pb.emit("v", lambda e, mcol=mcol, ag=ag: e.tensor_single_scalar(
                mcol, ag[:, 4:5], 0.5, ALU.mult), waits=[hsm])
            hd2 = pb.emit("v", lambda e, ag=ag: e.tensor_mul(ag[:, 7:8], ag[:, 6:7], ag[:, 6:7]),
                          waits=[hdf])
            hv2 = pb.emit("v", lambda e, vcol=vcol, ag=ag: e.scalar_tensor_tensor(
                vcol, ag[:, 7:8], 0.5, ag[:, 5:6], ALU.mult, ALU.add), waits=[hd2, hsm])
            hv = pb.emit("v", lambda e, vcol=vcol: e.tensor_single_scalar(
                vcol, vcol, 2e-5, ALU.add), waits=[hv2])
            hsd = pb.emit("s", lambda e, rcol=rcol, vcol=vcol: e.activation(
                out=rcol, in_=vcol, func=AF.Sqrt, scale=0.5), waits=[hv])
            hrs = pb.emit("v", lambda e, rcol=rcol: e.reciprocal(out=rcol, in_=rcol),
                          waits=[hsd])
            w_xn = [hrs, hmu] + ([prev.get(("tr", cp))] if prev.get(("tr", cp)) else [])
            hxn = pb.emit("v", lambda e, XB=XB, XN=XN, mcol=mcol, rcol=rcol: e.tensor_scalar(
                TRv[:, XN:XN + 1024], TRv[:, XB:XB + 1024], mcol[:, 0:1], rcol[:, 0:1],
                ALU.subtract, ALU.mult), waits=w_xn)
            prev[("xn", cp)] = hxn
            htr = None
            for f in range(8):
                bk = banks[bksel[f % 2]]
                htr = tr(bk, bk.t[:, 0:128], TRv[:, XN + f * 128:XN + (f + 1) * 128],
                         iden[:, :], waits=[hxn])
                xh = sb["xnfmh"][:, f * T_TILE + c * L: f * T_TILE + (c + 1) * L]
                xl = sb["xnfml"][:, f * T_TILE + c * L: f * T_TILE + (c + 1) * L]
                hh1 = evac("v", bk, bk.t[:, 0:128], xh,
                           waits=([xn_guard] if (xn_guard and f == 0) else []))
                lastev = rd_bank(pb, "v", bk, lambda e, xl=xl, bk=bk, xh=xh: e.tensor_sub(
                    xl, bk.t[:, 0:128], xh), waits=[hh1])
            prev[("tr", cp)] = htr
        return lastev

    def phase_A(xin_row0):
        prev = {}
        lastev = None
        for c in range(CPT):
            lastev = phase_A_chunk(xin_row0, c, prev)
        return lastev

    def proj_chunk(c, warm, snap):
        groups = [0, 1] if warm else [0, 1, 2, 3]
        sw = [h for h in (snap.get("t"), snap.get("v")) if h]
        evh = {}
        for p in range(8):
            bk = banks[p % 2]
            movh = sb["xnfmh"][:, p * T_TILE + c * L: p * T_TILE + (c + 1) * L]
            movl = sb["xnfml"][:, p * T_TILE + c * L: p * T_TILE + (c + 1) * L]
            for gi, g in enumerate(groups):
                mm3(bk, bk.t[:, g * 128:(g + 1) * 128],
                    sb["wblkh"][:, p * 512 + g * 128: p * 512 + (g + 1) * 128],
                    sb["wblkl"][:, p * 512 + g * 128: p * 512 + (g + 1) * 128],
                    movh, movl, start=(gi == 0), stop=(gi == len(groups) - 1),
                    waits=sw if gi == 0 else ())
            for g in groups:
                fn = AF.Silu if g < 3 else AF.Identity
                evh[(p, g)] = evac("s", bk, bk.t[:, g * 128:(g + 1) * 128],
                                   TRv[:, B_PROJ + p * 512 + g * 128: B_PROJ + p * 512 + (g + 1) * 128],
                                   func=fn, bias=sb["bxbcg"][:, p * 4 + g: p * 4 + g + 1],
                                   waits=sw)
        bkA = banks[2]
        for p in range(8):
            movh = sb["xnfmh"][:, p * T_TILE + c * L: p * T_TILE + (c + 1) * L]
            movl = sb["xnfml"][:, p * T_TILE + c * L: p * T_TILE + (c + 1) * L]
            for pi, (st_, mv_) in enumerate([
                    (sb["astath"][:, p * 16:(p + 1) * 16], movh),
                    (sb["astath"][:, p * 16:(p + 1) * 16], movl),
                    (sb["astatl"][:, p * 16:(p + 1) * 16], movh)]):
                mm(bkA, bkA.t[0:16, 0:128], st_, mv_,
                   start=(p == 0 and pi == 0), stop=(p == 7 and pi == 2))
        ha = evac("s", bkA, bkA.t[0:16, 0:128], sb["asb"][:, :],
                  func=AF.Identity, bias=sb["bacol"][:, 0:1], waits=sw)
        return evh, ha

    def decay_chunk(ha, warm, snap):
        ts_ = [snap["t"]] if snap.get("t") else []
        ss_ = [snap["s"]] if snap.get("s") else []
        vs_ = [snap["v"]] if snap.get("v") else []
        h1 = pb.emit("s", lambda e: e.activation(out=sb["sp16"][:, :], in_=sb["asb"][:, :], func=AF.Abs),
                     waits=[ha] + vs_)
        h2 = pb.emit("s", lambda e: e.activation(out=sb["sp16"][:, :], in_=sb["sp16"][:, :], func=AF.Exp, scale=-1.0),
                     waits=[h1])
        h3 = pb.emit("s", lambda e: e.activation(out=sb["sp16"][:, :], in_=sb["sp16"][:, :], func=AF.Ln, bias=1.0),
                     waits=[h2])
        h4 = pb.emit("v", lambda e: e.tensor_single_scalar(sb["crow"][:, :], sb["asb"][:, :], 0.0, ALU.max),
                     waits=[ha] + ts_)
        h5 = pb.emit("v", lambda e: e.tensor_add(sb["sp16"][:, :], sb["sp16"][:, :], sb["crow"][:, :]),
                     waits=[h3, h4])
        h6 = pb.emit("v", lambda e: e.tensor_tensor_scan(sb["crow"][:, :], sb["ones16"][:, :], sb["sp16"][:, :],
                                                         0.0, ALU.mult, ALU.add), waits=[h5])
        h7 = pb.emit("v", lambda e: e.tensor_copy(sb["clast"][:, :], sb["crow"][:, L - 1:L]), waits=[h6] + ss_)
        h8 = pb.emit("v", lambda e: e.tensor_scalar(sb["frow"][:, :], sb["crow"][:, :], sb["clast"][:, 0:1], None,
                                                    ALU.subtract), waits=[h7] + ts_)
        h9 = pb.emit("s", lambda e: e.activation(out=sb["frow"][:, :], in_=sb["frow"][:, :], func=AF.Exp),
                     waits=[h8])
        hD = pb.emit("s", lambda e: e.activation(out=sb["dcol"][:, :], in_=sb["clast"][:, :], func=AF.Exp, scale=-1.0),
                     waits=[h7])
        hDf = pb.emit("v", lambda e: e.tensor_copy(sb["dfree"][:, :], sb["dcol"][:, 0:1].broadcast_to([16, 128])),
                      waits=[hD] + ts_)
        hE = None
        if not warm:
            hE = pb.emit("s", lambda e: e.activation(out=sb["erow"][:, :], in_=sb["crow"][:, :], func=AF.Exp,
                                                     scale=-1.0), waits=[h6] + ts_)
        bk = banks[2]
        tr(bk, bk.t[0:128, 128:144], sb["frow"][:, :], iden[0:16, 0:16], waits=[h9])
        hfT = evac("v", bk, bk.t[0:128, 128:144], sb["fT"][:, :])
        tr(bk, bk.t[0:128, 144:160], sb["dfree"][:, :], iden[0:16, 0:16], waits=[hDf])
        hdT = evac("v", bk, bk.t[0:128, 144:160], sb["dT"][:, :])
        hcT = heT = None
        if not warm:
            tr(bk, bk.t[0:128, 160:176], sb["crow"][:, :], iden[0:16, 0:16], waits=[h6])
            cw = [h for h in (rt_state["gp"][H - 1], rt_state["sexp"]) if h]
            hcT = evac("v", bk, bk.t[0:128, 160:176], sb["cT"][:, :], waits=cw)
            tr(bk, bk.t[0:128, 176:192], sb["erow"][:, :], iden[0:16, 0:16], waits=[hE])
            heT = evac("v", bk, bk.t[0:128, 176:192], sb["eT"][:, :])
        return hfT, hdT, hcT, heT

    def xb_transpose(c, evh, warm, snap):
        ts_ = [snap["t"]] if snap.get("t") else []
        hx = {}
        for p in range(8):
            bk = banks[3]
            tr(bk, bk.t[:, 0:128], TRv[:, B_PROJ + p * 512: B_PROJ + p * 512 + 128],
               iden[:, :], waits=[evh[(p, 0)]])
            h1 = evac("v", bk, bk.t[:, 0:128], TRv[:, B_XT + p * 256: B_XT + p * 256 + 128],
                      waits=(ts_ if p == 0 else []))
            tr(bk, bk.t[:, 128:256], TRv[:, B_PROJ + p * 512 + 128: B_PROJ + p * 512 + 256],
               iden[:, :], waits=[evh[(p, 1)]])
            h2 = evac("v", bk, bk.t[:, 128:256], TRv[:, B_XT + p * 256 + 128: B_XT + p * 256 + 256])
            hx[p] = (h1, h2)
        return hx

    state_dep = [None]
    enc_state = {"split": None, "spk": {}}
    router_state = {}
    rt_state = {"gp": [None] * H, "tr": [None] * H, "sexp": None}

    def scan_chunk(c, evh, hx, hfT, hdT, hcT, heT, warm, snap):
        prev_yi_mm = snap.get("t") and ("t", snap["t"][1]) or None
        prev_hu_mm = prev_yi_mm
        hlast_y = None
        first_y = True
        prev_hmt = snap.get("v")
        if not warm:
            for h in range(H):
                wg = [hcT] + ([rt_state["tr"][h]] if rt_state["tr"][h] else [])
                rt_state["gp"][h] = pb.emit("g", lambda e, h=h: e.tensor_scalar(
                    sb["stm"][:, h * 128:(h + 1) * 128], sb["tripost"][:, :],
                    sb["cT"][:, h:h + 1], None, ALU.add), waits=wg)
        for h in range(H):
            p, half = h // 2, h % 2
            hb = half * 64
            Xtm = TRv[:, B_XT + p * 256: B_XT + p * 256 + 128]
            Btm = TRv[:, B_XT + p * 256 + 128: B_XT + p * 256 + 256]
            hst = sb["htst"][hb:hb + 64, p * 64:(p + 1) * 64]
            bkY = banks[5]
            if not warm:
                bkS = banks[3 + (h % 2)]
                Bfm = TRv[:, B_PROJ + p * 512 + 128 + c * 0: B_PROJ + p * 512 + 256]
                Cfm = TRv[:, B_PROJ + p * 512 + 256: B_PROJ + p * 512 + 384]
                mm(bkS, bkS.t[:, 0:128], Bfm[hb:hb + 64, :], Cfm[hb:hb + 64, :],
                   start=True, stop=True, waits=[evh[(p, 1)], evh[(p, 2)]])
                rt_state["tr"][h] = tr(bkS, bkS.t[:, 128:256],
                                       sb["stm"][:, h * 128:(h + 1) * 128], iden[:, :],
                                       waits=[rt_state["gp"][h]])
                hexp = evac("s", bkS, bkS.t[:, 128:256], TRv[:, B_GB:B_GB + 128],
                            func=AF.Exp, bias=sb["cT"][:, h:h + 1], scale=-1.0,
                            waits=[hcT] + ([prev_hmt] if prev_hmt else []))
                rt_state["sexp"] = hexp
                hmt = rd_bank(pb, "v", bkS, lambda e, bkS=bkS: e.tensor_mul(
                    TRv[:, B_MT:B_MT + 128], TRv[:, B_GB:B_GB + 128], bkS.t[:, 0:128]),
                    waits=[hexp] + ([prev_yi_mm] if prev_yi_mm else []))
                prev_hmt = hmt
                prev_yi_mm = mm(bkY, bkY.t[:, 0:64], TRv[:, B_MT:B_MT + 128], Xtm[:, hb:hb + 64],
                                start=True, stop=True, waits=[hmt, hx[p][0]])
                sw = [evh[(p, 2)]]
                if state_dep[0] is not None:
                    sw.append(state_dep[0])
                mm(bkY, bkY.t[:, 64:128], Cfm[hb:hb + 64, :], hst, start=True, stop=True,
                   waits=sw)
            hbw = pb.emit("v", lambda e, h=h, hb=hb, Btm=Btm: e.tensor_scalar(
                TRv[:, B_BW + hb:B_BW + hb + 64], Btm[:, hb:hb + 64], sb["fT"][:, h:h + 1],
                None, ALU.mult),
                waits=[hfT, hx[p][1]] + ([prev_hu_mm] if prev_hu_mm else []))
            prev_hu_mm = mm(bkY, bkY.t[:, 128:192] if not warm else bkY.t[:, 0:64],
                            TRv[:, B_BW:B_BW + 128], Xtm[:, hb:hb + 64],
                            start=True, stop=True, waits=[hbw, hx[p][0]])
            upd_ap = (bkY.t[:, 128:192] if not warm else bkY.t[:, 0:64])
            if not warm:
                hy1 = rd_bank(pb, "v", bkY, lambda e, bkY=bkY, h=h: e.tensor_scalar(
                    TRv[:, B_YT2:B_YT2 + 64], bkY.t[:, 64:128], sb["eT"][:, h:h + 1],
                    None, ALU.mult), waits=[heT])
                hlast_y = rd_bank(pb, "v", bkY, lambda e, bkY=bkY, h=h: e.tensor_add(
                    TRv[:, B_YTM + h * 64:B_YTM + (h + 1) * 64],
                    TRv[:, B_YT2:B_YT2 + 64], bkY.t[:, 0:64]),
                    waits=[hy1] + ([snap["t"]] if (first_y and snap.get("t")) else []))
                first_y = False
            hup = rd_bank(pb, "v", bkY, lambda e, hst=hst, h=h, hb=hb, upd_ap=upd_ap: e.scalar_tensor_tensor(
                hst, hst, sb["dT"][hb:hb + 64, h:h + 1], upd_ap[hb:hb + 64, :],
                ALU.mult, ALU.add), waits=[hdT])
            state_dep[0] = hup
        return hlast_y, None

    def yg_chunk(c, evh, hy, hrt):
        trb = TRv[:, :].bitcast(mybir.dt.bfloat16)
        hlast = None
        for p in range(8):
            bk = banks[0 + (p % 2)]
            tr(bk, bk.t[:, 0:128], TRv[:, B_YTM + p * 128:B_YTM + (p + 1) * 128],
               iden[:, :], waits=[hy])
            hm = rd_bank(pb, "v", bk, lambda e, bk=bk, p=p: e.tensor_mul(
                TRv[:, B_RT:B_RT + 128],
                TRv[:, B_PROJ + p * 512 + 384: B_PROJ + p * 512 + 512],
                bk.t[:, 0:128]), waits=[evh[(p, 3)]] + ([hrt] if (p == 0 and hrt) else []))
            ygh = trb[:, 2 * B_YG + p * 512 + c * 128: 2 * B_YG + p * 512 + c * 128 + 128]
            ygl = trb[:, 2 * B_YG + 4096 + p * 512 + c * 128: 2 * B_YG + 4096 + p * 512 + c * 128 + 128]
            hh = pb.emit("v", lambda e, ygh=ygh: e.tensor_copy(ygh, TRv[:, B_RT:B_RT + 128]),
                         waits=[hm])
            hlast = pb.emit("v", lambda e, ygl=ygl, ygh=ygh: e.tensor_sub(
                ygl, TRv[:, B_RT:B_RT + 128], ygh), waits=[hh])
        return hlast

    def wout_group(hwout, hyg):
        trb = TRv[:, :].bitcast(mybir.dt.bfloat16)
        for o in range(8):
            bk = banks[2 + (o % 2)]
            for k in range(8):
                sth = wsb[:, k * 1024 + o * 128: k * 1024 + (o + 1) * 128]
                stl = wsb[:, 8192 + k * 1024 + o * 128: 8192 + k * 1024 + (o + 1) * 128]
                mvh = trb[:, 2 * B_YG + k * 512: 2 * B_YG + (k + 1) * 512]
                mvl = trb[:, 2 * B_YG + 4096 + k * 512: 2 * B_YG + 4096 + (k + 1) * 512]
                mm3(bk, bk.t[:, 0:512], sth, stl, mvh, mvl,
                    start=(k == 0), stop=(k == 7),
                    waits=([hwout, hyg] if k == 0 else []))
            hsf = evac("s", bk, bk.t[:, 0:512], TRv[:, B_WSC:B_WSC + 512],
                       func=AF.Identity, bias=sb["boutc"][:, o:o + 1])
            for cc in range(CPT):
                bkt = banks[6 + (cc % 2)]
                tr(bkt, bkt.t[:, 0:128],
                   TRv[:, B_WSC + cc * 128: B_WSC + (cc + 1) * 128],
                   iden[:, :], waits=[hsf])
                hsf2 = evac("v", bkt, bkt.t[:, 0:128],
                            sb["stm"][:, cc * D + o * 128: cc * D + (o + 1) * 128])
            pb.wait_only("s", [pb.cnt and ("t", pb.cnt["t"])])

    def enc_j(g, j, henc):
        trb = TRv[:, :].bitcast(mybir.dt.bfloat16)
        bk = banks[7]
        bks = banks[6]
        for k in range(8):
            sth = wsb[:, k * 2048 + j * 128: k * 2048 + (j + 1) * 128]
            stl = wsb[:, 16384 + k * 2048 + j * 128: 16384 + k * 2048 + (j + 1) * 128]
            mvh = sb["xnfmh"][:, k * T_TILE + g * GRP: k * T_TILE + (g + 1) * GRP]
            mvl = sb["xnfml"][:, k * T_TILE + g * GRP: k * T_TILE + (g + 1) * GRP]
            mm3(bk, bk.t[:, 0:GRP], sth, stl, mvh, mvl,
                start=(k == 0), stop=(k == 7), waits=([henc] if k == 0 else []))
        hv0 = evac("s", bk, bk.t[:, 0:GRP], TRv[:, C_RS2:C_RS2 + GRP],
                   func=AF.Sigmoid, bias=sb["encbc"][:, j:j + 1],
                   waits=([enc_state["s2c"]] if enc_state["s2c"] else []))
        sh_ap = trb[:, 2 * C_SPB: 2 * C_SPB + GRP]
        sl_ap = trb[:, 2 * C_SPB + GRP: 2 * C_SPB + 2 * GRP]
        hv1 = pb.emit("g", lambda e, sh_ap=sh_ap: e.tensor_copy(sh_ap, TRv[:, C_RS2:C_RS2 + GRP]),
                      waits=[hv0])
        hv2 = pb.emit("g", lambda e, sl_ap=sl_ap, sh_ap=sh_ap: e.tensor_sub(
            sl_ap, TRv[:, C_RS2:C_RS2 + GRP], sh_ap), waits=[hv1])
        h = mm3(bks, bks.t[:, 0:GRP], sb["s2ch"][:, j * 128:(j + 1) * 128],
                sb["s2cl"][:, j * 128:(j + 1) * 128], sh_ap, sl_ap,
                start=(j == 0), stop=(j == 15), waits=[hv2])
        enc_state["s2c"] = h if h else enc_state["s2c"]
        if j == 15:
            hc = evac("v", bks, bks.t[:, 0:GRP], TRv[:, C_CONT:C_CONT + GRP])
            enc_state["cont"] = hc
    def router_group(g, hcont):
        contg = TRv[:, C_CONT + g * GRP: C_CONT + (g + 1) * GRP]
        bk = banks[5]
        mm(bk, bk.t[0:64, 0:GRP], sb["rw1"][:, :], contg[0:64, :], start=True,
           stop=True, waits=[hcont])
        hrh = evac("s", bk, bk.t[0:64, 0:GRP], sb["rhsb"][:, :], func=AF.Relu,
                   bias=sb["rb1c"][:, 0:1])
        for ci in range(CPT):
            cc = CPT * g + ci
            bkl = banks[6]
            mm(bkl, bkl.t[:, 0:8], sb["rhsb"][:, ci * 128:(ci + 1) * 128],
               sb["rw2"][:, :], start=True, stop=True, waits=[hrh])
            hl = rd_bank(pb, "v", bkl, lambda e, bkl=bkl: e.tensor_add(
                sb["logit"][:, :], bkl.t[:, 0:8], sb["rb2rep"][:, :]))
            hm = pb.emit("v", lambda e: e.reduce_max(sb["m1c"][:, :], sb["logit"][:, :], AX.X, negate=True),
                         waits=[hl])
            hm2_ = pb.emit("v", lambda e: e.tensor_scalar(sb["eu"][:, :], sb["logit"][:, :],
                                                          sb["m1c"][:, 0:1], None, ALU.add), waits=[hm])
            he = pb.emit("s", lambda e: e.activation(out=sb["eu"][:, :], in_=sb["eu"][:, :],
                                                     func=AF.Exp), waits=[hm2_])
            h1 = pb.emit("v", lambda e: e.reduce_max(sb["m1c"][:, :], sb["eu"][:, :], AX.X), waits=[he])
            h2 = pb.emit("v", lambda e: e.tensor_scalar(sb["combm"][:, 0:8], sb["eu"][:, :], sb["m1c"][:, 0:1],
                                                        None, ALU.is_ge), waits=[h1])
            h3 = pb.emit("v", lambda e: e.select(sb["selb"][:, :], sb["combm"][:, 0:8], sb["iota8"][:, :],
                                                 sb["big8"][:, :]), waits=[h2])
            h4 = pb.emit("v", lambda e: e.tensor_reduce(sb["i1c"][:, :], sb["selb"][:, :], AX.X, ALU.min),
                         waits=[h3])
            h5 = pb.emit("v", lambda e: e.tensor_scalar(sb["combm"][:, 0:8], sb["iota8"][:, :], sb["i1c"][:, 0:1],
                                                        None, ALU.is_equal), waits=[h4])
            h6 = pb.emit("v", lambda e: e.select(sb["eu2"][:, :], sb["combm"][:, 0:8], sb["neg1t"][:, :],
                                                 sb["eu"][:, :]), waits=[h5])
            h7 = pb.emit("v", lambda e: e.reduce_max(sb["m2c"][:, :], sb["eu2"][:, :], AX.X), waits=[h6])
            h8 = pb.emit("v", lambda e: e.tensor_scalar(sb["combm"][:, 8:16], sb["eu2"][:, :], sb["m2c"][:, 0:1],
                                                        None, ALU.is_ge), waits=[h7])
            h9 = pb.emit("v", lambda e: e.select(sb["selb"][:, :], sb["combm"][:, 8:16], sb["iota8"][:, :],
                                                 sb["big8"][:, :]), waits=[h8])
            h10 = pb.emit("v", lambda e: e.tensor_reduce(sb["i2c"][:, :], sb["selb"][:, :], AX.X, ALU.min),
                          waits=[h9])
            h11 = pb.emit("v", lambda e: e.tensor_scalar(sb["combm"][:, 8:16], sb["iota8"][:, :], sb["i2c"][:, 0:1],
                                                         None, ALU.is_equal), waits=[h10])
            hd = pb.emit("v", lambda e: e.tensor_add(sb["dnc"][:, :], sb["m1c"][:, :], sb["m2c"][:, :]),
                         waits=[h7])
            hd2 = pb.emit("v", lambda e: e.reciprocal(out=sb["dnc"][:, :], in_=sb["dnc"][:, :]), waits=[hd])
            hw1 = pb.emit("v", lambda e: e.tensor_mul(sb["w1c"][:, :], sb["m1c"][:, :], sb["dnc"][:, :]),
                          waits=[hd2])
            hw2 = pb.emit("v", lambda e: e.tensor_mul(sb["w2c"][:, :], sb["m2c"][:, :], sb["dnc"][:, :]),
                          waits=[hd2])
            hc1 = pb.emit("v", lambda e: e.tensor_scalar(sb["selb"][:, :],
                                                         sb["combm"][:, 0:8], sb["w1c"][:, 0:1], None,
                                                         ALU.mult), waits=[h5, hw1, h10])
            hc2 = pb.emit("v", lambda e, cc=cc: e.scalar_tensor_tensor(
                sb["comb"][:, cc * 8:(cc + 1) * 8], sb["combm"][:, 8:16], sb["w2c"][:, 0:1],
                sb["selb"][:, :], ALU.mult, ALU.add), waits=[h11, hw2, hc1])
            if DEBUG and cc == 1 and router_state.get("tile") == N_WARM_TILES:
                pb.barrier()
                pb.emit("g", lambda e: e.dma_start(out=dbg["eu"][:, :], in_=sb["eu"][:, :]), dma="B")
                pb.emit("g", lambda e: e.dma_start(out=dbg["logit"][:, :], in_=sb["logit"][:, :]), dma="B")
                pb.emit("g", lambda e: e.dma_start(out=dbg["msk"][:, :], in_=sb["combm"][:, :].bitcast(F32)[:, 0:16]), dma="B")
                pb.barrier()
            bkt = banks[7]
            tr(bkt, bkt.t[0:8, 0:128], sb["comb"][:, cc * 8:(cc + 1) * 8], iden[:, :],
               waits=[hc2])
            evac("v", bkt, bkt.t[0:8, 0:128], sb["combT"][:, cc * 128:(cc + 1) * 128])

    def experts_group(g, hexp):
        contg = TRv[:, C_CONT + g * GRP: C_CONT + (g + 1) * GRP]
        cbh = sb["contb"][:, g * GRP:(g + 1) * GRP]
        cbl = sb["contb"][:, T_TILE + g * GRP: T_TILE + (g + 1) * GRP]
        prev_eo = None
        for q in range(4):
            bkA, bkB = banks[q % 2], banks[2 + q % 2]
            ehh = {}
            for j in range(8):
                for half, bk in ((0, bkA), (1, bkB)):
                    sth = wsb[half * 64:half * 64 + 64,
                              q * 1024 + j * 128: q * 1024 + (j + 1) * 128]
                    stl = wsb[half * 64:half * 64 + 64,
                              4096 + q * 1024 + j * 128: 4096 + q * 1024 + (j + 1) * 128]
                    mh = cbh[half * 64:half * 64 + 64, :]
                    ml = cbl[half * 64:half * 64 + 64, :]
                    mm3(bk, bk.t[:, 0:GRP], sth, stl, mh, ml,
                        start=True, stop=True,
                        waits=([hexp] if j == 0 else []))
                    ehh[(j, half)] = evac(
                        "s", bk, bk.t[:, 0:GRP],
                        TRv[:, C_EH + half * 4096 + j * GRP: C_EH + half * 4096 + (j + 1) * GRP],
                        func=AF.Relu,
                        bias=sb["eb1c"][:, (2 * q + half) * 8 + j:(2 * q + half) * 8 + j + 1],
                        waits=([prev_eo] if prev_eo else []))
            for half in (0, 1):
                ee = 2 * q + half
                for ci in range(CPT):
                    cc = CPT * g + ci
                    bko = banks[4 + (ci % 2)]
                    for j in range(8):
                        prev_eo = mm(bko, bko.t[:, 0:64],
                                     TRv[:, C_EH + half * 4096 + j * GRP + ci * 128:
                                         C_EH + half * 4096 + j * GRP + ci * 128 + 128],
                                     wsb_f32[:, 4096 + ee * 512 + j * 64: 4096 + ee * 512 + (j + 1) * 64],
                                     start=(j == 0), stop=(j == 7),
                                     waits=[ehh[(j, half)]])
                    moeap = TRv[:, C_MOE + cc * 64: C_MOE + (cc + 1) * 64]
                    tmp64 = TRv[:, C_EO:C_EO + 64]
                    if ee == 0:
                        rd_bank(pb, "v", bko, lambda e, bko=bko, moeap=moeap, cc=cc: e.tensor_scalar(
                            moeap, bko.t[:, 0:64], sb["comb"][:, cc * 8:cc * 8 + 1], None, ALU.mult))
                    else:
                        hta = rd_bank(pb, "v", bko, lambda e, bko=bko, tmp64=tmp64, cc=cc, ee=ee: e.tensor_scalar(
                            tmp64, bko.t[:, 0:64], sb["comb"][:, cc * 8 + ee:cc * 8 + ee + 1], None, ALU.mult))
                        pb.emit("v", lambda e, moeap=moeap, tmp64=tmp64: e.tensor_add(
                            moeap, moeap, tmp64), waits=[hta])
        for ci in range(CPT):
            cc = CPT * g + ci
            bkb = banks[6]
            mm(bkb, bkb.t[:, 0:64], sb["combT"][:, cc * 128:(cc + 1) * 128],
               sb["eb2m"][:, :], start=True, stop=True)
            hmb = rd_bank(pb, "v", bkb, lambda e, bkb=bkb, cc=cc: e.tensor_add(
                TRv[:, C_MOE + cc * 64: C_MOE + (cc + 1) * 64],
                TRv[:, C_MOE + cc * 64: C_MOE + (cc + 1) * 64], bkb.t[:, 0:64]))
            h1 = pb.emit("v", lambda e, cc=cc: e.tensor_copy(
                TRv[:, C_MTMP:C_MTMP + 64], TRv[:, C_MOE + cc * 64: C_MOE + (cc + 1) * 64]),
                waits=[hmb])
            h2 = pb.emit("v", lambda e: e.tensor_copy(
                TRv[:, C_MTMP + 64:C_MTMP + 128], TRv[:, C_MTMP:C_MTMP + 64]), waits=[h1])
            bkt = banks[7]
            trb2 = TRv[:, :].bitcast(mybir.dt.bfloat16)
            tr(bkt, bkt.t[:, 0:128], TRv[:, C_MTMP:C_MTMP + 128], iden[:, :], waits=[h2])
            mfh = trb2[:, 2 * C_MFD + cc * 128: 2 * C_MFD + (cc + 1) * 128]
            mfl = trb2[:, 2 * C_MFD + 512 + cc * 128: 2 * C_MFD + 512 + (cc + 1) * 128]
            hq1 = evac("v", bkt, bkt.t[:, 0:128], mfh)
            rd_bank(pb, "v", bkt, lambda e, mfl=mfl, bkt=bkt, mfh=mfh: e.tensor_sub(
                mfl, bkt.t[:, 0:128], mfh), waits=[hq1])

    def ffn_dec_group(g, tile_idx, hdec):
        trb = TRv[:, :].bitcast(mybir.dt.bfloat16)
        mfh_a = trb[:, 2 * C_MFD: 2 * C_MFD + GRP]
        mfl_a = trb[:, 2 * C_MFD + 512: 2 * C_MFD + 512 + GRP]
        rat_h = {}
        prev_split = {C_RS: None}
        for jp in range(8):
            j0, j1 = 2 * jp, 2 * jp + 1
            bkA, bkB = banks[jp % 2], banks[2 + jp % 2]
            for (jj, bk_, rlo) in ((j0, bkA, 0), (j1, bkB, 64)):
                mm3(bk_, bk_.t[:, 0:GRP], sb["c2srh"][rlo:rlo + 64, jj * 128:(jj + 1) * 128],
                    sb["c2srl"][rlo:rlo + 64, jj * 128:(jj + 1) * 128],
                    mfh_a[rlo:rlo + 64, :], mfl_a[rlo:rlo + 64, :],
                    start=True, stop=True)
            for (jj, bk_) in ((j0, bkA), (j1, bkB)):
                hv0 = evac("s", bk_, bk_.t[:, 0:GRP],
                           TRv[:, C_RS:C_RS + GRP],
                           func=AF.Sigmoid, bias=sb["c2sbc"][:, jj:jj + 1],
                           waits=([prev_split[C_RS]] if prev_split[C_RS] else []))
                rh_ap = trb[:, 2 * C_RAT + jj * GRP: 2 * C_RAT + (jj + 1) * GRP]
                rl_ap = trb[:, 2 * C_RAT + 8192 + jj * GRP: 2 * C_RAT + 8192 + (jj + 1) * GRP]
                hv1 = pb.emit("v", lambda e, rh_ap=rh_ap: e.tensor_copy(
                    rh_ap, TRv[:, C_RS:C_RS + GRP]), waits=[hv0])
                rat_h[jj] = pb.emit("v", lambda e, rl_ap=rl_ap, rh_ap=rh_ap: e.tensor_sub(
                    rl_ap, TRv[:, C_RS:C_RS + GRP], rh_ap), waits=[hv1])
                prev_split[C_RS] = rat_h[jj]
        hdtm = {}
        prev_tr = None
        for o in range(8):
            bk = banks[o % 4]
            for k in range(16):
                sth = wsb[:, k * 1024 + o * 128: k * 1024 + (o + 1) * 128]
                stl = wsb[:, 16384 + k * 1024 + o * 128: 16384 + k * 1024 + (o + 1) * 128]
                rh_ap = trb[:, 2 * C_RAT + k * GRP: 2 * C_RAT + (k + 1) * GRP]
                rl_ap = trb[:, 2 * C_RAT + 8192 + k * GRP: 2 * C_RAT + 8192 + (k + 1) * GRP]
                mm3(bk, bk.t[:, 0:GRP], sth, stl, rh_ap, rl_ap,
                    start=(k == 0), stop=(k == 15),
                    waits=[rat_h[k]] + ([hdec] if k == 0 else []))
            hdv = evac("s", bk, bk.t[:, 0:GRP], TRv[:, C_RS:C_RS + GRP],
                       func=AF.Sigmoid, bias=sb["decbc"][:, o:o + 1],
                       waits=([prev_tr] if prev_tr else []))
            for ci in range(CPT):
                bkt = banks[4 + (ci % 2)]
                prev_tr = tr(bkt, bkt.t[:, 0:128],
                             TRv[:, C_RS + ci * 128: C_RS + (ci + 1) * 128],
                             iden[:, :], waits=[hdv])
                hdtm[(o, ci)] = evac("v", bkt, bkt.t[:, 0:128],
                                     TRv[:, C_DTM + ci * 1024 + o * 128: C_DTM + ci * 1024 + (o + 1) * 128])
        for ci in range(CPT):
            cc = ci
            dtm = TRv[:, C_DTM + ci * 1024:C_DTM + (ci + 1) * 1024]
            hlast = hdtm[(7, ci)]
            hx = pb.emit("y", lambda e, cc=cc, tile_idx=tile_idx: e.dma_start(
                out=TRv[:, C_XR:C_XR + 1024],
                in_=din["xin"][WARM + tile_idx * T_TILE + cc * L: WARM + tile_idx * T_TILE + (cc + 1) * L, :]),
                waits=[hlast], dma="A")
            hs1 = pb.emit("v", lambda e, dtm=dtm: e.reduce_sum(sb["mucol"][:, 0:1], dtm, AX.X),
                          waits=[hlast])
            hsq = pb.emit("s", lambda e, dtm=dtm: e.activation(out=TRv[:, C_G:C_G + 1024], in_=dtm,
                                                               func=AF.Square), waits=[hlast])
            hs2 = pb.emit("v", lambda e: e.reduce_sum(sb["s2col"][:, 0:1], TRv[:, C_G:C_G + 1024], AX.X),
                          waits=[hsq])
            hmu = pb.emit("v", lambda e: e.tensor_single_scalar(sb["mucol"][:, 0:1], sb["mucol"][:, 0:1], 1.0 / D, ALU.mult),
                          waits=[hs1])
            hm2 = pb.emit("v", lambda e: e.tensor_single_scalar(sb["m2col"][:, 0:1], sb["s2col"][:, 0:1], 1.0 / D, ALU.mult),
                          waits=[hs2])
            hv1 = pb.emit("v", lambda e: e.tensor_mul(sb["varcol"][:, 0:1], sb["mucol"][:, 0:1], sb["mucol"][:, 0:1]),
                          waits=[hmu])
            hv2 = pb.emit("v", lambda e: e.tensor_sub(sb["s2col"][:, 0:1], sb["m2col"][:, 0:1], sb["varcol"][:, 0:1]),
                          waits=[hm2, hv1])
            hv = pb.emit("v", lambda e: e.tensor_single_scalar(sb["varcol"][:, 0:1], sb["s2col"][:, 0:1], 1e-5, ALU.add),
                         waits=[hv2])
            hsd = pb.emit("s", lambda e: e.activation(out=sb["rstdcol"][:, 0:1], in_=sb["varcol"][:, 0:1], func=AF.Sqrt),
                          waits=[hv])
            hrs = pb.emit("v", lambda e: e.reciprocal(out=sb["rstdcol"][:, 0:1], in_=sb["rstdcol"][:, 0:1]),
                          waits=[hsd])
            hn = pb.emit("v", lambda e, dtm=dtm: e.tensor_scalar(TRv[:, C_G:C_G + 1024], dtm,
                                                                 sb["mucol"][:, 0:1], sb["rstdcol"][:, 0:1],
                                                                 ALU.subtract, ALU.mult), waits=[hrs, hmu])
            hg1 = pb.emit("v", lambda e: e.tensor_mul(TRv[:, C_G:C_G + 1024], TRv[:, C_G:C_G + 1024],
                                                      sb["lngrep"][:, :]), waits=[hn])
            hg2 = pb.emit("v", lambda e: e.tensor_add(TRv[:, C_G:C_G + 1024], TRv[:, C_G:C_G + 1024],
                                                      sb["lnbrep"][:, :]), waits=[hg1])
            hsg = pb.emit("s", lambda e: e.activation(out=TRv[:, C_SG:C_SG + 1024], in_=TRv[:, C_G:C_G + 1024],
                                                      func=AF.Silu), waits=[hg2])
            hcmb = pb.emit("v", lambda e, cc=cc: e.tensor_mul(
                TRv[:, C_SG:C_SG + 1024], TRv[:, C_SG:C_SG + 1024],
                sb["stm"][:, cc * D:(cc + 1) * D]), waits=[hsg])
            hout = pb.emit("v", lambda e: e.tensor_add(TRv[:, C_OB:C_OB + 1024], TRv[:, C_XR:C_XR + 1024],
                                                       TRv[:, C_SG:C_SG + 1024]), waits=[hcmb, hx])
            pb.emit("g", lambda e, cc=cc, tile_idx=tile_idx: e.dma_start(
                out=dout[tile_idx * T_TILE + cc * L: tile_idx * T_TILE + (cc + 1) * L, :],
                in_=TRv[:, C_OB:C_OB + 1024]), waits=[hout], dma="B")
            pb.barrier()

    def tap(nm, src_ap):
        if DEBUG:
            pb.emit("g", lambda e, nm=nm, src_ap=src_ap: e.dma_start(
                out=dbg[nm][:, :], in_=src_ap), dma="B")
            pb.barrier()

    prevA1 = {}
    ha_map = {}
    for tile in range(N_WARM_TILES + N_MAIN_TILES):
        warm = tile < N_WARM_TILES
        tile_idx = tile - N_WARM_TILES
        row0 = tile * T_TILE
        if tile != N_WARM_TILES:
            phase_A(row0)
        pb.barrier()
        henc = None if warm else load_enc()
        if tile == N_WARM_TILES:
            tap("xn", sb["xnfmh"][:, :4096])
        enc_state["s2c"] = None
        enc_state["cont"] = None
        snap = {}
        for c in range(CPT):
            evh, ha = proj_chunk(c, warm, snap)
            if not warm:
                enc_j(0, 4 * c + 0, henc)
            hfT, hdT, hcT, heT = decay_chunk(ha, warm, snap)
            if not warm:
                enc_j(0, 4 * c + 1, henc)
            hx = xb_transpose(c, evh, warm, snap)
            if not warm:
                enc_j(0, 4 * c + 2, henc)
            hy, hrt = scan_chunk(c, evh, hx, hfT, hdT, hcT, heT, warm, snap)
            if warm:
                snap = {"t": ("t", pb.cnt["t"]), "v": ("v", pb.cnt["v"]), "s": ("s", pb.cnt["s"])}
                emit_split_jobs(9)
                ha_map[c] = ha
                if tile == N_WARM_TILES - 1 and c >= 1:
                    phase_A_chunk(N_WARM_TILES * T_TILE, c - 1, prevA1,
                                  base=6144, bksel=(6, 7), xn_guard=ha_map[c - 1])
            else:
                enc_j(0, 4 * c + 3, henc)
                hyg = yg_chunk(c, evh, hy, hrt)
                snap = {"t": ("t", pb.cnt["t"]), "v": ("v", pb.cnt["v"]), "s": ("s", pb.cnt["s"])}
            if tile == N_WARM_TILES and c == 0:
                tap("proj", TRv[:, B_PROJ:B_PROJ + 4096])
                tap("crow", sb["crow"][:, :])
                tap("asb", sb["asb"][:, :])
                tap("ytm", TRv[:, B_YTM:B_YTM + 1024])
                tap("h1", sb["htst"][:, :])
                tap("xtm", TRv[:, B_XT:B_XT + 2048])
                pb.emit("g", lambda e: e.dma_start(out=dbg["cols"][:, 0:16], in_=sb["cT"][:, :]), dma="B") if DEBUG else None
                pb.emit("g", lambda e: e.dma_start(out=dbg["cols"][:, 16:32], in_=sb["fT"][:, :]), dma="B") if DEBUG else None
                pb.emit("g", lambda e: e.dma_start(out=dbg["cols"][:, 32:48], in_=sb["eT"][:, :]), dma="B") if DEBUG else None
                pb.emit("g", lambda e: e.dma_start(out=dbg["cols"][:, 48:64], in_=sb["dT"][:, :]), dma="B") if DEBUG else None
                pb.barrier()
        if warm:
            if tile == N_WARM_TILES - 1:
                phase_A_chunk(N_WARM_TILES * T_TILE, CPT - 1, prevA1,
                              base=6144, bksel=(6, 7), xn_guard=ha_map[CPT - 1])
                h = pb.emit("v", lambda e: e.tensor_mul(sb["htst"][:, :], sb["htst"][:, :],
                                                        sb["wmask"][:, 0:1].broadcast_to([128, 512])))
                pb.barrier()
                tap("h0", sb["htst"][:, :])
            continue
        hc = enc_state["cont"]
        hcb = pb.emit("v", lambda e: e.tensor_scalar(
            TRv[:, C_CONT:C_CONT + T_TILE], TRv[:, C_CONT:C_CONT + T_TILE],
            sb["s2cbc"][:, 0:1], None, ALU.add), waits=[hc])
        hcs = pb.emit("v", lambda e: e.tensor_copy(sb["contb"][:, 0:T_TILE],
                                                   TRv[:, C_CONT:C_CONT + T_TILE]), waits=[hcb])
        pb.emit("v", lambda e: e.tensor_sub(sb["contb"][:, T_TILE:2 * T_TILE],
                                            TRv[:, C_CONT:C_CONT + T_TILE],
                                            sb["contb"][:, 0:T_TILE]), waits=[hcs])
        pb.barrier()
        hwout = load_wout()
        wout_group(hwout, hyg)
        pb.barrier()
        if tile == N_WARM_TILES:
            tap("cont", TRv[:, C_CONT:C_CONT + 512])
            tap("stm", sb["stm"][:, :4096])
        hexp = load_experts()
        router_state["tile"] = tile
        for g in range(GPT):
            router_group(g, hcb)
            experts_group(g, hexp)
        pb.barrier()
        if tile == N_WARM_TILES:
            tap("comb", sb["comb"][:, :])
            tap("moe", TRv[:, C_MOE:C_MOE + 256])
            tap("mfd", TRv[:, C_MFD:C_MFD + 512])
        hdec = load_dec()
        for g in range(GPT):
            ffn_dec_group(g, tile_idx, hdec)
        pb.barrier()
        if tile == N_WARM_TILES:
            tap("rat", TRv[:, C_RAT:C_RAT + 4096])
            tap("dec", TRv[:, C_DTM:C_DTM + 2048])

    pb.wait_only("g", [("B", pb.cnt["B"])])

    with nc.Block() as block:
        @block.tensor
        def _(t):
            for fn in pb.prog["t"]:
                fn(t)

        @block.vector
        def _(v):
            for fn in pb.prog["v"]:
                fn(v)

        @block.scalar
        def _(s):
            for fn in pb.prog["s"]:
                fn(s)

        @block.gpsimd
        def _(g):
            for fn in pb.prog["g"]:
                fn(g)

        @block.sync
        def _(y):
            for fn in pb.prog["y"]:
                fn(y)

    es.close()
    if pb.ldwf_ins:
        import dataclasses
        targets = {id(i) for i in pb.ldwf_ins}
        npatched = 0
        for bb in nc.m.functions[0].blocks:
            il = bb.instructions
            for idx, ins0 in enumerate(il):
                if id(ins0) in targets:
                    il[idx] = dataclasses.replace(ins0, ldweights=False)
                    npatched += 1
        assert npatched == len(pb.ldwf_ins), (npatched, len(pb.ldwf_ins))
    return nc


def _layout_inputs(inp):
    f32 = np.float32
    x = np.asarray(inp["L0_attn_inp"], f32)
    wA = np.asarray(inp["L0_attn_wA"], f32)
    bA = np.asarray(inp["L0_attn_bA"], f32)
    wXBC = np.asarray(inp["L0_attn_wXBC"], f32)
    bXBC = np.asarray(inp["L0_attn_bXBC"], f32)
    wG = np.asarray(inp["L0_attn_wG"], f32)
    bG = np.asarray(inp["L0_attn_bG"], f32)
    Wout = np.asarray(inp["L0_attn_Wout"], f32)
    bout = np.asarray(inp["L0_attn_bout"], f32)
    encw = np.asarray(inp["L0_ffn_enc_w"], f32)
    encb = np.asarray(inp["L0_ffn_enc_b"], f32)
    s2cw = np.asarray(inp["L0_ffn_s2c_w"], f32)
    s2cb = np.asarray(inp["L0_ffn_s2c_b"], f32)
    rw1 = np.asarray(inp["L0_ffn_r_w1"], f32)
    rb1 = np.asarray(inp["L0_ffn_r_b1"], f32)
    rw2 = np.asarray(inp["L0_ffn_r_w2"], f32)
    rb2 = np.asarray(inp["L0_ffn_r_b2"], f32)
    ew1 = np.asarray(inp["L0_ffn_e_w1"], f32)
    eb1 = np.asarray(inp["L0_ffn_e_b1"], f32)
    ew2 = np.asarray(inp["L0_ffn_e_w2"], f32)
    eb2 = np.asarray(inp["L0_ffn_e_b2"], f32)
    c2sw = np.asarray(inp["L0_ffn_c2s_w"], f32)
    c2sb = np.asarray(inp["L0_ffn_c2s_b"], f32)
    decw = np.asarray(inp["L0_ffn_dec_w"], f32)
    decb = np.asarray(inp["L0_ffn_dec_b"], f32)
    lng = np.asarray(inp["L0_ffn_ln_g"], f32)
    lnb = np.asarray(inp["L0_ffn_ln_b"], f32)

    wblk = np.zeros((8, 128, 512), f32)
    astat = np.zeros((8, 128, 16), f32)
    bxbcg = np.zeros((128, 32), f32)
    for p in range(8):
        for half in range(2):
            h = 2 * p + half
            r = slice(half * 64, half * 64 + 64)
            cix = slice(half * 64, half * 64 + 64)
            wblk[p, r, 0:128][:, cix] = wXBC[h, 0:64, :].T
            wblk[p, r, 128:256][:, cix] = wXBC[h, 64:128, :].T
            wblk[p, r, 256:384][:, cix] = wXBC[h, 128:192, :].T
            wblk[p, r, 384:512][:, cix] = wG[h, :, :].T
            astat[p, r, h] = wA[h, 0, :]
            bxbcg[r, p * 4 + 0] = bXBC[h, 0:64]
            bxbcg[r, p * 4 + 1] = bXBC[h, 64:128]
            bxbcg[r, p * 4 + 2] = bXBC[h, 128:192]
            bxbcg[r, p * 4 + 3] = bG[h, :]

    common = {
        "wblk": wblk, "astat": astat, "bxbcg": bxbcg,
        "bacol": bA[:, 0:1].copy(),
        "woutT": np.ascontiguousarray(Wout.T),
        "boutc": np.ascontiguousarray(bout.reshape(8, 128).T),
        "encw": encw, "encbc": np.ascontiguousarray(encb.reshape(16, 128).T),
        "s2cd": np.concatenate([s2cw, s2cw], axis=1),
        "s2cbc": np.concatenate([s2cb, s2cb])[:, None].copy(),
        "rw1": rw1, "rb1c": rb1[:, None].copy(), "rw2": rw2,
        "rb2rep": np.broadcast_to(rb2[None, :], (128, 8)).copy(),
        "ew1p": np.concatenate([ew1[0::2], ew1[1::2]], axis=1),
        "eb1c": np.ascontiguousarray(
            eb1.reshape(8, 8, 128).transpose(2, 0, 1).reshape(128, 64)),
        "ew2s": ew2, "eb2m": eb2,
        "c2sr": np.concatenate([c2sw, c2sw], axis=0),
        "c2sbc": np.ascontiguousarray(c2sb.reshape(16, 128).T),
        "decw": decw, "decbc": np.ascontiguousarray(decb.reshape(8, 128).T),
        "lngrep": np.broadcast_to(lng[None, :], (128, D)).copy(),
        "lnbrep": np.broadcast_to(lnb[None, :], (128, D)).copy(),
        "iden": np.eye(128, dtype=f32),
        "tripos": np.where(np.arange(128)[None, :] >= np.arange(128)[:, None],
                           0.0, 1e30).astype(f32),
        "tripost": np.ascontiguousarray(
            np.where(np.arange(128)[None, :] >= np.arange(128)[:, None],
                     0.0, 1e30).astype(f32).T),
        "iota8": np.broadcast_to(np.arange(8, dtype=f32)[None, :], (128, 8)).copy(),
        "big8": np.full((128, 8), 1e30, f32),
        "neg1t": np.full((128, 8), -1.0, f32),
        "ones16": np.ones((16, 128), f32),
    }

    in_maps = []
    for core in range(NCORE):
        b, half = core // 2, core % 2
        m0 = half * T_CORE
        w0 = m0 - WARM if half == 1 else 0
        xin = np.concatenate([x[b, w0:w0 + WARM, :], x[b, m0:m0 + T_CORE, :]],
                             axis=0)
        im = dict(common)
        im["xin"] = np.ascontiguousarray(xin)
        im["wmask"] = np.full((128, 1), 1.0 if half == 1 else 0.0, f32)
        in_maps.append(im)
    return in_maps


def kernel(**inputs):
    in_maps = _layout_inputs(inputs)
    nc = build_nc()
    res = run_bass_kernel_spmd(nc, in_maps, core_ids=list(range(NCORE)))
    out = np.empty((4, 4096, D), np.float32)
    for core in range(NCORE):
        b, half = core // 2, core % 2
        out[b, half * T_CORE:(half + 1) * T_CORE, :] = res.results[core]["out"]
    return out


# version 26
# speedup vs baseline: 1.2127x; 1.2127x over previous
import contextlib
import numpy as np
import concourse.bass as bass
import concourse.mybir as mybir
from concourse.bass_utils import run_bass_kernel_spmd

F32 = mybir.dt.float32
I32 = mybir.dt.int32
AF = mybir.ActivationFunctionType
ALU = mybir.AluOpType
AX = mybir.AxisListType

H, CH, NST, D = 16, 64, 64, 1024
HID, MOE, NEXP, EXPH = 2048, 64, 8, 1024
NCORE = 8
T_CORE = 2048
WARM = 512
T_TILE = 512
L = 128
CPT = T_TILE // L
N_WARM_TILES = WARM // T_TILE
N_MAIN_TILES = T_CORE // T_TILE
GRP = 512
GPT = T_TILE // GRP
DEBUG = False


class PB:

    ENGS = ("t", "v", "s", "g", "y")

    def __init__(self, nc, es):
        self.nc = nc
        self.prog = {e: [] for e in self.ENGS}
        self.sem = {k: es.enter_context(nc.semaphore("sem_" + k))
                    for k in ("t", "v", "s", "g", "A", "B")}
        self.cnt = {k: 0 for k in ("t", "v", "s", "g", "A", "B")}
        self.ldwf_ins = []

    def _skey(self, eng):
        return {"t": "t", "v": "v", "s": "s", "g": "g"}[eng]

    def emit(self, eng, build, waits=(), dma=None, inc=True, ldwf=False):
        sem, cnt = self.sem, self.cnt
        wl = [(k, c) for (k, c) in waits if c is not None and c > 0]
        if dma is None:
            key, amt = self._skey(eng), 1
        else:
            key, amt = dma, 16
        if inc:
            cnt[key] += amt
            target = cnt[key]
        else:
            target = None

        def fn(e, build=build, wl=wl, key=key, inc=inc, ldwf=ldwf, self_=self):
            for (k, c) in wl:
                e.wait_ge(sem[k], c)
            ins = build(e)
            if inc:
                ins.then_inc(sem[key], 1 if key not in ("A", "B") else 16)
            if ldwf:
                self_.ldwf_ins.append(ins.ins)
        self.prog[eng].append(fn)
        return (key, target) if inc else None

    def wait_only(self, eng, waits):
        sem = self.sem
        wl = [(k, c) for (k, c) in waits if c > 0]
        if not wl:
            return

        def fn(e, wl=wl):
            for (k, c) in wl:
                e.wait_ge(sem[k], c)
        self.prog[eng].append(fn)

    def barrier(self):
        snap = dict(self.cnt)
        for eng in ("t", "v", "s", "g", "y"):
            self.wait_only(eng, [(k, snap[k]) for k in ("t", "v", "s", "g", "A", "B")])


class Bank:
    def __init__(self, tensor):
        self.t = tensor
        self.wr = None
        self.rd = {}


def mk_mm(pb):
    def mm(bank, out_ap, lhsT, rhs, start, stop, waits=(), tp=None):
        w = [x for x in waits if x is not None]
        if start:
            w += list(bank.rd.items())
            if bank.wr is not None:
                w.append(bank.wr)

        def b(e, out_ap=out_ap, lhsT=lhsT, rhs=rhs, start=start, stop=stop, tp=tp):
            return e.matmul(out_ap, lhsT, rhs, start=start, stop=stop,
                            tile_position=tp)
        h = pb.emit("t", b, waits=w, inc=stop)
        if stop:
            bank.wr = h
            bank.rd = {}
        return h
    return mm


def mk_tr(pb):
    def tr(bank, out_ap, in_ap, iden_ap, waits=()):
        w = list(waits) + list(bank.rd.items())
        if bank.wr is not None:
            w.append(bank.wr)

        def b(e, out_ap=out_ap, in_ap=in_ap, iden_ap=iden_ap):
            return e.transpose(out_ap, in_ap, iden_ap)
        h = pb.emit("t", b, waits=w)
        bank.wr = h
        bank.rd = {}
        return h
    return tr


def rd_bank(pb, eng, bank, build, waits=()):
    w = list(waits)
    if bank.wr is not None:
        w.append(bank.wr)
    h = pb.emit(eng, build, waits=w)
    bank.rd[h[0]] = h[1]
    return h


def build_nc():
    nc = bass.Bass("TRN2", target_bir_lowering=False, debug=False)
    din = {}
    for name, shape in [
        ("xin", [WARM + T_CORE, D]), ("wmask", [128, 1]),
        ("wblk", [8, 128, 512]), ("astat", [8, 128, 16]),
        ("bxbcg", [128, 32]), ("bacol", [16, 1]),
        ("woutT", [D, D]), ("boutc", [128, 8]),
        ("encw", [D, HID]), ("encbc", [128, 16]),
        ("s2cd", [HID, 128]), ("s2cbc", [128, 1]),
        ("rw1", [64, 64]), ("rb1c", [64, 1]), ("rw2", [64, 8]),
        ("rb2rep", [128, 8]),
        ("ew1p", [4, 128, EXPH]), ("eb1c", [128, 64]),
        ("ew2s", [NEXP, EXPH, MOE]), ("eb2m", [8, 64]),
        ("c2sr", [128, HID]), ("c2sbc", [128, 16]),
        ("decw", [HID, D]), ("decbc", [128, 8]),
        ("lngrep", [128, D]), ("lnbrep", [128, D]),
        ("iden", [128, 128]), ("tripos", [128, 128]), ("tripost", [128, 128]),
        ("iota8", [128, 8]), ("big8", [128, 8]), ("neg1t", [128, 8]),
        ("ones16", [16, 128]),
    ]:
        din[name] = nc.dram_tensor(name, shape, F32, kind="ExternalInput").ap()
    dout = nc.dram_tensor("out", [T_CORE, D], F32, kind="ExternalOutput").ap()
    BF16 = mybir.dt.bfloat16
    scr = {}
    for nm, shape in [("ench", [D, HID]), ("encl", [D, HID]),
                      ("dech", [HID, D]), ("decl", [HID, D]),
                      ("wouth", [D, D]), ("woutl", [D, D]),
                      ("ew1h", [4, 128, EXPH]), ("ew1l", [4, 128, EXPH])]:
        scr[nm] = nc.dram_tensor("scr_" + nm, shape, BF16, kind="Internal").ap()
    dbg = {}
    if DEBUG:
        for nm, shp in [("h0", [128, 512]), ("xn", [128, 4096]),
                        ("proj", [128, 4096]), ("crow", [16, 128]),
                        ("cols", [128, 64]), ("ytm", [128, 1024]),
                        ("h1", [128, 512]), ("asb", [16, 128]),
                        ("stm", [128, 4096]), ("cont", [128, 512]),
                        ("spk", [128, 4096]), ("comb", [128, 32]),
                        ("moe", [128, 256]), ("mfd", [128, 512]),
                        ("rat", [128, 4096]), ("dec", [128, 2048]),
                        ("xtm", [128, 2048]), ("ygf", [128, 2048]),
                        ("lncols", [128, 8]), ("xb", [128, 1024]),
                        ("eu", [128, 8]), ("logit", [128, 8]), ("msk", [128, 16])]:
            dbg[nm] = nc.dram_tensor("dbg_" + nm, shp, F32, kind="ExternalOutput").ap()

    es = contextlib.ExitStack()
    sb = {}

    def S(name, shape, dt=F32):
        sb[name] = es.enter_context(nc.sbuf_tensor("sb_" + name, shape, dt))
        return sb[name]

    BF = mybir.dt.bfloat16
    S("wblkh", [128, 8 * 512], BF)
    S("wblkl", [128, 8 * 512], BF)
    S("astath", [128, 8 * 16], BF)
    S("astatl", [128, 8 * 16], BF)
    S("bxbcg", [128, 32]); S("bacol", [16, 1])
    S("encbc", [128, 16]); S("s2ch", [128, 16 * 128], BF); S("s2cl", [128, 16 * 128], BF)
    S("s2cbc", [128, 1])
    S("rw1", [64, 64]); S("rb1c", [64, 1]); S("rw2", [64, 8])
    S("rb2rep", [128, 8]); S("eb1c", [128, 64]); S("eb2m", [8, 64])
    S("c2srh", [128, HID], BF); S("c2srl", [128, HID], BF); S("c2sbc", [128, 16])
    S("decbc", [128, 8]); S("boutc", [128, 8])
    S("lngrep", [128, D]); S("lnbrep", [128, D])
    S("iden", [128, 128]); S("tripos", [128, 128]); S("tripost", [128, 128])
    S("iota8", [128, 8]); S("big8", [128, 8]); S("neg1t", [128, 8])
    S("ones16", [16, 128]); S("wmask", [128, 1])
    S("wstrb", [128, 32768], BF)
    S("xnfmh", [128, 8 * T_TILE], BF)
    S("xnfml", [128, 8 * T_TILE], BF)
    S("contb", [128, 2 * T_TILE], BF)
    S("stm", [128, CPT * D])
    S("htst", [128, 512])
    S("tr", [128, 13440])
    S("asb", [16, L]); S("sp16", [16, L]); S("crow", [16, L])
    S("frow", [16, L]); S("erow", [16, L]); S("dfree", [16, 128])
    S("cT", [128, 16]); S("fT", [128, 16]); S("eT", [128, 16]); S("dT", [128, 16])
    S("mucol", [128, 2]); S("s2col", [128, 2]); S("m2col", [128, 2])
    S("varcol", [128, 2]); S("rstdcol", [128, 2])
    S("bnst", [128, 24]); S("bnag", [128, 16])
    S("rhsb", [64, GRP])
    S("combm", [128, CPT * 8], I32)
    S("comb", [128, CPT * 8])
    S("combT", [8, CPT * 128])
    S("m1c", [128, 1]); S("m2c", [128, 1]); S("i1c", [128, 1]); S("i2c", [128, 1])
    S("dnc", [128, 1]); S("w1c", [128, 1]); S("w2c", [128, 1])
    S("logit", [128, 8]); S("eu", [128, 8]); S("eu2", [128, 8]); S("selb", [128, 8])
    S("clast", [16, 1]); S("dcol", [16, 1])

    pb = PB(nc, es)
    mm = mk_mm(pb)
    tr = mk_tr(pb)

    def mm3(bank, out_ap, sth, stl, mvh, mvl, start, stop, waits=()):
        w = [x for x in waits if x is not None]
        if start:
            w += list(bank.rd.items())
            if bank.wr is not None:
                w.append(bank.wr)
        if sth.base_partition() != 0:
            pb.emit("t", lambda e, out_ap=out_ap, sth=sth, mvh=mvh, start=start: e.matmul(
                out_ap, sth, mvh, start=start, stop=False), waits=w, inc=False)
            pb.emit("t", lambda e, out_ap=out_ap, sth=sth, mvl=mvl: e.matmul(
                out_ap, sth, mvl, start=False, stop=False), inc=False)
            h = pb.emit("t", lambda e, out_ap=out_ap, stl=stl, mvh=mvh, stop=stop: e.matmul(
                out_ap, stl, mvh, start=False, stop=stop), inc=stop)
            if stop:
                bank.wr = h
                bank.rd = {}
            return h
        pb.emit("t", lambda e, sth=sth: e.ldweights(sth), waits=w, inc=False)
        pb.emit("t", lambda e, out_ap=out_ap, sth=sth, mvh=mvh, start=start: e.matmul(
            out_ap, sth, mvh, start=start, stop=False), inc=False, ldwf=True)
        pb.emit("t", lambda e, out_ap=out_ap, sth=sth, mvl=mvl: e.matmul(
            out_ap, sth, mvl, start=False, stop=False), inc=False, ldwf=True)
        pb.emit("t", lambda e, stl=stl: e.ldweights(stl), inc=False)
        h = pb.emit("t", lambda e, out_ap=out_ap, stl=stl, mvh=mvh, stop=stop: e.matmul(
            out_ap, stl, mvh, start=False, stop=stop), inc=stop, ldwf=True)
        if stop:
            bank.wr = h
            bank.rd = {}
        return h
    banks = [Bank(es.enter_context(nc.psum_tensor("pb%d" % i, [128, 512], F32)))
             for i in range(8)]

    TRv = sb["tr"]
    A_XB, A_X2, A_XN = 0, 1024, 2048
    B_PROJ = 0
    B_XT = 4096
    B_YTM = 6144
    B_YG = 7168
    B_RT, B_GB, B_MT = 11264, 11392, 11520
    B_YT2, B_BW = 11648, 11712
    B_WSC = 11264
    C_RS2 = 11840
    C_SPB = 12352
    C_CONT = 12864
    C_EH = 0
    C_MOE = 9216
    C_MFD = 9472
    C_MTMP = 9984
    C_RAT = 0
    C_RS = 8192
    C_DTM = 8704
    C_G, C_SG, C_XR, C_OB = 0, 1024, 2048, 3072
    C_EO = 13056

    iden = sb["iden"]

    def evac(eng, bank, psrc, dst, func=None, bias=None, scale=1.0, waits=()):
        if eng == "s":
            def b(e, dst=dst, psrc=psrc, func=func, bias=bias, scale=scale):
                return e.activation(out=dst, in_=psrc,
                                    func=(func or AF.Identity),
                                    bias=(0.0 if bias is None else bias),
                                    scale=scale)
        else:
            def b(e, dst=dst, psrc=psrc):
                return e.tensor_copy(dst, psrc)
        return rd_bank(pb, eng, bank, b, waits=waits)

    ld = {}
    init_loads = []
    for name in ["bxbcg", "bacol", "encbc", "s2cbc", "rw1", "rb1c", "rw2",
                 "rb2rep", "eb1c", "eb2m", "c2sbc", "decbc", "boutc",
                 "lngrep", "lnbrep", "iden", "tripos", "tripost", "iota8", "big8",
                 "neg1t", "ones16", "wmask"]:
        init_loads.append((name, sb[name], [(din[name][:, :], sb[name][:, :])]))

    for name, _sbuf, pairs in init_loads:
        for (src, dst) in pairs:
            h = pb.emit("y", lambda e, dst=dst, src=src: e.dma_start(out=dst, in_=src),
                        dma="A")
        ld[name] = h
    init_done = pb.cnt["A"]
    pb.barrier()
    pb.emit("v", lambda e: e.memset(sb["htst"][:, :], 0.0))
    pb.emit("v", lambda e: e.memset(TRv[:, B_BW:B_BW + 128], 0.0))
    pb.barrier()

    wsb = sb["wstrb"]
    wsb_f32 = wsb[:, :].bitcast(F32)

    def split_to(dst_h, dst_l, f32_ap, hload):
        h1 = pb.emit("s", lambda e, dst_h=dst_h, f32_ap=f32_ap: e.activation(
            out=dst_h, in_=f32_ap, func=AF.Identity), waits=[hload])
        h2 = pb.emit("g", lambda e, dst_l=dst_l, f32_ap=f32_ap, dst_h=dst_h: e.tensor_sub(
            dst_l, f32_ap, dst_h), waits=[h1])
        return h2

    ph0 = {}
    for p in range(8):
        pp = p % 2
        of = pp * 1024
        wd = [h for h in (ph0.get(pp),) if h]
        hl = pb.emit("y", lambda e, p=p, of=of: e.dma_start(out=TRv[:, of:of + 512], in_=din["wblk"][p, :, :]),
                     waits=wd, dma="A")
        hs = split_to(sb["wblkh"][:, p * 512:(p + 1) * 512], sb["wblkl"][:, p * 512:(p + 1) * 512],
                      TRv[:, of:of + 512], hl)
        hl2 = pb.emit("y", lambda e, p=p, of=of: e.dma_start(out=TRv[:, of + 512:of + 528], in_=din["astat"][p, :, :]),
                      waits=[hs], dma="A")
        hs2 = split_to(sb["astath"][:, p * 16:(p + 1) * 16], sb["astatl"][:, p * 16:(p + 1) * 16],
                       TRv[:, of + 512:of + 528], hl2)
        ph0[pp] = hs2
    ph1 = {}
    for k in range(16):
        kp = k % 2
        of = kp * 128
        wd = [h for h in (ph1.get(kp),) if h]
        hl = pb.emit("y", lambda e, k=k, of=of: e.dma_start(out=TRv[:, 2048 + of:2048 + of + 128],
                                                            in_=din["s2cd"][k * 128:(k + 1) * 128, :]),
                     waits=wd, dma="A")
        hs = split_to(sb["s2ch"][:, k * 128:(k + 1) * 128], sb["s2cl"][:, k * 128:(k + 1) * 128],
                      TRv[:, 2048 + of:2048 + of + 128], hl)
        ph1[kp] = hs
    hl = pb.emit("y", lambda e: e.dma_start(out=TRv[:, 2432:4480], in_=din["c2sr"][:, :]), dma="A")
    hs = split_to(sb["c2srh"][:, :], sb["c2srl"][:, :], TRv[:, 2432:4480], hl)
    big_specs = [("enc", din["encw"], scr["ench"], scr["encl"], 8, 2048),
                 ("wout", din["woutT"], scr["wouth"], scr["woutl"], 8, 1024),
                 ("dec", din["decw"], scr["dech"], scr["decl"], 16, 1024)]
    jobs = []
    for (kind, dsrc, dh, dl, nchunk, cw) in big_specs:
        for k in range(nchunk):
            jobs.append((kind, dsrc[k * 128:(k + 1) * 128, :], dh[k * 128:(k + 1) * 128, :],
                         dl[k * 128:(k + 1) * 128, :], cw))
    for q in range(4):
        jobs.append(("ew1", din["ew1p"][q, :, :], scr["ew1h"][q, :, :], scr["ew1l"][q, :, :], 1024))
    store_h = {}
    split_ph = {}
    split_pos = [0]

    def emit_split_jobs(n):
        i0 = split_pos[0]
        for i in range(i0, min(i0 + n, len(jobs))):
            kind, jsrc, jdh, jdl, cw = jobs[i]
            p = i % 2
            trof = 8192 + p * 2048
            wbof = p * 4096
            wd = [h for h in (split_ph.get(("sp", p)),) if h]
            hl = pb.emit("y", lambda e, jsrc=jsrc, trof=trof, cw=cw: e.dma_start(
                out=wsb_f32[:, trof:trof + cw], in_=jsrc), waits=wd, dma="A")
            ws = [hl] + [h for h in (split_ph.get(("st", p)),) if h]
            h1 = pb.emit("s", lambda e, wbof=wbof, trof=trof, cw=cw: e.activation(
                out=wsb[:, wbof:wbof + cw], in_=wsb_f32[:, trof:trof + cw],
                func=AF.Identity), waits=ws)
            h2 = pb.emit("g", lambda e, wbof=wbof, trof=trof, cw=cw: e.tensor_sub(
                wsb[:, wbof + cw:wbof + 2 * cw], wsb_f32[:, trof:trof + cw],
                wsb[:, wbof:wbof + cw]), waits=[h1])
            split_ph[("sp", p)] = h2
            pb.emit("g", lambda e, jdh=jdh, wbof=wbof, cw=cw: e.dma_start(
                out=jdh, in_=wsb[:, wbof:wbof + cw]), waits=[h2], dma="B")
            hs2 = pb.emit("g", lambda e, jdl=jdl, wbof=wbof, cw=cw: e.dma_start(
                out=jdl, in_=wsb[:, wbof + cw:wbof + 2 * cw]), waits=[h2], dma="B")
            split_ph[("st", p)] = hs2
            store_h[kind] = hs2
        split_pos[0] = min(i0 + n, len(jobs))
    pb.wait_only("y", [h for h in (ph0.get(0), ph0.get(1), ph1.get(0), ph1.get(1), hs) if h])
    pb.barrier()

    def load_wout():
        h = None
        for k in range(8):
            pb.emit("y", lambda e, k=k: e.dma_start(
                out=wsb[:, k * 1024:(k + 1) * 1024],
                in_=scr["wouth"][k * 128:(k + 1) * 128, :]), dma="A",
                waits=([store_h["wout"]] if (k == 0 and "wout" in store_h) else []))
            h = pb.emit("y", lambda e, k=k: e.dma_start(
                out=wsb[:, 8192 + k * 1024:8192 + (k + 1) * 1024],
                in_=scr["woutl"][k * 128:(k + 1) * 128, :]), dma="A")
        return h

    def load_enc():
        h = None
        for k in range(8):
            pb.emit("y", lambda e, k=k: e.dma_start(
                out=wsb[:, k * 2048:(k + 1) * 2048],
                in_=scr["ench"][k * 128:(k + 1) * 128, :]), dma="A",
                waits=([store_h["enc"]] if (k == 0 and "enc" in store_h) else []))
            h = pb.emit("y", lambda e, k=k: e.dma_start(
                out=wsb[:, 16384 + k * 2048:16384 + (k + 1) * 2048],
                in_=scr["encl"][k * 128:(k + 1) * 128, :]), dma="A")
        return h

    def load_experts():
        for q in range(4):
            pb.emit("y", lambda e, q=q: e.dma_start(
                out=wsb[:, q * 1024:(q + 1) * 1024], in_=scr["ew1h"][q, :, :]), dma="A",
                waits=([store_h["ew1"]] if (q == 0 and "ew1" in store_h) else []))
            pb.emit("y", lambda e, q=q: e.dma_start(
                out=wsb[:, 4096 + q * 1024:4096 + (q + 1) * 1024], in_=scr["ew1l"][q, :, :]), dma="A")
        h = None
        for ee in range(NEXP):
            for k in range(8):
                h = pb.emit("y", lambda e, ee=ee, k=k: e.dma_start(
                    out=wsb_f32[:, 4096 + ee * 512 + k * 64: 4096 + ee * 512 + (k + 1) * 64],
                    in_=din["ew2s"][ee, k * 128:(k + 1) * 128, :]), dma="A")
        return h

    def load_dec():
        h = None
        for k in range(16):
            pb.emit("y", lambda e, k=k: e.dma_start(
                out=wsb[:, k * 1024:(k + 1) * 1024],
                in_=scr["dech"][k * 128:(k + 1) * 128, :]), dma="A",
                waits=([store_h["dec"]] if (k == 0 and "dec" in store_h) else []))
            h = pb.emit("y", lambda e, k=k: e.dma_start(
                out=wsb[:, 16384 + k * 1024:16384 + (k + 1) * 1024],
                in_=scr["decl"][k * 128:(k + 1) * 128, :]), dma="A")
        return h

    def phase_A_chunk(xin_row0, c, prev, base=0, bksel=(0, 1), xn_guard=None):
        lastev = None
        if True:
            cp = c % 2
            XB, XN = base + cp * 2048, base + cp * 2048 + 1024
            mcol = sb["mucol"][:, cp:cp + 1]
            vcol = sb["varcol"][:, cp:cp + 1]
            rcol = sb["rstdcol"][:, cp:cp + 1]
            bnst = sb["bnst"][:, cp * 12:(cp + 1) * 12]
            ag = sb["bnag"][:, cp * 8:(cp + 1) * 8]
            w_dma = [h for h in (prev.get(("xn", cp)),) if h]
            hdma = pb.emit("y", lambda e, XB=XB, c=c: e.dma_start(
                out=TRv[:, XB:XB + 1024],
                in_=din["xin"][xin_row0 + c * L: xin_row0 + (c + 1) * L, :]),
                waits=w_dma, dma="A")
            pb.emit("v", lambda e, XB=XB, bnst=bnst: e.bn_stats(bnst[:, 0:6], TRv[:, XB:XB + 512]),
                    waits=[hdma])
            pb.emit("v", lambda e, XB=XB, bnst=bnst: e.bn_stats(bnst[:, 6:12], TRv[:, XB + 512:XB + 1024]))
            pb.emit("v", lambda e, ag=ag, bnst=bnst: e.bn_aggr(ag[:, 0:2], bnst[:, 0:6]))
            ha2 = pb.emit("v", lambda e, ag=ag, bnst=bnst: e.bn_aggr(ag[:, 2:4], bnst[:, 6:12]))
            hsm = pb.emit("v", lambda e, ag=ag: e.tensor_add(ag[:, 4:6], ag[:, 0:2], ag[:, 2:4]),
                          waits=[ha2])
            hdf = pb.emit("v", lambda e, ag=ag: e.tensor_sub(ag[:, 6:7], ag[:, 0:1], ag[:, 2:3]),
                          waits=[ha2])
            hmu = pb.emit("v", lambda e, mcol=mcol, ag=ag: e.tensor_single_scalar(
                mcol, ag[:, 4:5], 0.5, ALU.mult), waits=[hsm])
            hd2 = pb.emit("v", lambda e, ag=ag: e.tensor_mul(ag[:, 7:8], ag[:, 6:7], ag[:, 6:7]),
                          waits=[hdf])
            hv2 = pb.emit("v", lambda e, vcol=vcol, ag=ag: e.scalar_tensor_tensor(
                vcol, ag[:, 7:8], 0.5, ag[:, 5:6], ALU.mult, ALU.add), waits=[hd2, hsm])
            hv = pb.emit("v", lambda e, vcol=vcol: e.tensor_single_scalar(
                vcol, vcol, 2e-5, ALU.add), waits=[hv2])
            hsd = pb.emit("s", lambda e, rcol=rcol, vcol=vcol: e.activation(
                out=rcol, in_=vcol, func=AF.Sqrt, scale=0.5), waits=[hv])
            hrs = pb.emit("v", lambda e, rcol=rcol: e.reciprocal(out=rcol, in_=rcol),
                          waits=[hsd])
            w_xn = [hrs, hmu] + ([prev.get(("tr", cp))] if prev.get(("tr", cp)) else [])
            hxn = pb.emit("v", lambda e, XB=XB, XN=XN, mcol=mcol, rcol=rcol: e.tensor_scalar(
                TRv[:, XN:XN + 1024], TRv[:, XB:XB + 1024], mcol[:, 0:1], rcol[:, 0:1],
                ALU.subtract, ALU.mult), waits=w_xn)
            prev[("xn", cp)] = hxn
            htr = None
            for f in range(8):
                bk = banks[bksel[f % 2]]
                htr = tr(bk, bk.t[:, 0:128], TRv[:, XN + f * 128:XN + (f + 1) * 128],
                         iden[:, :], waits=[hxn])
                xh = sb["xnfmh"][:, f * T_TILE + c * L: f * T_TILE + (c + 1) * L]
                xl = sb["xnfml"][:, f * T_TILE + c * L: f * T_TILE + (c + 1) * L]
                hh1 = evac("v", bk, bk.t[:, 0:128], xh,
                           waits=([xn_guard] if (xn_guard and f == 0) else []))
                lastev = rd_bank(pb, "v", bk, lambda e, xl=xl, bk=bk, xh=xh: e.tensor_sub(
                    xl, bk.t[:, 0:128], xh), waits=[hh1])
            prev[("tr", cp)] = htr
        return lastev

    def phase_A(xin_row0):
        prev = {}
        lastev = None
        for c in range(CPT):
            lastev = phase_A_chunk(xin_row0, c, prev)
        return lastev

    def proj_chunk(c, warm, snap):
        groups = [0, 1] if warm else [0, 1, 2, 3]
        sw = [h for h in (snap.get("t"), snap.get("v")) if h]
        evh = {}
        for p in range(8):
            bk = banks[p % 2]
            movh = sb["xnfmh"][:, p * T_TILE + c * L: p * T_TILE + (c + 1) * L]
            movl = sb["xnfml"][:, p * T_TILE + c * L: p * T_TILE + (c + 1) * L]
            for gi, g in enumerate(groups):
                mm3(bk, bk.t[:, g * 128:(g + 1) * 128],
                    sb["wblkh"][:, p * 512 + g * 128: p * 512 + (g + 1) * 128],
                    sb["wblkl"][:, p * 512 + g * 128: p * 512 + (g + 1) * 128],
                    movh, movl, start=(gi == 0), stop=(gi == len(groups) - 1),
                    waits=sw if gi == 0 else ())
            for g in groups:
                fn = AF.Silu if g < 3 else AF.Identity
                evh[(p, g)] = evac("s", bk, bk.t[:, g * 128:(g + 1) * 128],
                                   TRv[:, B_PROJ + p * 512 + g * 128: B_PROJ + p * 512 + (g + 1) * 128],
                                   func=fn, bias=sb["bxbcg"][:, p * 4 + g: p * 4 + g + 1],
                                   waits=sw)
        bkA = banks[2]
        for p in range(8):
            movh = sb["xnfmh"][:, p * T_TILE + c * L: p * T_TILE + (c + 1) * L]
            movl = sb["xnfml"][:, p * T_TILE + c * L: p * T_TILE + (c + 1) * L]
            for pi, (st_, mv_) in enumerate([
                    (sb["astath"][:, p * 16:(p + 1) * 16], movh),
                    (sb["astath"][:, p * 16:(p + 1) * 16], movl),
                    (sb["astatl"][:, p * 16:(p + 1) * 16], movh)]):
                mm(bkA, bkA.t[0:16, 0:128], st_, mv_,
                   start=(p == 0 and pi == 0), stop=(p == 7 and pi == 2))
        ha = evac("s", bkA, bkA.t[0:16, 0:128], sb["asb"][:, :],
                  func=AF.Identity, bias=sb["bacol"][:, 0:1], waits=sw)
        return evh, ha

    def decay_chunk(ha, warm, snap):
        ts_ = [snap["t"]] if snap.get("t") else []
        ss_ = [snap["s"]] if snap.get("s") else []
        vs_ = [snap["v"]] if snap.get("v") else []
        hn = pb.emit("v", lambda e: e.tensor_single_scalar(sb["sp16"][:, :], sb["asb"][:, :], -1.0, ALU.mult),
                     waits=[ha] + vs_ + ss_)
        h1 = pb.emit("v", lambda e: e.tensor_max(sb["sp16"][:, :], sb["sp16"][:, :], sb["asb"][:, :]),
                     waits=[hn])
        h2 = pb.emit("s", lambda e: e.activation(out=sb["sp16"][:, :], in_=sb["sp16"][:, :], func=AF.Exp, scale=-1.0),
                     waits=[h1])
        h3 = pb.emit("s", lambda e: e.activation(out=sb["sp16"][:, :], in_=sb["sp16"][:, :], func=AF.Ln, bias=1.0),
                     waits=[h2])
        h4 = pb.emit("v", lambda e: e.tensor_single_scalar(sb["crow"][:, :], sb["asb"][:, :], 0.0, ALU.max),
                     waits=[ha] + ts_)
        h5 = pb.emit("v", lambda e: e.tensor_add(sb["sp16"][:, :], sb["sp16"][:, :], sb["crow"][:, :]),
                     waits=[h3, h4])
        h6 = pb.emit("v", lambda e: e.tensor_tensor_scan(sb["crow"][:, :], sb["ones16"][:, :], sb["sp16"][:, :],
                                                         0.0, ALU.mult, ALU.add), waits=[h5])
        h7 = pb.emit("v", lambda e: e.tensor_copy(sb["clast"][:, :], sb["crow"][:, L - 1:L]), waits=[h6] + ss_)
        h8 = pb.emit("v", lambda e: e.tensor_scalar(sb["frow"][:, :], sb["crow"][:, :], sb["clast"][:, 0:1], None,
                                                    ALU.subtract), waits=[h7] + ts_)
        h9 = pb.emit("s", lambda e: e.activation(out=sb["frow"][:, :], in_=sb["frow"][:, :], func=AF.Exp),
                     waits=[h8])
        hD = pb.emit("s", lambda e: e.activation(out=sb["dcol"][:, :], in_=sb["clast"][:, :], func=AF.Exp, scale=-1.0),
                     waits=[h7])
        hDf = pb.emit("v", lambda e: e.tensor_copy(sb["dfree"][:, :], sb["dcol"][:, 0:1].broadcast_to([16, 128])),
                      waits=[hD] + ts_)
        hE = None
        if not warm:
            hE = pb.emit("s", lambda e: e.activation(out=sb["erow"][:, :], in_=sb["crow"][:, :], func=AF.Exp,
                                                     scale=-1.0), waits=[h6] + ts_)
        bk = banks[2]
        tr(bk, bk.t[0:128, 128:144], sb["frow"][:, :], iden[0:16, 0:16], waits=[h9])
        hfT = evac("v", bk, bk.t[0:128, 128:144], sb["fT"][:, :])
        tr(bk, bk.t[0:128, 144:160], sb["dfree"][:, :], iden[0:16, 0:16], waits=[hDf])
        hdT = evac("v", bk, bk.t[0:128, 144:160], sb["dT"][:, :])
        hcT = heT = None
        if not warm:
            tr(bk, bk.t[0:128, 160:176], sb["crow"][:, :], iden[0:16, 0:16], waits=[h6])
            cw = [h for h in (rt_state["gp"][H - 1], rt_state["sexp"]) if h]
            hcT = evac("v", bk, bk.t[0:128, 160:176], sb["cT"][:, :], waits=cw)
            tr(bk, bk.t[0:128, 176:192], sb["erow"][:, :], iden[0:16, 0:16], waits=[hE])
            heT = evac("v", bk, bk.t[0:128, 176:192], sb["eT"][:, :])
        return hfT, hdT, hcT, heT

    def xb_transpose(c, evh, warm, snap):
        ts_ = [snap["t"]] if snap.get("t") else []
        hx = {}
        for p in range(8):
            bk = banks[3]
            tr(bk, bk.t[:, 0:128], TRv[:, B_PROJ + p * 512: B_PROJ + p * 512 + 128],
               iden[:, :], waits=[evh[(p, 0)]])
            h1 = evac("v", bk, bk.t[:, 0:128], TRv[:, B_XT + p * 256: B_XT + p * 256 + 128],
                      waits=(ts_ if p == 0 else []))
            tr(bk, bk.t[:, 128:256], TRv[:, B_PROJ + p * 512 + 128: B_PROJ + p * 512 + 256],
               iden[:, :], waits=[evh[(p, 1)]])
            h2 = evac("v", bk, bk.t[:, 128:256], TRv[:, B_XT + p * 256 + 128: B_XT + p * 256 + 256])
            hx[p] = (h1, h2)
        return hx

    state_dep = [None]
    enc_state = {"split": None, "spk": {}}
    router_state = {}
    rt_state = {"gp": [None] * H, "tr": [None] * H, "sexp": None}

    def scan_chunk(c, evh, hx, hfT, hdT, hcT, heT, warm, snap):
        prev_yi_mm = snap.get("t") and ("t", snap["t"][1]) or None
        prev_hu_mm = prev_yi_mm
        hlast_y = None
        first_y = True
        prev_hmt = snap.get("v")
        if not warm:
            for h in range(H):
                wg = [hcT] + ([rt_state["tr"][h]] if rt_state["tr"][h] else [])
                rt_state["gp"][h] = pb.emit("g", lambda e, h=h: e.tensor_scalar(
                    sb["stm"][:, h * 128:(h + 1) * 128], sb["tripost"][:, :],
                    sb["cT"][:, h:h + 1], None, ALU.add), waits=wg)
        for h in range(H):
            p, half = h // 2, h % 2
            hb = half * 64
            Xtm = TRv[:, B_XT + p * 256: B_XT + p * 256 + 128]
            Btm = TRv[:, B_XT + p * 256 + 128: B_XT + p * 256 + 256]
            hst = sb["htst"][hb:hb + 64, p * 64:(p + 1) * 64]
            bkY = banks[5]
            if not warm:
                bkS = banks[3 + (h % 2)]
                Bfm = TRv[:, B_PROJ + p * 512 + 128 + c * 0: B_PROJ + p * 512 + 256]
                Cfm = TRv[:, B_PROJ + p * 512 + 256: B_PROJ + p * 512 + 384]
                mm(bkS, bkS.t[:, 0:128], Bfm[hb:hb + 64, :], Cfm[hb:hb + 64, :],
                   start=True, stop=True, waits=[evh[(p, 1)], evh[(p, 2)]])
                rt_state["tr"][h] = tr(bkS, bkS.t[:, 128:256],
                                       sb["stm"][:, h * 128:(h + 1) * 128], iden[:, :],
                                       waits=[rt_state["gp"][h]])
                hexp = evac("s", bkS, bkS.t[:, 128:256], TRv[:, B_GB:B_GB + 128],
                            func=AF.Exp, bias=sb["cT"][:, h:h + 1], scale=-1.0,
                            waits=[hcT] + ([prev_hmt] if prev_hmt else []))
                rt_state["sexp"] = hexp
                hmt = rd_bank(pb, "v", bkS, lambda e, bkS=bkS: e.tensor_mul(
                    TRv[:, B_MT:B_MT + 128], TRv[:, B_GB:B_GB + 128], bkS.t[:, 0:128]),
                    waits=[hexp] + ([prev_yi_mm] if prev_yi_mm else []))
                prev_hmt = hmt
                prev_yi_mm = mm(bkY, bkY.t[:, 0:64], TRv[:, B_MT:B_MT + 128], Xtm[:, hb:hb + 64],
                                start=True, stop=True, waits=[hmt, hx[p][0]])
                sw = [evh[(p, 2)]]
                if state_dep[0] is not None:
                    sw.append(state_dep[0])
                mm(bkY, bkY.t[:, 64:128], Cfm[hb:hb + 64, :], hst, start=True, stop=True,
                   waits=sw)
            hbw = pb.emit("v", lambda e, h=h, hb=hb, Btm=Btm: e.tensor_scalar(
                TRv[:, B_BW + hb:B_BW + hb + 64], Btm[:, hb:hb + 64], sb["fT"][:, h:h + 1],
                None, ALU.mult),
                waits=[hfT, hx[p][1]] + ([prev_hu_mm] if prev_hu_mm else []))
            prev_hu_mm = mm(bkY, bkY.t[:, 128:192] if not warm else bkY.t[:, 0:64],
                            TRv[:, B_BW:B_BW + 128], Xtm[:, hb:hb + 64],
                            start=True, stop=True, waits=[hbw, hx[p][0]])
            upd_ap = (bkY.t[:, 128:192] if not warm else bkY.t[:, 0:64])
            if not warm:
                hy1 = rd_bank(pb, "v", bkY, lambda e, bkY=bkY, h=h: e.tensor_scalar(
                    TRv[:, B_YT2:B_YT2 + 64], bkY.t[:, 64:128], sb["eT"][:, h:h + 1],
                    None, ALU.mult), waits=[heT])
                hlast_y = rd_bank(pb, "v", bkY, lambda e, bkY=bkY, h=h: e.tensor_add(
                    TRv[:, B_YTM + h * 64:B_YTM + (h + 1) * 64],
                    TRv[:, B_YT2:B_YT2 + 64], bkY.t[:, 0:64]),
                    waits=[hy1] + ([snap["t"]] if (first_y and snap.get("t")) else []))
                first_y = False
            hup = rd_bank(pb, "v", bkY, lambda e, hst=hst, h=h, hb=hb, upd_ap=upd_ap: e.scalar_tensor_tensor(
                hst, hst, sb["dT"][hb:hb + 64, h:h + 1], upd_ap[hb:hb + 64, :],
                ALU.mult, ALU.add), waits=[hdT])
            state_dep[0] = hup
        return hlast_y, None

    def yg_chunk(c, evh, hy, hrt):
        trb = TRv[:, :].bitcast(mybir.dt.bfloat16)
        hlast = None
        for p in range(8):
            bk = banks[0 + (p % 2)]
            tr(bk, bk.t[:, 0:128], TRv[:, B_YTM + p * 128:B_YTM + (p + 1) * 128],
               iden[:, :], waits=[hy])
            hm = rd_bank(pb, "v", bk, lambda e, bk=bk, p=p: e.tensor_mul(
                TRv[:, B_RT:B_RT + 128],
                TRv[:, B_PROJ + p * 512 + 384: B_PROJ + p * 512 + 512],
                bk.t[:, 0:128]), waits=[evh[(p, 3)]] + ([hrt] if (p == 0 and hrt) else []))
            ygh = trb[:, 2 * B_YG + p * 512 + c * 128: 2 * B_YG + p * 512 + c * 128 + 128]
            ygl = trb[:, 2 * B_YG + 4096 + p * 512 + c * 128: 2 * B_YG + 4096 + p * 512 + c * 128 + 128]
            hh = pb.emit("v", lambda e, ygh=ygh: e.tensor_copy(ygh, TRv[:, B_RT:B_RT + 128]),
                         waits=[hm])
            hlast = pb.emit("v", lambda e, ygl=ygl, ygh=ygh: e.tensor_sub(
                ygl, TRv[:, B_RT:B_RT + 128], ygh), waits=[hh])
        return hlast

    def wout_group(hwout, hyg):
        trb = TRv[:, :].bitcast(mybir.dt.bfloat16)
        for o in range(8):
            bk = banks[2 + (o % 2)]
            for k in range(8):
                sth = wsb[:, k * 1024 + o * 128: k * 1024 + (o + 1) * 128]
                stl = wsb[:, 8192 + k * 1024 + o * 128: 8192 + k * 1024 + (o + 1) * 128]
                mvh = trb[:, 2 * B_YG + k * 512: 2 * B_YG + (k + 1) * 512]
                mvl = trb[:, 2 * B_YG + 4096 + k * 512: 2 * B_YG + 4096 + (k + 1) * 512]
                mm3(bk, bk.t[:, 0:512], sth, stl, mvh, mvl,
                    start=(k == 0), stop=(k == 7),
                    waits=([hwout, hyg] if k == 0 else []))
            hsf = evac("s", bk, bk.t[:, 0:512], TRv[:, B_WSC:B_WSC + 512],
                       func=AF.Identity, bias=sb["boutc"][:, o:o + 1])
            for cc in range(CPT):
                bkt = banks[6 + (cc % 2)]
                tr(bkt, bkt.t[:, 0:128],
                   TRv[:, B_WSC + cc * 128: B_WSC + (cc + 1) * 128],
                   iden[:, :], waits=[hsf])
                hsf2 = evac("v", bkt, bkt.t[:, 0:128],
                            sb["stm"][:, cc * D + o * 128: cc * D + (o + 1) * 128])
            pb.wait_only("s", [pb.cnt and ("t", pb.cnt["t"])])

    def enc_j(g, j, henc):
        trb = TRv[:, :].bitcast(mybir.dt.bfloat16)
        bk = banks[7]
        bks = banks[6]
        for k in range(8):
            sth = wsb[:, k * 2048 + j * 128: k * 2048 + (j + 1) * 128]
            stl = wsb[:, 16384 + k * 2048 + j * 128: 16384 + k * 2048 + (j + 1) * 128]
            mvh = sb["xnfmh"][:, k * T_TILE + g * GRP: k * T_TILE + (g + 1) * GRP]
            mvl = sb["xnfml"][:, k * T_TILE + g * GRP: k * T_TILE + (g + 1) * GRP]
            mm3(bk, bk.t[:, 0:GRP], sth, stl, mvh, mvl,
                start=(k == 0), stop=(k == 7), waits=([henc] if k == 0 else []))
        hv0 = evac("s", bk, bk.t[:, 0:GRP], TRv[:, C_RS2:C_RS2 + GRP],
                   func=AF.Sigmoid, bias=sb["encbc"][:, j:j + 1],
                   waits=([enc_state["s2c"]] if enc_state["s2c"] else []))
        sh_ap = trb[:, 2 * C_SPB: 2 * C_SPB + GRP]
        sl_ap = trb[:, 2 * C_SPB + GRP: 2 * C_SPB + 2 * GRP]
        hv1 = pb.emit("v", lambda e, sh_ap=sh_ap: e.tensor_copy(sh_ap, TRv[:, C_RS2:C_RS2 + GRP]),
                      waits=[hv0])
        hv2 = pb.emit("v", lambda e, sl_ap=sl_ap, sh_ap=sh_ap: e.tensor_sub(
            sl_ap, TRv[:, C_RS2:C_RS2 + GRP], sh_ap), waits=[hv1])
        h = mm3(bks, bks.t[:, 0:GRP], sb["s2ch"][:, j * 128:(j + 1) * 128],
                sb["s2cl"][:, j * 128:(j + 1) * 128], sh_ap, sl_ap,
                start=(j == 0), stop=(j == 15), waits=[hv2])
        enc_state["s2c"] = h if h else enc_state["s2c"]
        if j == 15:
            hc = evac("v", bks, bks.t[:, 0:GRP], TRv[:, C_CONT:C_CONT + GRP])
            enc_state["cont"] = hc
    def router_group(g, hcont):
        contg = TRv[:, C_CONT + g * GRP: C_CONT + (g + 1) * GRP]
        bk = banks[5]
        mm(bk, bk.t[0:64, 0:GRP], sb["rw1"][:, :], contg[0:64, :], start=True,
           stop=True, waits=[hcont])
        hrh = evac("s", bk, bk.t[0:64, 0:GRP], sb["rhsb"][:, :], func=AF.Relu,
                   bias=sb["rb1c"][:, 0:1])
        for ci in range(CPT):
            cc = CPT * g + ci
            bkl = banks[6]
            mm(bkl, bkl.t[:, 0:8], sb["rhsb"][:, ci * 128:(ci + 1) * 128],
               sb["rw2"][:, :], start=True, stop=True, waits=[hrh])
            hl = rd_bank(pb, "v", bkl, lambda e, bkl=bkl: e.tensor_add(
                sb["logit"][:, :], bkl.t[:, 0:8], sb["rb2rep"][:, :]))
            hm = pb.emit("v", lambda e: e.reduce_max(sb["m1c"][:, :], sb["logit"][:, :], AX.X, negate=True),
                         waits=[hl])
            hm2_ = pb.emit("v", lambda e: e.tensor_scalar(sb["eu"][:, :], sb["logit"][:, :],
                                                          sb["m1c"][:, 0:1], None, ALU.add), waits=[hm])
            he = pb.emit("s", lambda e: e.activation(out=sb["eu"][:, :], in_=sb["eu"][:, :],
                                                     func=AF.Exp), waits=[hm2_])
            h1 = pb.emit("v", lambda e: e.reduce_max(sb["m1c"][:, :], sb["eu"][:, :], AX.X), waits=[he])
            h2 = pb.emit("v", lambda e: e.tensor_scalar(sb["combm"][:, 0:8], sb["eu"][:, :], sb["m1c"][:, 0:1],
                                                        None, ALU.is_ge), waits=[h1])
            h3 = pb.emit("v", lambda e: e.select(sb["selb"][:, :], sb["combm"][:, 0:8], sb["iota8"][:, :],
                                                 sb["big8"][:, :]), waits=[h2])
            h4 = pb.emit("v", lambda e: e.tensor_reduce(sb["i1c"][:, :], sb["selb"][:, :], AX.X, ALU.min),
                         waits=[h3])
            h5 = pb.emit("v", lambda e: e.tensor_scalar(sb["combm"][:, 0:8], sb["iota8"][:, :], sb["i1c"][:, 0:1],
                                                        None, ALU.is_equal), waits=[h4])
            h6 = pb.emit("v", lambda e: e.select(sb["eu2"][:, :], sb["combm"][:, 0:8], sb["neg1t"][:, :],
                                                 sb["eu"][:, :]), waits=[h5])
            h7 = pb.emit("v", lambda e: e.reduce_max(sb["m2c"][:, :], sb["eu2"][:, :], AX.X), waits=[h6])
            h8 = pb.emit("v", lambda e: e.tensor_scalar(sb["combm"][:, 8:16], sb["eu2"][:, :], sb["m2c"][:, 0:1],
                                                        None, ALU.is_ge), waits=[h7])
            h9 = pb.emit("v", lambda e: e.select(sb["selb"][:, :], sb["combm"][:, 8:16], sb["iota8"][:, :],
                                                 sb["big8"][:, :]), waits=[h8])
            h10 = pb.emit("v", lambda e: e.tensor_reduce(sb["i2c"][:, :], sb["selb"][:, :], AX.X, ALU.min),
                          waits=[h9])
            h11 = pb.emit("v", lambda e: e.tensor_scalar(sb["combm"][:, 8:16], sb["iota8"][:, :], sb["i2c"][:, 0:1],
                                                         None, ALU.is_equal), waits=[h10])
            hd = pb.emit("v", lambda e: e.tensor_add(sb["dnc"][:, :], sb["m1c"][:, :], sb["m2c"][:, :]),
                         waits=[h7])
            hd2 = pb.emit("v", lambda e: e.reciprocal(out=sb["dnc"][:, :], in_=sb["dnc"][:, :]), waits=[hd])
            hw1 = pb.emit("v", lambda e: e.tensor_mul(sb["w1c"][:, :], sb["m1c"][:, :], sb["dnc"][:, :]),
                          waits=[hd2])
            hw2 = pb.emit("v", lambda e: e.tensor_mul(sb["w2c"][:, :], sb["m2c"][:, :], sb["dnc"][:, :]),
                          waits=[hd2])
            hc1 = pb.emit("v", lambda e: e.tensor_scalar(sb["selb"][:, :],
                                                         sb["combm"][:, 0:8], sb["w1c"][:, 0:1], None,
                                                         ALU.mult), waits=[h5, hw1, h10])
            hc2 = pb.emit("v", lambda e, cc=cc: e.scalar_tensor_tensor(
                sb["comb"][:, cc * 8:(cc + 1) * 8], sb["combm"][:, 8:16], sb["w2c"][:, 0:1],
                sb["selb"][:, :], ALU.mult, ALU.add), waits=[h11, hw2, hc1])
            if DEBUG and cc == 1 and router_state.get("tile") == N_WARM_TILES:
                pb.barrier()
                pb.emit("g", lambda e: e.dma_start(out=dbg["eu"][:, :], in_=sb["eu"][:, :]), dma="B")
                pb.emit("g", lambda e: e.dma_start(out=dbg["logit"][:, :], in_=sb["logit"][:, :]), dma="B")
                pb.emit("g", lambda e: e.dma_start(out=dbg["msk"][:, :], in_=sb["combm"][:, :].bitcast(F32)[:, 0:16]), dma="B")
                pb.barrier()
            bkt = banks[7]
            tr(bkt, bkt.t[0:8, 0:128], sb["comb"][:, cc * 8:(cc + 1) * 8], iden[:, :],
               waits=[hc2])
            evac("v", bkt, bkt.t[0:8, 0:128], sb["combT"][:, cc * 128:(cc + 1) * 128])

    def experts_group(g, hexp):
        contg = TRv[:, C_CONT + g * GRP: C_CONT + (g + 1) * GRP]
        cbh = sb["contb"][:, g * GRP:(g + 1) * GRP]
        cbl = sb["contb"][:, T_TILE + g * GRP: T_TILE + (g + 1) * GRP]
        prev_eo = None
        for q in range(4):
            bkA, bkB = banks[q % 2], banks[2 + q % 2]
            ehh = {}
            for j in range(8):
                for half, bk in ((0, bkA), (1, bkB)):
                    sth = wsb[half * 64:half * 64 + 64,
                              q * 1024 + j * 128: q * 1024 + (j + 1) * 128]
                    stl = wsb[half * 64:half * 64 + 64,
                              4096 + q * 1024 + j * 128: 4096 + q * 1024 + (j + 1) * 128]
                    mh = cbh[half * 64:half * 64 + 64, :]
                    ml = cbl[half * 64:half * 64 + 64, :]
                    mm3(bk, bk.t[:, 0:GRP], sth, stl, mh, ml,
                        start=True, stop=True,
                        waits=([hexp] if j == 0 else []))
                    ehh[(j, half)] = evac(
                        "s", bk, bk.t[:, 0:GRP],
                        TRv[:, C_EH + half * 4096 + j * GRP: C_EH + half * 4096 + (j + 1) * GRP],
                        func=AF.Relu,
                        bias=sb["eb1c"][:, (2 * q + half) * 8 + j:(2 * q + half) * 8 + j + 1],
                        waits=([prev_eo] if prev_eo else []))
            for half in (0, 1):
                ee = 2 * q + half
                for ci in range(CPT):
                    cc = CPT * g + ci
                    bko = banks[4 + (ci % 2)]
                    for j in range(8):
                        prev_eo = mm(bko, bko.t[:, 0:64],
                                     TRv[:, C_EH + half * 4096 + j * GRP + ci * 128:
                                         C_EH + half * 4096 + j * GRP + ci * 128 + 128],
                                     wsb_f32[:, 4096 + ee * 512 + j * 64: 4096 + ee * 512 + (j + 1) * 64],
                                     start=(j == 0), stop=(j == 7),
                                     waits=[ehh[(j, half)]])
                    moeap = TRv[:, C_MOE + cc * 64: C_MOE + (cc + 1) * 64]
                    tmp64 = TRv[:, C_EO:C_EO + 64]
                    if ee == 0:
                        rd_bank(pb, "v", bko, lambda e, bko=bko, moeap=moeap, cc=cc: e.tensor_scalar(
                            moeap, bko.t[:, 0:64], sb["comb"][:, cc * 8:cc * 8 + 1], None, ALU.mult))
                    else:
                        hta = rd_bank(pb, "v", bko, lambda e, bko=bko, tmp64=tmp64, cc=cc, ee=ee: e.tensor_scalar(
                            tmp64, bko.t[:, 0:64], sb["comb"][:, cc * 8 + ee:cc * 8 + ee + 1], None, ALU.mult))
                        pb.emit("v", lambda e, moeap=moeap, tmp64=tmp64: e.tensor_add(
                            moeap, moeap, tmp64), waits=[hta])
        for ci in range(CPT):
            cc = CPT * g + ci
            bkb = banks[6]
            mm(bkb, bkb.t[:, 0:64], sb["combT"][:, cc * 128:(cc + 1) * 128],
               sb["eb2m"][:, :], start=True, stop=True)
            hmb = rd_bank(pb, "v", bkb, lambda e, bkb=bkb, cc=cc: e.tensor_add(
                TRv[:, C_MOE + cc * 64: C_MOE + (cc + 1) * 64],
                TRv[:, C_MOE + cc * 64: C_MOE + (cc + 1) * 64], bkb.t[:, 0:64]))
            h1 = pb.emit("v", lambda e, cc=cc: e.tensor_copy(
                TRv[:, C_MTMP:C_MTMP + 64], TRv[:, C_MOE + cc * 64: C_MOE + (cc + 1) * 64]),
                waits=[hmb])
            h2 = pb.emit("v", lambda e: e.tensor_copy(
                TRv[:, C_MTMP + 64:C_MTMP + 128], TRv[:, C_MTMP:C_MTMP + 64]), waits=[h1])
            bkt = banks[7]
            trb2 = TRv[:, :].bitcast(mybir.dt.bfloat16)
            tr(bkt, bkt.t[:, 0:128], TRv[:, C_MTMP:C_MTMP + 128], iden[:, :], waits=[h2])
            mfh = trb2[:, 2 * C_MFD + cc * 128: 2 * C_MFD + (cc + 1) * 128]
            mfl = trb2[:, 2 * C_MFD + 512 + cc * 128: 2 * C_MFD + 512 + (cc + 1) * 128]
            hq1 = evac("v", bkt, bkt.t[:, 0:128], mfh)
            rd_bank(pb, "v", bkt, lambda e, mfl=mfl, bkt=bkt, mfh=mfh: e.tensor_sub(
                mfl, bkt.t[:, 0:128], mfh), waits=[hq1])

    def ffn_dec_group(g, tile_idx, hdec):
        trb = TRv[:, :].bitcast(mybir.dt.bfloat16)
        mfh_a = trb[:, 2 * C_MFD: 2 * C_MFD + GRP]
        mfl_a = trb[:, 2 * C_MFD + 512: 2 * C_MFD + 512 + GRP]
        rat_h = {}
        prev_split = {C_RS: None}
        for jp in range(8):
            j0, j1 = 2 * jp, 2 * jp + 1
            bkA, bkB = banks[jp % 2], banks[2 + jp % 2]
            for (jj, bk_, rlo) in ((j0, bkA, 0), (j1, bkB, 64)):
                mm3(bk_, bk_.t[:, 0:GRP], sb["c2srh"][rlo:rlo + 64, jj * 128:(jj + 1) * 128],
                    sb["c2srl"][rlo:rlo + 64, jj * 128:(jj + 1) * 128],
                    mfh_a[rlo:rlo + 64, :], mfl_a[rlo:rlo + 64, :],
                    start=True, stop=True)
            for (jj, bk_) in ((j0, bkA), (j1, bkB)):
                hv0 = evac("s", bk_, bk_.t[:, 0:GRP],
                           TRv[:, C_RS:C_RS + GRP],
                           func=AF.Sigmoid, bias=sb["c2sbc"][:, jj:jj + 1],
                           waits=([prev_split[C_RS]] if prev_split[C_RS] else []))
                rh_ap = trb[:, 2 * C_RAT + jj * GRP: 2 * C_RAT + (jj + 1) * GRP]
                rl_ap = trb[:, 2 * C_RAT + 8192 + jj * GRP: 2 * C_RAT + 8192 + (jj + 1) * GRP]
                hv1 = pb.emit("v", lambda e, rh_ap=rh_ap: e.tensor_copy(
                    rh_ap, TRv[:, C_RS:C_RS + GRP]), waits=[hv0])
                rat_h[jj] = pb.emit("v", lambda e, rl_ap=rl_ap, rh_ap=rh_ap: e.tensor_sub(
                    rl_ap, TRv[:, C_RS:C_RS + GRP], rh_ap), waits=[hv1])
                prev_split[C_RS] = rat_h[jj]
        hdtm = {}
        prev_tr = None
        for o in range(8):
            bk = banks[o % 4]
            for k in range(16):
                sth = wsb[:, k * 1024 + o * 128: k * 1024 + (o + 1) * 128]
                stl = wsb[:, 16384 + k * 1024 + o * 128: 16384 + k * 1024 + (o + 1) * 128]
                rh_ap = trb[:, 2 * C_RAT + k * GRP: 2 * C_RAT + (k + 1) * GRP]
                rl_ap = trb[:, 2 * C_RAT + 8192 + k * GRP: 2 * C_RAT + 8192 + (k + 1) * GRP]
                mm3(bk, bk.t[:, 0:GRP], sth, stl, rh_ap, rl_ap,
                    start=(k == 0), stop=(k == 15),
                    waits=[rat_h[k]] + ([hdec] if k == 0 else []))
            hdv = evac("s", bk, bk.t[:, 0:GRP], TRv[:, C_RS:C_RS + GRP],
                       func=AF.Sigmoid, bias=sb["decbc"][:, o:o + 1],
                       waits=([prev_tr] if prev_tr else []))
            for ci in range(CPT):
                bkt = banks[4 + (ci % 2)]
                prev_tr = tr(bkt, bkt.t[:, 0:128],
                             TRv[:, C_RS + ci * 128: C_RS + (ci + 1) * 128],
                             iden[:, :], waits=[hdv])
                hdtm[(o, ci)] = evac("v", bkt, bkt.t[:, 0:128],
                                     TRv[:, C_DTM + ci * 1024 + o * 128: C_DTM + ci * 1024 + (o + 1) * 128])
        for ci in range(CPT):
            cc = ci
            dtm = TRv[:, C_DTM + ci * 1024:C_DTM + (ci + 1) * 1024]
            hlast = hdtm[(7, ci)]
            hx = pb.emit("y", lambda e, cc=cc, tile_idx=tile_idx: e.dma_start(
                out=TRv[:, C_XR:C_XR + 1024],
                in_=din["xin"][WARM + tile_idx * T_TILE + cc * L: WARM + tile_idx * T_TILE + (cc + 1) * L, :]),
                waits=[hlast], dma="A")
            hs1 = pb.emit("v", lambda e, dtm=dtm: e.reduce_sum(sb["mucol"][:, 0:1], dtm, AX.X),
                          waits=[hlast])
            hsq = pb.emit("s", lambda e, dtm=dtm: e.activation(out=TRv[:, C_G:C_G + 1024], in_=dtm,
                                                               func=AF.Square), waits=[hlast])
            hs2 = pb.emit("v", lambda e: e.reduce_sum(sb["s2col"][:, 0:1], TRv[:, C_G:C_G + 1024], AX.X),
                          waits=[hsq])
            hmu = pb.emit("v", lambda e: e.tensor_single_scalar(sb["mucol"][:, 0:1], sb["mucol"][:, 0:1], 1.0 / D, ALU.mult),
                          waits=[hs1])
            hm2 = pb.emit("v", lambda e: e.tensor_single_scalar(sb["m2col"][:, 0:1], sb["s2col"][:, 0:1], 1.0 / D, ALU.mult),
                          waits=[hs2])
            hv1 = pb.emit("v", lambda e: e.tensor_mul(sb["varcol"][:, 0:1], sb["mucol"][:, 0:1], sb["mucol"][:, 0:1]),
                          waits=[hmu])
            hv2 = pb.emit("v", lambda e: e.tensor_sub(sb["s2col"][:, 0:1], sb["m2col"][:, 0:1], sb["varcol"][:, 0:1]),
                          waits=[hm2, hv1])
            hv = pb.emit("v", lambda e: e.tensor_single_scalar(sb["varcol"][:, 0:1], sb["s2col"][:, 0:1], 1e-5, ALU.add),
                         waits=[hv2])
            hsd = pb.emit("s", lambda e: e.activation(out=sb["rstdcol"][:, 0:1], in_=sb["varcol"][:, 0:1], func=AF.Sqrt),
                          waits=[hv])
            hrs = pb.emit("v", lambda e: e.reciprocal(out=sb["rstdcol"][:, 0:1], in_=sb["rstdcol"][:, 0:1]),
                          waits=[hsd])
            hn = pb.emit("v", lambda e, dtm=dtm: e.tensor_scalar(TRv[:, C_G:C_G + 1024], dtm,
                                                                 sb["mucol"][:, 0:1], sb["rstdcol"][:, 0:1],
                                                                 ALU.subtract, ALU.mult), waits=[hrs, hmu])
            hg1 = pb.emit("v", lambda e: e.tensor_mul(TRv[:, C_G:C_G + 1024], TRv[:, C_G:C_G + 1024],
                                                      sb["lngrep"][:, :]), waits=[hn])
            hg2 = pb.emit("v", lambda e: e.tensor_add(TRv[:, C_G:C_G + 1024], TRv[:, C_G:C_G + 1024],
                                                      sb["lnbrep"][:, :]), waits=[hg1])
            hsg = pb.emit("s", lambda e: e.activation(out=TRv[:, C_SG:C_SG + 1024], in_=TRv[:, C_G:C_G + 1024],
                                                      func=AF.Silu), waits=[hg2])
            hcmb = pb.emit("v", lambda e, cc=cc: e.tensor_mul(
                TRv[:, C_SG:C_SG + 1024], TRv[:, C_SG:C_SG + 1024],
                sb["stm"][:, cc * D:(cc + 1) * D]), waits=[hsg])
            hout = pb.emit("v", lambda e: e.tensor_add(TRv[:, C_OB:C_OB + 1024], TRv[:, C_XR:C_XR + 1024],
                                                       TRv[:, C_SG:C_SG + 1024]), waits=[hcmb, hx])
            pb.emit("g", lambda e, cc=cc, tile_idx=tile_idx: e.dma_start(
                out=dout[tile_idx * T_TILE + cc * L: tile_idx * T_TILE + (cc + 1) * L, :],
                in_=TRv[:, C_OB:C_OB + 1024]), waits=[hout], dma="B")
            pb.barrier()

    def tap(nm, src_ap):
        if DEBUG:
            pb.emit("g", lambda e, nm=nm, src_ap=src_ap: e.dma_start(
                out=dbg[nm][:, :], in_=src_ap), dma="B")
            pb.barrier()

    prevA1 = {}
    ha_map = {}
    for tile in range(N_WARM_TILES + N_MAIN_TILES):
        warm = tile < N_WARM_TILES
        tile_idx = tile - N_WARM_TILES
        row0 = tile * T_TILE
        if tile != N_WARM_TILES:
            phase_A(row0)
        pb.barrier()
        henc = None if warm else load_enc()
        if tile == N_WARM_TILES:
            tap("xn", sb["xnfmh"][:, :4096])
        enc_state["s2c"] = None
        enc_state["cont"] = None
        snap = {}
        for c in range(CPT):
            evh, ha = proj_chunk(c, warm, snap)
            if not warm:
                enc_j(0, 4 * c + 0, henc)
            hfT, hdT, hcT, heT = decay_chunk(ha, warm, snap)
            if not warm:
                enc_j(0, 4 * c + 1, henc)
            hx = xb_transpose(c, evh, warm, snap)
            if not warm:
                enc_j(0, 4 * c + 2, henc)
            hy, hrt = scan_chunk(c, evh, hx, hfT, hdT, hcT, heT, warm, snap)
            if warm:
                snap = {"t": ("t", pb.cnt["t"]), "v": ("v", pb.cnt["v"]), "s": ("s", pb.cnt["s"])}
                emit_split_jobs(9)
                ha_map[c] = ha
                if tile == N_WARM_TILES - 1 and c >= 1:
                    phase_A_chunk(N_WARM_TILES * T_TILE, c - 1, prevA1,
                                  base=6144, bksel=(6, 7), xn_guard=ha_map[c - 1])
            else:
                enc_j(0, 4 * c + 3, henc)
                hyg = yg_chunk(c, evh, hy, hrt)
                snap = {"t": ("t", pb.cnt["t"]), "v": ("v", pb.cnt["v"]), "s": ("s", pb.cnt["s"])}
            if tile == N_WARM_TILES and c == 0:
                tap("proj", TRv[:, B_PROJ:B_PROJ + 4096])
                tap("crow", sb["crow"][:, :])
                tap("asb", sb["asb"][:, :])
                tap("ytm", TRv[:, B_YTM:B_YTM + 1024])
                tap("h1", sb["htst"][:, :])
                tap("xtm", TRv[:, B_XT:B_XT + 2048])
                pb.emit("g", lambda e: e.dma_start(out=dbg["cols"][:, 0:16], in_=sb["cT"][:, :]), dma="B") if DEBUG else None
                pb.emit("g", lambda e: e.dma_start(out=dbg["cols"][:, 16:32], in_=sb["fT"][:, :]), dma="B") if DEBUG else None
                pb.emit("g", lambda e: e.dma_start(out=dbg["cols"][:, 32:48], in_=sb["eT"][:, :]), dma="B") if DEBUG else None
                pb.emit("g", lambda e: e.dma_start(out=dbg["cols"][:, 48:64], in_=sb["dT"][:, :]), dma="B") if DEBUG else None
                pb.barrier()
        if warm:
            if tile == N_WARM_TILES - 1:
                phase_A_chunk(N_WARM_TILES * T_TILE, CPT - 1, prevA1,
                              base=6144, bksel=(6, 7), xn_guard=ha_map[CPT - 1])
                h = pb.emit("v", lambda e: e.tensor_mul(sb["htst"][:, :], sb["htst"][:, :],
                                                        sb["wmask"][:, 0:1].broadcast_to([128, 512])))
                pb.barrier()
                tap("h0", sb["htst"][:, :])
            continue
        hc = enc_state["cont"]
        hcb = pb.emit("v", lambda e: e.tensor_scalar(
            TRv[:, C_CONT:C_CONT + T_TILE], TRv[:, C_CONT:C_CONT + T_TILE],
            sb["s2cbc"][:, 0:1], None, ALU.add), waits=[hc])
        hcs = pb.emit("v", lambda e: e.tensor_copy(sb["contb"][:, 0:T_TILE],
                                                   TRv[:, C_CONT:C_CONT + T_TILE]), waits=[hcb])
        pb.emit("v", lambda e: e.tensor_sub(sb["contb"][:, T_TILE:2 * T_TILE],
                                            TRv[:, C_CONT:C_CONT + T_TILE],
                                            sb["contb"][:, 0:T_TILE]), waits=[hcs])
        pb.barrier()
        hwout = load_wout()
        wout_group(hwout, hyg)
        pb.barrier()
        if tile == N_WARM_TILES:
            tap("cont", TRv[:, C_CONT:C_CONT + 512])
            tap("stm", sb["stm"][:, :4096])
        hexp = load_experts()
        router_state["tile"] = tile
        for g in range(GPT):
            router_group(g, hcb)
            experts_group(g, hexp)
        pb.barrier()
        if tile == N_WARM_TILES:
            tap("comb", sb["comb"][:, :])
            tap("moe", TRv[:, C_MOE:C_MOE + 256])
            tap("mfd", TRv[:, C_MFD:C_MFD + 512])
        hdec = load_dec()
        for g in range(GPT):
            ffn_dec_group(g, tile_idx, hdec)
        pb.barrier()
        if tile == N_WARM_TILES:
            tap("rat", TRv[:, C_RAT:C_RAT + 4096])
            tap("dec", TRv[:, C_DTM:C_DTM + 2048])

    pb.wait_only("g", [("B", pb.cnt["B"])])

    with nc.Block() as block:
        @block.tensor
        def _(t):
            for fn in pb.prog["t"]:
                fn(t)

        @block.vector
        def _(v):
            for fn in pb.prog["v"]:
                fn(v)

        @block.scalar
        def _(s):
            for fn in pb.prog["s"]:
                fn(s)

        @block.gpsimd
        def _(g):
            for fn in pb.prog["g"]:
                fn(g)

        @block.sync
        def _(y):
            for fn in pb.prog["y"]:
                fn(y)

    es.close()
    if pb.ldwf_ins:
        import dataclasses
        targets = {id(i) for i in pb.ldwf_ins}
        npatched = 0
        for bb in nc.m.functions[0].blocks:
            il = bb.instructions
            for idx, ins0 in enumerate(il):
                if id(ins0) in targets:
                    il[idx] = dataclasses.replace(ins0, ldweights=False)
                    npatched += 1
        assert npatched == len(pb.ldwf_ins), (npatched, len(pb.ldwf_ins))
    return nc


def _layout_inputs(inp):
    f32 = np.float32
    x = np.asarray(inp["L0_attn_inp"], f32)
    wA = np.asarray(inp["L0_attn_wA"], f32)
    bA = np.asarray(inp["L0_attn_bA"], f32)
    wXBC = np.asarray(inp["L0_attn_wXBC"], f32)
    bXBC = np.asarray(inp["L0_attn_bXBC"], f32)
    wG = np.asarray(inp["L0_attn_wG"], f32)
    bG = np.asarray(inp["L0_attn_bG"], f32)
    Wout = np.asarray(inp["L0_attn_Wout"], f32)
    bout = np.asarray(inp["L0_attn_bout"], f32)
    encw = np.asarray(inp["L0_ffn_enc_w"], f32)
    encb = np.asarray(inp["L0_ffn_enc_b"], f32)
    s2cw = np.asarray(inp["L0_ffn_s2c_w"], f32)
    s2cb = np.asarray(inp["L0_ffn_s2c_b"], f32)
    rw1 = np.asarray(inp["L0_ffn_r_w1"], f32)
    rb1 = np.asarray(inp["L0_ffn_r_b1"], f32)
    rw2 = np.asarray(inp["L0_ffn_r_w2"], f32)
    rb2 = np.asarray(inp["L0_ffn_r_b2"], f32)
    ew1 = np.asarray(inp["L0_ffn_e_w1"], f32)
    eb1 = np.asarray(inp["L0_ffn_e_b1"], f32)
    ew2 = np.asarray(inp["L0_ffn_e_w2"], f32)
    eb2 = np.asarray(inp["L0_ffn_e_b2"], f32)
    c2sw = np.asarray(inp["L0_ffn_c2s_w"], f32)
    c2sb = np.asarray(inp["L0_ffn_c2s_b"], f32)
    decw = np.asarray(inp["L0_ffn_dec_w"], f32)
    decb = np.asarray(inp["L0_ffn_dec_b"], f32)
    lng = np.asarray(inp["L0_ffn_ln_g"], f32)
    lnb = np.asarray(inp["L0_ffn_ln_b"], f32)

    wblk = np.zeros((8, 128, 512), f32)
    astat = np.zeros((8, 128, 16), f32)
    bxbcg = np.zeros((128, 32), f32)
    for p in range(8):
        for half in range(2):
            h = 2 * p + half
            r = slice(half * 64, half * 64 + 64)
            cix = slice(half * 64, half * 64 + 64)
            wblk[p, r, 0:128][:, cix] = wXBC[h, 0:64, :].T
            wblk[p, r, 128:256][:, cix] = wXBC[h, 64:128, :].T
            wblk[p, r, 256:384][:, cix] = wXBC[h, 128:192, :].T
            wblk[p, r, 384:512][:, cix] = wG[h, :, :].T
            astat[p, r, h] = wA[h, 0, :]
            bxbcg[r, p * 4 + 0] = bXBC[h, 0:64]
            bxbcg[r, p * 4 + 1] = bXBC[h, 64:128]
            bxbcg[r, p * 4 + 2] = bXBC[h, 128:192]
            bxbcg[r, p * 4 + 3] = bG[h, :]

    common = {
        "wblk": wblk, "astat": astat, "bxbcg": bxbcg,
        "bacol": bA[:, 0:1].copy(),
        "woutT": np.ascontiguousarray(Wout.T),
        "boutc": np.ascontiguousarray(bout.reshape(8, 128).T),
        "encw": encw, "encbc": np.ascontiguousarray(encb.reshape(16, 128).T),
        "s2cd": np.concatenate([s2cw, s2cw], axis=1),
        "s2cbc": np.concatenate([s2cb, s2cb])[:, None].copy(),
        "rw1": rw1, "rb1c": rb1[:, None].copy(), "rw2": rw2,
        "rb2rep": np.broadcast_to(rb2[None, :], (128, 8)).copy(),
        "ew1p": np.concatenate([ew1[0::2], ew1[1::2]], axis=1),
        "eb1c": np.ascontiguousarray(
            eb1.reshape(8, 8, 128).transpose(2, 0, 1).reshape(128, 64)),
        "ew2s": ew2, "eb2m": eb2,
        "c2sr": np.concatenate([c2sw, c2sw], axis=0),
        "c2sbc": np.ascontiguousarray(c2sb.reshape(16, 128).T),
        "decw": decw, "decbc": np.ascontiguousarray(decb.reshape(8, 128).T),
        "lngrep": np.broadcast_to(lng[None, :], (128, D)).copy(),
        "lnbrep": np.broadcast_to(lnb[None, :], (128, D)).copy(),
        "iden": np.eye(128, dtype=f32),
        "tripos": np.where(np.arange(128)[None, :] >= np.arange(128)[:, None],
                           0.0, 1e30).astype(f32),
        "tripost": np.ascontiguousarray(
            np.where(np.arange(128)[None, :] >= np.arange(128)[:, None],
                     0.0, 1e30).astype(f32).T),
        "iota8": np.broadcast_to(np.arange(8, dtype=f32)[None, :], (128, 8)).copy(),
        "big8": np.full((128, 8), 1e30, f32),
        "neg1t": np.full((128, 8), -1.0, f32),
        "ones16": np.ones((16, 128), f32),
    }

    in_maps = []
    for core in range(NCORE):
        b, half = core // 2, core % 2
        m0 = half * T_CORE
        w0 = m0 - WARM if half == 1 else 0
        xin = np.concatenate([x[b, w0:w0 + WARM, :], x[b, m0:m0 + T_CORE, :]],
                             axis=0)
        im = dict(common)
        im["xin"] = np.ascontiguousarray(xin)
        im["wmask"] = np.full((128, 1), 1.0 if half == 1 else 0.0, f32)
        in_maps.append(im)
    return in_maps


def kernel(**inputs):
    in_maps = _layout_inputs(inputs)
    nc = build_nc()
    res = run_bass_kernel_spmd(nc, in_maps, core_ids=list(range(NCORE)))
    out = np.empty((4, 4096, D), np.float32)
    for core in range(NCORE):
        b, half = core // 2, core % 2
        out[b, half * T_CORE:(half + 1) * T_CORE, :] = res.results[core]["out"]
    return out


# version 27
# speedup vs baseline: 1.2203x; 1.0062x over previous
import contextlib
import numpy as np
import concourse.bass as bass
import concourse.mybir as mybir
from concourse.bass_utils import run_bass_kernel_spmd

F32 = mybir.dt.float32
I32 = mybir.dt.int32
AF = mybir.ActivationFunctionType
ALU = mybir.AluOpType
AX = mybir.AxisListType

H, CH, NST, D = 16, 64, 64, 1024
HID, MOE, NEXP, EXPH = 2048, 64, 8, 1024
NCORE = 8
T_CORE = 2048
WARM = 512
T_TILE = 512
L = 128
CPT = T_TILE // L
N_WARM_TILES = WARM // T_TILE
N_MAIN_TILES = T_CORE // T_TILE
GRP = 512
GPT = T_TILE // GRP
DEBUG = False


class PB:

    ENGS = ("t", "v", "s", "g", "y")

    def __init__(self, nc, es):
        self.nc = nc
        self.prog = {e: [] for e in self.ENGS}
        self.sem = {k: es.enter_context(nc.semaphore("sem_" + k))
                    for k in ("t", "v", "s", "g", "A", "B")}
        self.cnt = {k: 0 for k in ("t", "v", "s", "g", "A", "B")}
        self.ldwf_ins = []

    def _skey(self, eng):
        return {"t": "t", "v": "v", "s": "s", "g": "g"}[eng]

    def emit(self, eng, build, waits=(), dma=None, inc=True, ldwf=False):
        sem, cnt = self.sem, self.cnt
        wl = [(k, c) for (k, c) in waits if c is not None and c > 0]
        if dma is None:
            key, amt = self._skey(eng), 1
        else:
            key, amt = dma, 16
        if inc:
            cnt[key] += amt
            target = cnt[key]
        else:
            target = None

        def fn(e, build=build, wl=wl, key=key, inc=inc, ldwf=ldwf, self_=self):
            for (k, c) in wl:
                e.wait_ge(sem[k], c)
            ins = build(e)
            if inc:
                ins.then_inc(sem[key], 1 if key not in ("A", "B") else 16)
            if ldwf:
                self_.ldwf_ins.append(ins.ins)
        self.prog[eng].append(fn)
        return (key, target) if inc else None

    def wait_only(self, eng, waits):
        sem = self.sem
        wl = [(k, c) for (k, c) in waits if c > 0]
        if not wl:
            return

        def fn(e, wl=wl):
            for (k, c) in wl:
                e.wait_ge(sem[k], c)
        self.prog[eng].append(fn)

    def barrier(self):
        snap = dict(self.cnt)
        for eng in ("t", "v", "s", "g", "y"):
            self.wait_only(eng, [(k, snap[k]) for k in ("t", "v", "s", "g", "A", "B")])


class Bank:
    def __init__(self, tensor):
        self.t = tensor
        self.wr = None
        self.rd = {}


def mk_mm(pb):
    def mm(bank, out_ap, lhsT, rhs, start, stop, waits=(), tp=None):
        w = [x for x in waits if x is not None]
        if start:
            w += list(bank.rd.items())
            if bank.wr is not None:
                w.append(bank.wr)

        def b(e, out_ap=out_ap, lhsT=lhsT, rhs=rhs, start=start, stop=stop, tp=tp):
            return e.matmul(out_ap, lhsT, rhs, start=start, stop=stop,
                            tile_position=tp)
        h = pb.emit("t", b, waits=w, inc=stop)
        if stop:
            bank.wr = h
            bank.rd = {}
        return h
    return mm


def mk_tr(pb):
    def tr(bank, out_ap, in_ap, iden_ap, waits=()):
        w = list(waits) + list(bank.rd.items())
        if bank.wr is not None:
            w.append(bank.wr)

        def b(e, out_ap=out_ap, in_ap=in_ap, iden_ap=iden_ap):
            return e.transpose(out_ap, in_ap, iden_ap)
        h = pb.emit("t", b, waits=w)
        bank.wr = h
        bank.rd = {}
        return h
    return tr


def rd_bank(pb, eng, bank, build, waits=()):
    w = list(waits)
    if bank.wr is not None:
        w.append(bank.wr)
    h = pb.emit(eng, build, waits=w)
    bank.rd[h[0]] = h[1]
    return h


def build_nc():
    nc = bass.Bass("TRN2", target_bir_lowering=False, debug=False)
    din = {}
    for name, shape in [
        ("xin", [WARM + T_CORE, D]), ("wmask", [128, 1]),
        ("wblk", [8, 128, 512]), ("astat", [8, 128, 16]),
        ("bxbcg", [128, 32]), ("bacol", [16, 1]),
        ("woutT", [D, D]), ("boutc", [128, 8]),
        ("encw", [D, HID]), ("encbc", [128, 16]),
        ("s2cd", [HID, 128]), ("s2cbc", [128, 1]),
        ("rw1", [64, 64]), ("rb1c", [64, 1]), ("rw2", [64, 8]),
        ("rb2rep", [128, 8]),
        ("ew1p", [4, 128, EXPH]), ("eb1c", [128, 64]),
        ("ew2s", [NEXP, EXPH, MOE]), ("eb2m", [8, 64]),
        ("c2sr", [128, HID]), ("c2sbc", [128, 16]),
        ("decw", [HID, D]), ("decbc", [128, 8]),
        ("lngrep", [128, D]), ("lnbrep", [128, D]),
        ("iden", [128, 128]), ("tripos", [128, 128]), ("tripost", [128, 128]),
        ("iota8", [128, 8]), ("big8", [128, 8]), ("neg1t", [128, 8]),
        ("ones16", [16, 128]),
    ]:
        din[name] = nc.dram_tensor(name, shape, F32, kind="ExternalInput").ap()
    dout = nc.dram_tensor("out", [T_CORE, D], F32, kind="ExternalOutput").ap()
    BF16 = mybir.dt.bfloat16
    scr = {}
    for nm, shape in [("ench", [D, HID]), ("encl", [D, HID]),
                      ("dech", [HID, D]), ("decl", [HID, D]),
                      ("wouth", [D, D]), ("woutl", [D, D]),
                      ("ew1h", [4, 128, EXPH]), ("ew1l", [4, 128, EXPH])]:
        scr[nm] = nc.dram_tensor("scr_" + nm, shape, BF16, kind="Internal").ap()
    dbg = {}
    if DEBUG:
        for nm, shp in [("h0", [128, 512]), ("xn", [128, 4096]),
                        ("proj", [128, 4096]), ("crow", [16, 128]),
                        ("cols", [128, 64]), ("ytm", [128, 1024]),
                        ("h1", [128, 512]), ("asb", [16, 128]),
                        ("stm", [128, 4096]), ("cont", [128, 512]),
                        ("spk", [128, 4096]), ("comb", [128, 32]),
                        ("moe", [128, 256]), ("mfd", [128, 512]),
                        ("rat", [128, 4096]), ("dec", [128, 2048]),
                        ("xtm", [128, 2048]), ("ygf", [128, 2048]),
                        ("lncols", [128, 8]), ("xb", [128, 1024]),
                        ("eu", [128, 8]), ("logit", [128, 8]), ("msk", [128, 16])]:
            dbg[nm] = nc.dram_tensor("dbg_" + nm, shp, F32, kind="ExternalOutput").ap()

    es = contextlib.ExitStack()
    sb = {}

    def S(name, shape, dt=F32):
        sb[name] = es.enter_context(nc.sbuf_tensor("sb_" + name, shape, dt))
        return sb[name]

    BF = mybir.dt.bfloat16
    S("wblkh", [128, 8 * 512], BF)
    S("wblkl", [128, 8 * 512], BF)
    S("astath", [128, 8 * 16], BF)
    S("astatl", [128, 8 * 16], BF)
    S("bxbcg", [128, 32]); S("bacol", [16, 1])
    S("encbc", [128, 16]); S("s2ch", [128, 16 * 128], BF); S("s2cl", [128, 16 * 128], BF)
    S("s2cbc", [128, 1])
    S("rw1", [64, 64]); S("rb1c", [64, 1]); S("rw2", [64, 8])
    S("rb2rep", [128, 8]); S("eb1c", [128, 64]); S("eb2m", [8, 64])
    S("c2srh", [128, HID], BF); S("c2srl", [128, HID], BF); S("c2sbc", [128, 16])
    S("decbc", [128, 8]); S("boutc", [128, 8])
    S("lngrep", [128, D]); S("lnbrep", [128, D])
    S("iden", [128, 128]); S("tripos", [128, 128]); S("tripost", [128, 128])
    S("iota8", [128, 8]); S("big8", [128, 8]); S("neg1t", [128, 8])
    S("ones16", [16, 128]); S("wmask", [128, 1])
    S("wstrb", [128, 32768], BF)
    S("xnfmh", [128, 8 * T_TILE], BF)
    S("xnfml", [128, 8 * T_TILE], BF)
    S("contb", [128, 2 * T_TILE], BF)
    S("stm", [128, CPT * D])
    S("htst", [128, 512])
    S("tr", [128, 13440])
    S("asb", [16, L]); S("sp16", [16, L]); S("crow", [16, L])
    S("frow", [16, L]); S("erow", [16, L]); S("dfree", [16, 128])
    S("cT", [128, 16]); S("fT", [128, 16]); S("eT", [128, 16]); S("dT", [128, 16])
    S("mucol", [128, 2]); S("s2col", [128, 2]); S("m2col", [128, 2])
    S("varcol", [128, 2]); S("rstdcol", [128, 2])
    S("bnst", [128, 24]); S("bnag", [128, 16])
    S("rhsb", [64, GRP])
    S("combm", [128, CPT * 8], I32)
    S("comb", [128, CPT * 8])
    S("combT", [8, CPT * 128])
    S("m1c", [128, 1]); S("m2c", [128, 1]); S("i1c", [128, 1]); S("i2c", [128, 1])
    S("dnc", [128, 1]); S("w1c", [128, 1]); S("w2c", [128, 1])
    S("logit", [128, 8]); S("eu", [128, 8]); S("eu2", [128, 8]); S("selb", [128, 8])
    S("clast", [16, 1]); S("dcol", [16, 1])

    pb = PB(nc, es)
    mm = mk_mm(pb)
    tr = mk_tr(pb)

    def mm3(bank, out_ap, sth, stl, mvh, mvl, start, stop, waits=()):
        w = [x for x in waits if x is not None]
        if start:
            w += list(bank.rd.items())
            if bank.wr is not None:
                w.append(bank.wr)
        if sth.base_partition() != 0:
            pb.emit("t", lambda e, out_ap=out_ap, sth=sth, mvh=mvh, start=start: e.matmul(
                out_ap, sth, mvh, start=start, stop=False), waits=w, inc=False)
            pb.emit("t", lambda e, out_ap=out_ap, sth=sth, mvl=mvl: e.matmul(
                out_ap, sth, mvl, start=False, stop=False), inc=False)
            h = pb.emit("t", lambda e, out_ap=out_ap, stl=stl, mvh=mvh, stop=stop: e.matmul(
                out_ap, stl, mvh, start=False, stop=stop), inc=stop)
            if stop:
                bank.wr = h
                bank.rd = {}
            return h
        pb.emit("t", lambda e, sth=sth: e.ldweights(sth), waits=w, inc=False)
        pb.emit("t", lambda e, out_ap=out_ap, sth=sth, mvh=mvh, start=start: e.matmul(
            out_ap, sth, mvh, start=start, stop=False), inc=False, ldwf=True)
        pb.emit("t", lambda e, out_ap=out_ap, sth=sth, mvl=mvl: e.matmul(
            out_ap, sth, mvl, start=False, stop=False), inc=False, ldwf=True)
        pb.emit("t", lambda e, stl=stl: e.ldweights(stl), inc=False)
        h = pb.emit("t", lambda e, out_ap=out_ap, stl=stl, mvh=mvh, stop=stop: e.matmul(
            out_ap, stl, mvh, start=False, stop=stop), inc=stop, ldwf=True)
        if stop:
            bank.wr = h
            bank.rd = {}
        return h
    banks = [Bank(es.enter_context(nc.psum_tensor("pb%d" % i, [128, 512], F32)))
             for i in range(8)]

    TRv = sb["tr"]
    A_XB, A_X2, A_XN = 0, 1024, 2048
    B_PROJ = 0
    B_XT = 4096
    B_YTM = 6144
    B_YG = 7168
    B_RT, B_GB, B_MT = 11264, 11392, 11520
    B_YT2, B_BW = 11648, 11712
    B_WSC = 11264
    C_RS2 = 11840
    C_SPB = 12352
    C_CONT = 12864
    C_EH = 0
    C_MOE = 9216
    C_MFD = 9472
    C_MTMP = 9984
    C_RAT = 0
    C_RS = 8192
    C_DTM = 8704
    C_G, C_SG, C_XR, C_OB = 0, 1024, 2048, 3072
    C_EO = 13056

    iden = sb["iden"]

    def evac(eng, bank, psrc, dst, func=None, bias=None, scale=1.0, waits=()):
        if eng == "s":
            def b(e, dst=dst, psrc=psrc, func=func, bias=bias, scale=scale):
                return e.activation(out=dst, in_=psrc,
                                    func=(func or AF.Identity),
                                    bias=(0.0 if bias is None else bias),
                                    scale=scale)
        else:
            def b(e, dst=dst, psrc=psrc):
                return e.tensor_copy(dst, psrc)
        return rd_bank(pb, eng, bank, b, waits=waits)

    ld = {}
    init_loads = []
    for name in ["bxbcg", "bacol", "encbc", "s2cbc", "rw1", "rb1c", "rw2",
                 "rb2rep", "eb1c", "eb2m", "c2sbc", "decbc", "boutc",
                 "lngrep", "lnbrep", "iden", "tripos", "tripost", "iota8", "big8",
                 "neg1t", "ones16", "wmask"]:
        init_loads.append((name, sb[name], [(din[name][:, :], sb[name][:, :])]))

    for name, _sbuf, pairs in init_loads:
        for (src, dst) in pairs:
            h = pb.emit("y", lambda e, dst=dst, src=src: e.dma_start(out=dst, in_=src),
                        dma="A")
        ld[name] = h
    init_done = pb.cnt["A"]
    pb.barrier()
    pb.emit("v", lambda e: e.memset(sb["htst"][:, :], 0.0))
    pb.emit("v", lambda e: e.memset(TRv[:, B_BW:B_BW + 128], 0.0))
    pb.barrier()

    wsb = sb["wstrb"]
    wsb_f32 = wsb[:, :].bitcast(F32)

    def split_to(dst_h, dst_l, f32_ap, hload):
        h1 = pb.emit("s", lambda e, dst_h=dst_h, f32_ap=f32_ap: e.activation(
            out=dst_h, in_=f32_ap, func=AF.Identity), waits=[hload])
        h2 = pb.emit("g", lambda e, dst_l=dst_l, f32_ap=f32_ap, dst_h=dst_h: e.tensor_sub(
            dst_l, f32_ap, dst_h), waits=[h1])
        return h2

    ph0 = {}
    for p in range(8):
        pp = p % 2
        of = pp * 1024
        wd = [h for h in (ph0.get(pp),) if h]
        hl = pb.emit("y", lambda e, p=p, of=of: e.dma_start(out=TRv[:, of:of + 512], in_=din["wblk"][p, :, :]),
                     waits=wd, dma="A")
        hs = split_to(sb["wblkh"][:, p * 512:(p + 1) * 512], sb["wblkl"][:, p * 512:(p + 1) * 512],
                      TRv[:, of:of + 512], hl)
        hl2 = pb.emit("y", lambda e, p=p, of=of: e.dma_start(out=TRv[:, of + 512:of + 528], in_=din["astat"][p, :, :]),
                      waits=[hs], dma="A")
        hs2 = split_to(sb["astath"][:, p * 16:(p + 1) * 16], sb["astatl"][:, p * 16:(p + 1) * 16],
                       TRv[:, of + 512:of + 528], hl2)
        ph0[pp] = hs2
    ph1 = {}
    for k in range(16):
        kp = k % 2
        of = kp * 128
        wd = [h for h in (ph1.get(kp),) if h]
        hl = pb.emit("y", lambda e, k=k, of=of: e.dma_start(out=TRv[:, 2048 + of:2048 + of + 128],
                                                            in_=din["s2cd"][k * 128:(k + 1) * 128, :]),
                     waits=wd, dma="A")
        hs = split_to(sb["s2ch"][:, k * 128:(k + 1) * 128], sb["s2cl"][:, k * 128:(k + 1) * 128],
                      TRv[:, 2048 + of:2048 + of + 128], hl)
        ph1[kp] = hs
    hl = pb.emit("y", lambda e: e.dma_start(out=TRv[:, 2432:4480], in_=din["c2sr"][:, :]), dma="A")
    hs = split_to(sb["c2srh"][:, :], sb["c2srl"][:, :], TRv[:, 2432:4480], hl)
    big_specs = [("enc", din["encw"], scr["ench"], scr["encl"], 8, 2048),
                 ("wout", din["woutT"], scr["wouth"], scr["woutl"], 8, 1024),
                 ("dec", din["decw"], scr["dech"], scr["decl"], 16, 1024)]
    jobs = []
    for (kind, dsrc, dh, dl, nchunk, cw) in big_specs:
        for k in range(nchunk):
            jobs.append((kind, dsrc[k * 128:(k + 1) * 128, :], dh[k * 128:(k + 1) * 128, :],
                         dl[k * 128:(k + 1) * 128, :], cw))
    for q in range(4):
        jobs.append(("ew1", din["ew1p"][q, :, :], scr["ew1h"][q, :, :], scr["ew1l"][q, :, :], 1024))
    store_h = {}
    split_ph = {}
    split_pos = [0]

    def emit_split_jobs(n):
        i0 = split_pos[0]
        for i in range(i0, min(i0 + n, len(jobs))):
            kind, jsrc, jdh, jdl, cw = jobs[i]
            p = i % 2
            trof = 8192 + p * 2048
            wbof = p * 4096
            wd = [h for h in (split_ph.get(("sp", p)),) if h]
            hl = pb.emit("y", lambda e, jsrc=jsrc, trof=trof, cw=cw: e.dma_start(
                out=wsb_f32[:, trof:trof + cw], in_=jsrc), waits=wd, dma="A")
            ws = [hl] + [h for h in (split_ph.get(("st", p)),) if h]
            h1 = pb.emit("s", lambda e, wbof=wbof, trof=trof, cw=cw: e.activation(
                out=wsb[:, wbof:wbof + cw], in_=wsb_f32[:, trof:trof + cw],
                func=AF.Identity), waits=ws)
            h2 = pb.emit("g", lambda e, wbof=wbof, trof=trof, cw=cw: e.tensor_sub(
                wsb[:, wbof + cw:wbof + 2 * cw], wsb_f32[:, trof:trof + cw],
                wsb[:, wbof:wbof + cw]), waits=[h1])
            split_ph[("sp", p)] = h2
            pb.emit("g", lambda e, jdh=jdh, wbof=wbof, cw=cw: e.dma_start(
                out=jdh, in_=wsb[:, wbof:wbof + cw]), waits=[h2], dma="B")
            hs2 = pb.emit("g", lambda e, jdl=jdl, wbof=wbof, cw=cw: e.dma_start(
                out=jdl, in_=wsb[:, wbof + cw:wbof + 2 * cw]), waits=[h2], dma="B")
            split_ph[("st", p)] = hs2
            store_h[kind] = hs2
        split_pos[0] = min(i0 + n, len(jobs))
    pb.wait_only("y", [h for h in (ph0.get(0), ph0.get(1), ph1.get(0), ph1.get(1), hs) if h])
    pb.barrier()

    def load_wout():
        h = None
        for k in range(8):
            pb.emit("y", lambda e, k=k: e.dma_start(
                out=wsb[:, k * 1024:(k + 1) * 1024],
                in_=scr["wouth"][k * 128:(k + 1) * 128, :]), dma="A",
                waits=([store_h["wout"]] if (k == 0 and "wout" in store_h) else []))
            h = pb.emit("y", lambda e, k=k: e.dma_start(
                out=wsb[:, 8192 + k * 1024:8192 + (k + 1) * 1024],
                in_=scr["woutl"][k * 128:(k + 1) * 128, :]), dma="A")
        return h

    def load_enc():
        h = None
        for k in range(8):
            pb.emit("y", lambda e, k=k: e.dma_start(
                out=wsb[:, k * 2048:(k + 1) * 2048],
                in_=scr["ench"][k * 128:(k + 1) * 128, :]), dma="A",
                waits=([store_h["enc"]] if (k == 0 and "enc" in store_h) else []))
            h = pb.emit("y", lambda e, k=k: e.dma_start(
                out=wsb[:, 16384 + k * 2048:16384 + (k + 1) * 2048],
                in_=scr["encl"][k * 128:(k + 1) * 128, :]), dma="A")
        return h

    def load_experts():
        for q in range(4):
            pb.emit("y", lambda e, q=q: e.dma_start(
                out=wsb[:, q * 1024:(q + 1) * 1024], in_=scr["ew1h"][q, :, :]), dma="A",
                waits=([store_h["ew1"]] if (q == 0 and "ew1" in store_h) else []))
            pb.emit("y", lambda e, q=q: e.dma_start(
                out=wsb[:, 4096 + q * 1024:4096 + (q + 1) * 1024], in_=scr["ew1l"][q, :, :]), dma="A")
        h = None
        for ee in range(NEXP):
            for k in range(8):
                h = pb.emit("y", lambda e, ee=ee, k=k: e.dma_start(
                    out=wsb_f32[:, 4096 + ee * 512 + k * 64: 4096 + ee * 512 + (k + 1) * 64],
                    in_=din["ew2s"][ee, k * 128:(k + 1) * 128, :]), dma="A")
        return h

    def load_dec():
        h = None
        for k in range(16):
            pb.emit("y", lambda e, k=k: e.dma_start(
                out=wsb[:, k * 1024:(k + 1) * 1024],
                in_=scr["dech"][k * 128:(k + 1) * 128, :]), dma="A",
                waits=([store_h["dec"]] if (k == 0 and "dec" in store_h) else []))
            h = pb.emit("y", lambda e, k=k: e.dma_start(
                out=wsb[:, 16384 + k * 1024:16384 + (k + 1) * 1024],
                in_=scr["decl"][k * 128:(k + 1) * 128, :]), dma="A")
        return h

    def phase_A_chunk(xin_row0, c, prev, base=0, bksel=(0, 1), xn_guard=None):
        lastev = None
        if True:
            cp = c % 2
            XB, XN = base + cp * 2048, base + cp * 2048 + 1024
            mcol = sb["mucol"][:, cp:cp + 1]
            vcol = sb["varcol"][:, cp:cp + 1]
            rcol = sb["rstdcol"][:, cp:cp + 1]
            bnst = sb["bnst"][:, cp * 12:(cp + 1) * 12]
            ag = sb["bnag"][:, cp * 8:(cp + 1) * 8]
            w_dma = [h for h in (prev.get(("xn", cp)),) if h]
            hdma = pb.emit("y", lambda e, XB=XB, c=c: e.dma_start(
                out=TRv[:, XB:XB + 1024],
                in_=din["xin"][xin_row0 + c * L: xin_row0 + (c + 1) * L, :]),
                waits=w_dma, dma="A")
            pb.emit("v", lambda e, XB=XB, bnst=bnst: e.bn_stats(bnst[:, 0:6], TRv[:, XB:XB + 512]),
                    waits=[hdma])
            pb.emit("v", lambda e, XB=XB, bnst=bnst: e.bn_stats(bnst[:, 6:12], TRv[:, XB + 512:XB + 1024]))
            pb.emit("v", lambda e, ag=ag, bnst=bnst: e.bn_aggr(ag[:, 0:2], bnst[:, 0:6]))
            ha2 = pb.emit("v", lambda e, ag=ag, bnst=bnst: e.bn_aggr(ag[:, 2:4], bnst[:, 6:12]))
            hsm = pb.emit("v", lambda e, ag=ag: e.tensor_add(ag[:, 4:6], ag[:, 0:2], ag[:, 2:4]),
                          waits=[ha2])
            hdf = pb.emit("v", lambda e, ag=ag: e.tensor_sub(ag[:, 6:7], ag[:, 0:1], ag[:, 2:3]),
                          waits=[ha2])
            hmu = pb.emit("v", lambda e, mcol=mcol, ag=ag: e.tensor_single_scalar(
                mcol, ag[:, 4:5], 0.5, ALU.mult), waits=[hsm])
            hd2 = pb.emit("v", lambda e, ag=ag: e.tensor_mul(ag[:, 7:8], ag[:, 6:7], ag[:, 6:7]),
                          waits=[hdf])
            hv2 = pb.emit("v", lambda e, vcol=vcol, ag=ag: e.scalar_tensor_tensor(
                vcol, ag[:, 7:8], 0.5, ag[:, 5:6], ALU.mult, ALU.add), waits=[hd2, hsm])
            hv = pb.emit("v", lambda e, vcol=vcol: e.tensor_single_scalar(
                vcol, vcol, 2e-5, ALU.add), waits=[hv2])
            hsd = pb.emit("s", lambda e, rcol=rcol, vcol=vcol: e.activation(
                out=rcol, in_=vcol, func=AF.Sqrt, scale=0.5), waits=[hv])
            hrs = pb.emit("v", lambda e, rcol=rcol: e.reciprocal(out=rcol, in_=rcol),
                          waits=[hsd])
            w_xn = [hrs, hmu] + ([prev.get(("tr", cp))] if prev.get(("tr", cp)) else [])
            hxn = pb.emit("v", lambda e, XB=XB, XN=XN, mcol=mcol, rcol=rcol: e.tensor_scalar(
                TRv[:, XN:XN + 1024], TRv[:, XB:XB + 1024], mcol[:, 0:1], rcol[:, 0:1],
                ALU.subtract, ALU.mult), waits=w_xn)
            prev[("xn", cp)] = hxn
            htr = None
            for f in range(8):
                bk = banks[bksel[f % 2]]
                htr = tr(bk, bk.t[:, 0:128], TRv[:, XN + f * 128:XN + (f + 1) * 128],
                         iden[:, :], waits=[hxn])
                xh = sb["xnfmh"][:, f * T_TILE + c * L: f * T_TILE + (c + 1) * L]
                xl = sb["xnfml"][:, f * T_TILE + c * L: f * T_TILE + (c + 1) * L]
                hh1 = evac("v", bk, bk.t[:, 0:128], xh,
                           waits=([xn_guard] if (xn_guard and f == 0) else []))
                lastev = rd_bank(pb, "v", bk, lambda e, xl=xl, bk=bk, xh=xh: e.tensor_sub(
                    xl, bk.t[:, 0:128], xh), waits=[hh1])
            prev[("tr", cp)] = htr
        return lastev

    def phase_A(xin_row0):
        prev = {}
        lastev = None
        for c in range(CPT):
            lastev = phase_A_chunk(xin_row0, c, prev)
        return lastev

    def proj_chunk(c, warm, snap):
        groups = [0, 1] if warm else [0, 1, 2, 3]
        sw = [h for h in (snap.get("t"), snap.get("v")) if h]
        evh = {}
        for p in range(8):
            bk = banks[p % 2]
            movh = sb["xnfmh"][:, p * T_TILE + c * L: p * T_TILE + (c + 1) * L]
            movl = sb["xnfml"][:, p * T_TILE + c * L: p * T_TILE + (c + 1) * L]
            for gi, g in enumerate(groups):
                mm3(bk, bk.t[:, g * 128:(g + 1) * 128],
                    sb["wblkh"][:, p * 512 + g * 128: p * 512 + (g + 1) * 128],
                    sb["wblkl"][:, p * 512 + g * 128: p * 512 + (g + 1) * 128],
                    movh, movl, start=(gi == 0), stop=(gi == len(groups) - 1),
                    waits=sw if gi == 0 else ())
            for g in groups:
                fn = AF.Silu if g < 3 else AF.Identity
                evh[(p, g)] = evac("s", bk, bk.t[:, g * 128:(g + 1) * 128],
                                   TRv[:, B_PROJ + p * 512 + g * 128: B_PROJ + p * 512 + (g + 1) * 128],
                                   func=fn, bias=sb["bxbcg"][:, p * 4 + g: p * 4 + g + 1],
                                   waits=sw)
        bkA = banks[2]
        for p in range(8):
            movh = sb["xnfmh"][:, p * T_TILE + c * L: p * T_TILE + (c + 1) * L]
            movl = sb["xnfml"][:, p * T_TILE + c * L: p * T_TILE + (c + 1) * L]
            for pi, (st_, mv_) in enumerate([
                    (sb["astath"][:, p * 16:(p + 1) * 16], movh),
                    (sb["astath"][:, p * 16:(p + 1) * 16], movl),
                    (sb["astatl"][:, p * 16:(p + 1) * 16], movh)]):
                mm(bkA, bkA.t[0:16, 0:128], st_, mv_,
                   start=(p == 0 and pi == 0), stop=(p == 7 and pi == 2))
        ha = evac("s", bkA, bkA.t[0:16, 0:128], sb["asb"][:, :],
                  func=AF.Identity, bias=sb["bacol"][:, 0:1], waits=sw)
        return evh, ha

    def decay_chunk(ha, warm, snap):
        ts_ = [snap["t"]] if snap.get("t") else []
        ss_ = [snap["s"]] if snap.get("s") else []
        vs_ = [snap["v"]] if snap.get("v") else []
        h1 = pb.emit("s", lambda e: e.activation(out=sb["sp16"][:, :], in_=sb["asb"][:, :], func=AF.Abs),
                     waits=[ha] + vs_)
        h2 = pb.emit("s", lambda e: e.activation(out=sb["sp16"][:, :], in_=sb["sp16"][:, :], func=AF.Exp, scale=-1.0),
                     waits=[h1])
        h3 = pb.emit("s", lambda e: e.activation(out=sb["sp16"][:, :], in_=sb["sp16"][:, :], func=AF.Ln, bias=1.0),
                     waits=[h2])
        h4 = pb.emit("v", lambda e: e.tensor_single_scalar(sb["crow"][:, :], sb["asb"][:, :], 0.0, ALU.max),
                     waits=[ha] + ts_)
        h5 = pb.emit("v", lambda e: e.tensor_add(sb["sp16"][:, :], sb["sp16"][:, :], sb["crow"][:, :]),
                     waits=[h3, h4])
        h6 = pb.emit("v", lambda e: e.tensor_tensor_scan(sb["crow"][:, :], sb["ones16"][:, :], sb["sp16"][:, :],
                                                         0.0, ALU.mult, ALU.add), waits=[h5])
        h7 = pb.emit("v", lambda e: e.tensor_copy(sb["clast"][:, :], sb["crow"][:, L - 1:L]), waits=[h6] + ss_)
        h8 = pb.emit("v", lambda e: e.tensor_scalar(sb["frow"][:, :], sb["crow"][:, :], sb["clast"][:, 0:1], None,
                                                    ALU.subtract), waits=[h7] + ts_)
        h9 = pb.emit("s", lambda e: e.activation(out=sb["frow"][:, :], in_=sb["frow"][:, :], func=AF.Exp),
                     waits=[h8])
        hD = pb.emit("s", lambda e: e.activation(out=sb["dcol"][:, :], in_=sb["clast"][:, :], func=AF.Exp, scale=-1.0),
                     waits=[h7])
        hDf = pb.emit("v", lambda e: e.tensor_copy(sb["dfree"][:, :], sb["dcol"][:, 0:1].broadcast_to([16, 128])),
                      waits=[hD] + ts_)
        hE = None
        if not warm:
            hE = pb.emit("s", lambda e: e.activation(out=sb["erow"][:, :], in_=sb["crow"][:, :], func=AF.Exp,
                                                     scale=-1.0), waits=[h6] + ts_)
        bk = banks[2]
        tr(bk, bk.t[0:128, 128:144], sb["frow"][:, :], iden[0:16, 0:16], waits=[h9])
        hfT = evac("v", bk, bk.t[0:128, 128:144], sb["fT"][:, :])
        tr(bk, bk.t[0:128, 144:160], sb["dfree"][:, :], iden[0:16, 0:16], waits=[hDf])
        hdT = evac("v", bk, bk.t[0:128, 144:160], sb["dT"][:, :])
        hcT = heT = None
        if not warm:
            tr(bk, bk.t[0:128, 160:176], sb["crow"][:, :], iden[0:16, 0:16], waits=[h6])
            cw = [h for h in (rt_state["gp"][H - 1], rt_state["sexp"]) if h]
            hcT = evac("v", bk, bk.t[0:128, 160:176], sb["cT"][:, :], waits=cw)
            tr(bk, bk.t[0:128, 176:192], sb["erow"][:, :], iden[0:16, 0:16], waits=[hE])
            heT = evac("v", bk, bk.t[0:128, 176:192], sb["eT"][:, :])
        return hfT, hdT, hcT, heT

    def xb_transpose(c, evh, warm, snap):
        ts_ = [snap["t"]] if snap.get("t") else []
        hx = {}
        for p in range(8):
            bk = banks[3]
            tr(bk, bk.t[:, 0:128], TRv[:, B_PROJ + p * 512: B_PROJ + p * 512 + 128],
               iden[:, :], waits=[evh[(p, 0)]])
            h1 = evac("v", bk, bk.t[:, 0:128], TRv[:, B_XT + p * 256: B_XT + p * 256 + 128],
                      waits=(ts_ if p == 0 else []))
            tr(bk, bk.t[:, 128:256], TRv[:, B_PROJ + p * 512 + 128: B_PROJ + p * 512 + 256],
               iden[:, :], waits=[evh[(p, 1)]])
            h2 = evac("v", bk, bk.t[:, 128:256], TRv[:, B_XT + p * 256 + 128: B_XT + p * 256 + 256])
            hx[p] = (h1, h2)
        return hx

    state_dep = [None]
    enc_state = {"split": None, "spk": {}}
    router_state = {}
    rt_state = {"gp": [None] * H, "tr": [None] * H, "sexp": None}

    def scan_chunk(c, evh, hx, hfT, hdT, hcT, heT, warm, snap):
        prev_yi_mm = snap.get("t") and ("t", snap["t"][1]) or None
        prev_hu_mm = prev_yi_mm
        hlast_y = None
        first_y = True
        prev_hmt = snap.get("v")
        if not warm:
            for h in range(H):
                wg = [hcT] + ([rt_state["tr"][h]] if rt_state["tr"][h] else [])
                rt_state["gp"][h] = pb.emit("g", lambda e, h=h: e.tensor_scalar(
                    sb["stm"][:, h * 128:(h + 1) * 128], sb["tripost"][:, :],
                    sb["cT"][:, h:h + 1], None, ALU.add), waits=wg)
        for h in range(H):
            p, half = h // 2, h % 2
            hb = half * 64
            Xtm = TRv[:, B_XT + p * 256: B_XT + p * 256 + 128]
            Btm = TRv[:, B_XT + p * 256 + 128: B_XT + p * 256 + 256]
            hst = sb["htst"][hb:hb + 64, p * 64:(p + 1) * 64]
            bkY = banks[5]
            if not warm:
                bkS = banks[3 + (h % 2)]
                Bfm = TRv[:, B_PROJ + p * 512 + 128 + c * 0: B_PROJ + p * 512 + 256]
                Cfm = TRv[:, B_PROJ + p * 512 + 256: B_PROJ + p * 512 + 384]
                mm(bkS, bkS.t[:, 0:128], Bfm[hb:hb + 64, :], Cfm[hb:hb + 64, :],
                   start=True, stop=True, waits=[evh[(p, 1)], evh[(p, 2)]])
                rt_state["tr"][h] = tr(bkS, bkS.t[:, 128:256],
                                       sb["stm"][:, h * 128:(h + 1) * 128], iden[:, :],
                                       waits=[rt_state["gp"][h]])
                hexp = evac("s", bkS, bkS.t[:, 128:256], TRv[:, B_GB:B_GB + 128],
                            func=AF.Exp, bias=sb["cT"][:, h:h + 1], scale=-1.0,
                            waits=[hcT] + ([prev_hmt] if prev_hmt else []))
                rt_state["sexp"] = hexp
                hmt = rd_bank(pb, "v", bkS, lambda e, bkS=bkS: e.tensor_mul(
                    TRv[:, B_MT:B_MT + 128], TRv[:, B_GB:B_GB + 128], bkS.t[:, 0:128]),
                    waits=[hexp] + ([prev_yi_mm] if prev_yi_mm else []))
                prev_hmt = hmt
                prev_yi_mm = mm(bkY, bkY.t[:, 0:64], TRv[:, B_MT:B_MT + 128], Xtm[:, hb:hb + 64],
                                start=True, stop=True, waits=[hmt, hx[p][0]])
                sw = [evh[(p, 2)]]
                if state_dep[0] is not None:
                    sw.append(state_dep[0])
                mm(bkY, bkY.t[:, 64:128], Cfm[hb:hb + 64, :], hst, start=True, stop=True,
                   waits=sw)
            hbw = pb.emit("v", lambda e, h=h, hb=hb, Btm=Btm: e.tensor_scalar(
                TRv[:, B_BW + hb:B_BW + hb + 64], Btm[:, hb:hb + 64], sb["fT"][:, h:h + 1],
                None, ALU.mult),
                waits=[hfT, hx[p][1]] + ([prev_hu_mm] if prev_hu_mm else []))
            prev_hu_mm = mm(bkY, bkY.t[:, 128:192] if not warm else bkY.t[:, 0:64],
                            TRv[:, B_BW:B_BW + 128], Xtm[:, hb:hb + 64],
                            start=True, stop=True, waits=[hbw, hx[p][0]])
            upd_ap = (bkY.t[:, 128:192] if not warm else bkY.t[:, 0:64])
            if not warm:
                hy1 = rd_bank(pb, "v", bkY, lambda e, bkY=bkY, h=h: e.tensor_scalar(
                    TRv[:, B_YT2:B_YT2 + 64], bkY.t[:, 64:128], sb["eT"][:, h:h + 1],
                    None, ALU.mult), waits=[heT])
                hlast_y = rd_bank(pb, "v", bkY, lambda e, bkY=bkY, h=h: e.tensor_add(
                    TRv[:, B_YTM + h * 64:B_YTM + (h + 1) * 64],
                    TRv[:, B_YT2:B_YT2 + 64], bkY.t[:, 0:64]),
                    waits=[hy1] + ([snap["t"]] if (first_y and snap.get("t")) else []))
                first_y = False
            hup = rd_bank(pb, "v", bkY, lambda e, hst=hst, h=h, hb=hb, upd_ap=upd_ap: e.scalar_tensor_tensor(
                hst, hst, sb["dT"][hb:hb + 64, h:h + 1], upd_ap[hb:hb + 64, :],
                ALU.mult, ALU.add), waits=[hdT])
            state_dep[0] = hup
        return hlast_y, None

    def yg_chunk(c, evh, hy, hrt):
        trb = TRv[:, :].bitcast(mybir.dt.bfloat16)
        hlast = None
        for p in range(8):
            bk = banks[0 + (p % 2)]
            tr(bk, bk.t[:, 0:128], TRv[:, B_YTM + p * 128:B_YTM + (p + 1) * 128],
               iden[:, :], waits=[hy])
            hm = rd_bank(pb, "v", bk, lambda e, bk=bk, p=p: e.tensor_mul(
                TRv[:, B_RT:B_RT + 128],
                TRv[:, B_PROJ + p * 512 + 384: B_PROJ + p * 512 + 512],
                bk.t[:, 0:128]), waits=[evh[(p, 3)]] + ([hrt] if (p == 0 and hrt) else []))
            ygh = trb[:, 2 * B_YG + p * 512 + c * 128: 2 * B_YG + p * 512 + c * 128 + 128]
            ygl = trb[:, 2 * B_YG + 4096 + p * 512 + c * 128: 2 * B_YG + 4096 + p * 512 + c * 128 + 128]
            hh = pb.emit("v", lambda e, ygh=ygh: e.tensor_copy(ygh, TRv[:, B_RT:B_RT + 128]),
                         waits=[hm])
            hlast = pb.emit("v", lambda e, ygl=ygl, ygh=ygh: e.tensor_sub(
                ygl, TRv[:, B_RT:B_RT + 128], ygh), waits=[hh])
        return hlast

    def wout_group(hwout, hyg):
        trb = TRv[:, :].bitcast(mybir.dt.bfloat16)
        for o in range(8):
            bk = banks[2 + (o % 2)]
            for k in range(8):
                sth = wsb[:, k * 1024 + o * 128: k * 1024 + (o + 1) * 128]
                stl = wsb[:, 8192 + k * 1024 + o * 128: 8192 + k * 1024 + (o + 1) * 128]
                mvh = trb[:, 2 * B_YG + k * 512: 2 * B_YG + (k + 1) * 512]
                mvl = trb[:, 2 * B_YG + 4096 + k * 512: 2 * B_YG + 4096 + (k + 1) * 512]
                mm3(bk, bk.t[:, 0:512], sth, stl, mvh, mvl,
                    start=(k == 0), stop=(k == 7),
                    waits=([hwout, hyg] if k == 0 else []))
            hsf = evac("s", bk, bk.t[:, 0:512], TRv[:, B_WSC:B_WSC + 512],
                       func=AF.Identity, bias=sb["boutc"][:, o:o + 1])
            for cc in range(CPT):
                bkt = banks[6 + (cc % 2)]
                tr(bkt, bkt.t[:, 0:128],
                   TRv[:, B_WSC + cc * 128: B_WSC + (cc + 1) * 128],
                   iden[:, :], waits=[hsf])
                hsf2 = evac("v", bkt, bkt.t[:, 0:128],
                            sb["stm"][:, cc * D + o * 128: cc * D + (o + 1) * 128])
            pb.wait_only("s", [pb.cnt and ("t", pb.cnt["t"])])

    def enc_j(g, j, henc):
        trb = TRv[:, :].bitcast(mybir.dt.bfloat16)
        bk = banks[7]
        bks = banks[6]
        for k in range(8):
            sth = wsb[:, k * 2048 + j * 128: k * 2048 + (j + 1) * 128]
            stl = wsb[:, 16384 + k * 2048 + j * 128: 16384 + k * 2048 + (j + 1) * 128]
            mvh = sb["xnfmh"][:, k * T_TILE + g * GRP: k * T_TILE + (g + 1) * GRP]
            mvl = sb["xnfml"][:, k * T_TILE + g * GRP: k * T_TILE + (g + 1) * GRP]
            mm3(bk, bk.t[:, 0:GRP], sth, stl, mvh, mvl,
                start=(k == 0), stop=(k == 7), waits=([henc] if k == 0 else []))
        hv0 = evac("s", bk, bk.t[:, 0:GRP], TRv[:, C_RS2:C_RS2 + GRP],
                   func=AF.Sigmoid, bias=sb["encbc"][:, j:j + 1],
                   waits=([enc_state["s2c"]] if enc_state["s2c"] else []))
        sh_ap = trb[:, 2 * C_SPB: 2 * C_SPB + GRP]
        sl_ap = trb[:, 2 * C_SPB + GRP: 2 * C_SPB + 2 * GRP]
        hv1 = pb.emit("v", lambda e, sh_ap=sh_ap: e.tensor_copy(sh_ap, TRv[:, C_RS2:C_RS2 + GRP]),
                      waits=[hv0])
        hv2 = pb.emit("v", lambda e, sl_ap=sl_ap, sh_ap=sh_ap: e.tensor_sub(
            sl_ap, TRv[:, C_RS2:C_RS2 + GRP], sh_ap), waits=[hv1])
        h = mm3(bks, bks.t[:, 0:GRP], sb["s2ch"][:, j * 128:(j + 1) * 128],
                sb["s2cl"][:, j * 128:(j + 1) * 128], sh_ap, sl_ap,
                start=(j == 0), stop=(j == 15), waits=[hv2])
        enc_state["s2c"] = h if h else enc_state["s2c"]
        if j == 15:
            hc = evac("v", bks, bks.t[:, 0:GRP], TRv[:, C_CONT:C_CONT + GRP])
            enc_state["cont"] = hc
    def router_group(g, hcont):
        contg = TRv[:, C_CONT + g * GRP: C_CONT + (g + 1) * GRP]
        bk = banks[5]
        mm(bk, bk.t[0:64, 0:GRP], sb["rw1"][:, :], contg[0:64, :], start=True,
           stop=True, waits=[hcont])
        hrh = evac("s", bk, bk.t[0:64, 0:GRP], sb["rhsb"][:, :], func=AF.Relu,
                   bias=sb["rb1c"][:, 0:1])
        for ci in range(CPT):
            cc = CPT * g + ci
            bkl = banks[6]
            mm(bkl, bkl.t[:, 0:8], sb["rhsb"][:, ci * 128:(ci + 1) * 128],
               sb["rw2"][:, :], start=True, stop=True, waits=[hrh])
            hl = rd_bank(pb, "v", bkl, lambda e, bkl=bkl: e.tensor_add(
                sb["logit"][:, :], bkl.t[:, 0:8], sb["rb2rep"][:, :]))
            hm = pb.emit("v", lambda e: e.reduce_max(sb["m1c"][:, :], sb["logit"][:, :], AX.X, negate=True),
                         waits=[hl])
            hm2_ = pb.emit("v", lambda e: e.tensor_scalar(sb["eu"][:, :], sb["logit"][:, :],
                                                          sb["m1c"][:, 0:1], None, ALU.add), waits=[hm])
            he = pb.emit("s", lambda e: e.activation(out=sb["eu"][:, :], in_=sb["eu"][:, :],
                                                     func=AF.Exp), waits=[hm2_])
            h1 = pb.emit("v", lambda e: e.reduce_max(sb["m1c"][:, :], sb["eu"][:, :], AX.X), waits=[he])
            h2 = pb.emit("v", lambda e: e.tensor_scalar(sb["combm"][:, 0:8], sb["eu"][:, :], sb["m1c"][:, 0:1],
                                                        None, ALU.is_ge), waits=[h1])
            h3 = pb.emit("v", lambda e: e.select(sb["selb"][:, :], sb["combm"][:, 0:8], sb["iota8"][:, :],
                                                 sb["big8"][:, :]), waits=[h2])
            h4 = pb.emit("v", lambda e: e.tensor_reduce(sb["i1c"][:, :], sb["selb"][:, :], AX.X, ALU.min),
                         waits=[h3])
            h5 = pb.emit("v", lambda e: e.tensor_scalar(sb["combm"][:, 0:8], sb["iota8"][:, :], sb["i1c"][:, 0:1],
                                                        None, ALU.is_equal), waits=[h4])
            h6 = pb.emit("v", lambda e: e.select(sb["eu2"][:, :], sb["combm"][:, 0:8], sb["neg1t"][:, :],
                                                 sb["eu"][:, :]), waits=[h5])
            h7 = pb.emit("v", lambda e: e.reduce_max(sb["m2c"][:, :], sb["eu2"][:, :], AX.X), waits=[h6])
            h8 = pb.emit("v", lambda e: e.tensor_scalar(sb["combm"][:, 8:16], sb["eu2"][:, :], sb["m2c"][:, 0:1],
                                                        None, ALU.is_ge), waits=[h7])
            h9 = pb.emit("v", lambda e: e.select(sb["selb"][:, :], sb["combm"][:, 8:16], sb["iota8"][:, :],
                                                 sb["big8"][:, :]), waits=[h8])
            h10 = pb.emit("v", lambda e: e.tensor_reduce(sb["i2c"][:, :], sb["selb"][:, :], AX.X, ALU.min),
                          waits=[h9])
            h11 = pb.emit("v", lambda e: e.tensor_scalar(sb["combm"][:, 8:16], sb["iota8"][:, :], sb["i2c"][:, 0:1],
                                                         None, ALU.is_equal), waits=[h10])
            hd = pb.emit("v", lambda e: e.tensor_add(sb["dnc"][:, :], sb["m1c"][:, :], sb["m2c"][:, :]),
                         waits=[h7])
            hd2 = pb.emit("v", lambda e: e.reciprocal(out=sb["dnc"][:, :], in_=sb["dnc"][:, :]), waits=[hd])
            hw1 = pb.emit("v", lambda e: e.tensor_mul(sb["w1c"][:, :], sb["m1c"][:, :], sb["dnc"][:, :]),
                          waits=[hd2])
            hw2 = pb.emit("v", lambda e: e.tensor_mul(sb["w2c"][:, :], sb["m2c"][:, :], sb["dnc"][:, :]),
                          waits=[hd2])
            hc1 = pb.emit("v", lambda e: e.tensor_scalar(sb["selb"][:, :],
                                                         sb["combm"][:, 0:8], sb["w1c"][:, 0:1], None,
                                                         ALU.mult), waits=[h5, hw1, h10])
            hc2 = pb.emit("v", lambda e, cc=cc: e.scalar_tensor_tensor(
                sb["comb"][:, cc * 8:(cc + 1) * 8], sb["combm"][:, 8:16], sb["w2c"][:, 0:1],
                sb["selb"][:, :], ALU.mult, ALU.add), waits=[h11, hw2, hc1])
            if DEBUG and cc == 1 and router_state.get("tile") == N_WARM_TILES:
                pb.barrier()
                pb.emit("g", lambda e: e.dma_start(out=dbg["eu"][:, :], in_=sb["eu"][:, :]), dma="B")
                pb.emit("g", lambda e: e.dma_start(out=dbg["logit"][:, :], in_=sb["logit"][:, :]), dma="B")
                pb.emit("g", lambda e: e.dma_start(out=dbg["msk"][:, :], in_=sb["combm"][:, :].bitcast(F32)[:, 0:16]), dma="B")
                pb.barrier()
            bkt = banks[7]
            tr(bkt, bkt.t[0:8, 0:128], sb["comb"][:, cc * 8:(cc + 1) * 8], iden[:, :],
               waits=[hc2])
            evac("v", bkt, bkt.t[0:8, 0:128], sb["combT"][:, cc * 128:(cc + 1) * 128])

    def experts_group(g, hexp):
        contg = TRv[:, C_CONT + g * GRP: C_CONT + (g + 1) * GRP]
        cbh = sb["contb"][:, g * GRP:(g + 1) * GRP]
        cbl = sb["contb"][:, T_TILE + g * GRP: T_TILE + (g + 1) * GRP]
        prev_eo = None
        for q in range(4):
            bkA, bkB = banks[q % 2], banks[2 + q % 2]
            ehh = {}
            for j in range(8):
                for half, bk in ((0, bkA), (1, bkB)):
                    sth = wsb[half * 64:half * 64 + 64,
                              q * 1024 + j * 128: q * 1024 + (j + 1) * 128]
                    stl = wsb[half * 64:half * 64 + 64,
                              4096 + q * 1024 + j * 128: 4096 + q * 1024 + (j + 1) * 128]
                    mh = cbh[half * 64:half * 64 + 64, :]
                    ml = cbl[half * 64:half * 64 + 64, :]
                    mm3(bk, bk.t[:, 0:GRP], sth, stl, mh, ml,
                        start=True, stop=True,
                        waits=([hexp] if j == 0 else []))
                    ehh[(j, half)] = evac(
                        "s", bk, bk.t[:, 0:GRP],
                        TRv[:, C_EH + half * 4096 + j * GRP: C_EH + half * 4096 + (j + 1) * GRP],
                        func=AF.Relu,
                        bias=sb["eb1c"][:, (2 * q + half) * 8 + j:(2 * q + half) * 8 + j + 1],
                        waits=([prev_eo] if prev_eo else []))
            for half in (0, 1):
                ee = 2 * q + half
                for ci in range(CPT):
                    cc = CPT * g + ci
                    bko = banks[4 + (ci % 2)]
                    for j in range(8):
                        prev_eo = mm(bko, bko.t[:, 0:64],
                                     TRv[:, C_EH + half * 4096 + j * GRP + ci * 128:
                                         C_EH + half * 4096 + j * GRP + ci * 128 + 128],
                                     wsb_f32[:, 4096 + ee * 512 + j * 64: 4096 + ee * 512 + (j + 1) * 64],
                                     start=(j == 0), stop=(j == 7),
                                     waits=[ehh[(j, half)]])
                    moeap = TRv[:, C_MOE + cc * 64: C_MOE + (cc + 1) * 64]
                    tmp64 = TRv[:, C_EO:C_EO + 64]
                    if ee == 0:
                        rd_bank(pb, "v", bko, lambda e, bko=bko, moeap=moeap, cc=cc: e.tensor_scalar(
                            moeap, bko.t[:, 0:64], sb["comb"][:, cc * 8:cc * 8 + 1], None, ALU.mult))
                    else:
                        hta = rd_bank(pb, "v", bko, lambda e, bko=bko, tmp64=tmp64, cc=cc, ee=ee: e.tensor_scalar(
                            tmp64, bko.t[:, 0:64], sb["comb"][:, cc * 8 + ee:cc * 8 + ee + 1], None, ALU.mult))
                        pb.emit("v", lambda e, moeap=moeap, tmp64=tmp64: e.tensor_add(
                            moeap, moeap, tmp64), waits=[hta])
        for ci in range(CPT):
            cc = CPT * g + ci
            bkb = banks[6]
            mm(bkb, bkb.t[:, 0:64], sb["combT"][:, cc * 128:(cc + 1) * 128],
               sb["eb2m"][:, :], start=True, stop=True)
            hmb = rd_bank(pb, "v", bkb, lambda e, bkb=bkb, cc=cc: e.tensor_add(
                TRv[:, C_MOE + cc * 64: C_MOE + (cc + 1) * 64],
                TRv[:, C_MOE + cc * 64: C_MOE + (cc + 1) * 64], bkb.t[:, 0:64]))
            h1 = pb.emit("v", lambda e, cc=cc: e.tensor_copy(
                TRv[:, C_MTMP:C_MTMP + 64], TRv[:, C_MOE + cc * 64: C_MOE + (cc + 1) * 64]),
                waits=[hmb])
            h2 = pb.emit("v", lambda e: e.tensor_copy(
                TRv[:, C_MTMP + 64:C_MTMP + 128], TRv[:, C_MTMP:C_MTMP + 64]), waits=[h1])
            bkt = banks[7]
            trb2 = TRv[:, :].bitcast(mybir.dt.bfloat16)
            tr(bkt, bkt.t[:, 0:128], TRv[:, C_MTMP:C_MTMP + 128], iden[:, :], waits=[h2])
            mfh = trb2[:, 2 * C_MFD + cc * 128: 2 * C_MFD + (cc + 1) * 128]
            mfl = trb2[:, 2 * C_MFD + 512 + cc * 128: 2 * C_MFD + 512 + (cc + 1) * 128]
            hq1 = evac("v", bkt, bkt.t[:, 0:128], mfh)
            rd_bank(pb, "v", bkt, lambda e, mfl=mfl, bkt=bkt, mfh=mfh: e.tensor_sub(
                mfl, bkt.t[:, 0:128], mfh), waits=[hq1])

    def ffn_dec_group(g, tile_idx, hdec):
        trb = TRv[:, :].bitcast(mybir.dt.bfloat16)
        mfh_a = trb[:, 2 * C_MFD: 2 * C_MFD + GRP]
        mfl_a = trb[:, 2 * C_MFD + 512: 2 * C_MFD + 512 + GRP]
        rat_h = {}
        prev_split = {C_RS: None}
        for jp in range(8):
            j0, j1 = 2 * jp, 2 * jp + 1
            bkA, bkB = banks[jp % 2], banks[2 + jp % 2]
            for (jj, bk_, rlo) in ((j0, bkA, 0), (j1, bkB, 64)):
                mm3(bk_, bk_.t[:, 0:GRP], sb["c2srh"][rlo:rlo + 64, jj * 128:(jj + 1) * 128],
                    sb["c2srl"][rlo:rlo + 64, jj * 128:(jj + 1) * 128],
                    mfh_a[rlo:rlo + 64, :], mfl_a[rlo:rlo + 64, :],
                    start=True, stop=True)
            for (jj, bk_) in ((j0, bkA), (j1, bkB)):
                hv0 = evac("s", bk_, bk_.t[:, 0:GRP],
                           TRv[:, C_RS:C_RS + GRP],
                           func=AF.Sigmoid, bias=sb["c2sbc"][:, jj:jj + 1],
                           waits=([prev_split[C_RS]] if prev_split[C_RS] else []))
                rh_ap = trb[:, 2 * C_RAT + jj * GRP: 2 * C_RAT + (jj + 1) * GRP]
                rl_ap = trb[:, 2 * C_RAT + 8192 + jj * GRP: 2 * C_RAT + 8192 + (jj + 1) * GRP]
                hv1 = pb.emit("v", lambda e, rh_ap=rh_ap: e.tensor_copy(
                    rh_ap, TRv[:, C_RS:C_RS + GRP]), waits=[hv0])
                rat_h[jj] = pb.emit("v", lambda e, rl_ap=rl_ap, rh_ap=rh_ap: e.tensor_sub(
                    rl_ap, TRv[:, C_RS:C_RS + GRP], rh_ap), waits=[hv1])
                prev_split[C_RS] = rat_h[jj]
        hdtm = {}
        prev_tr = None
        for o in range(8):
            bk = banks[o % 4]
            for k in range(16):
                sth = wsb[:, k * 1024 + o * 128: k * 1024 + (o + 1) * 128]
                stl = wsb[:, 16384 + k * 1024 + o * 128: 16384 + k * 1024 + (o + 1) * 128]
                rh_ap = trb[:, 2 * C_RAT + k * GRP: 2 * C_RAT + (k + 1) * GRP]
                rl_ap = trb[:, 2 * C_RAT + 8192 + k * GRP: 2 * C_RAT + 8192 + (k + 1) * GRP]
                mm3(bk, bk.t[:, 0:GRP], sth, stl, rh_ap, rl_ap,
                    start=(k == 0), stop=(k == 15),
                    waits=[rat_h[k]] + ([hdec] if k == 0 else []))
            hdv = evac("s", bk, bk.t[:, 0:GRP], TRv[:, C_RS:C_RS + GRP],
                       func=AF.Sigmoid, bias=sb["decbc"][:, o:o + 1],
                       waits=([prev_tr] if prev_tr else []))
            for ci in range(CPT):
                bkt = banks[4 + (ci % 2)]
                prev_tr = tr(bkt, bkt.t[:, 0:128],
                             TRv[:, C_RS + ci * 128: C_RS + (ci + 1) * 128],
                             iden[:, :], waits=[hdv])
                hdtm[(o, ci)] = evac("v", bkt, bkt.t[:, 0:128],
                                     TRv[:, C_DTM + ci * 1024 + o * 128: C_DTM + ci * 1024 + (o + 1) * 128])
        for ci in range(CPT):
            cc = ci
            dtm = TRv[:, C_DTM + ci * 1024:C_DTM + (ci + 1) * 1024]
            hlast = hdtm[(7, ci)]
            hx = pb.emit("y", lambda e, cc=cc, tile_idx=tile_idx: e.dma_start(
                out=TRv[:, C_XR:C_XR + 1024],
                in_=din["xin"][WARM + tile_idx * T_TILE + cc * L: WARM + tile_idx * T_TILE + (cc + 1) * L, :]),
                waits=[hlast], dma="A")
            pb.emit("v", lambda e, dtm=dtm: e.bn_stats(sb["bnst"][:, 0:6], dtm[:, 0:512]),
                    waits=[hlast])
            pb.emit("v", lambda e, dtm=dtm: e.bn_stats(sb["bnst"][:, 6:12], dtm[:, 512:1024]))
            pb.emit("v", lambda e: e.bn_aggr(sb["bnag"][:, 0:2], sb["bnst"][:, 0:6]))
            ha2 = pb.emit("v", lambda e: e.bn_aggr(sb["bnag"][:, 2:4], sb["bnst"][:, 6:12]))
            hsm = pb.emit("v", lambda e: e.tensor_add(sb["bnag"][:, 4:6], sb["bnag"][:, 0:2],
                                                      sb["bnag"][:, 2:4]), waits=[ha2])
            hdf = pb.emit("v", lambda e: e.tensor_sub(sb["bnag"][:, 6:7], sb["bnag"][:, 0:1],
                                                      sb["bnag"][:, 2:3]), waits=[ha2])
            hmu = pb.emit("v", lambda e: e.tensor_single_scalar(sb["mucol"][:, 0:1], sb["bnag"][:, 4:5],
                                                                0.5, ALU.mult), waits=[hsm])
            hd2 = pb.emit("v", lambda e: e.tensor_mul(sb["bnag"][:, 7:8], sb["bnag"][:, 6:7],
                                                      sb["bnag"][:, 6:7]), waits=[hdf])
            hv2 = pb.emit("v", lambda e: e.scalar_tensor_tensor(sb["varcol"][:, 0:1], sb["bnag"][:, 7:8],
                                                                0.5, sb["bnag"][:, 5:6],
                                                                ALU.mult, ALU.add), waits=[hd2, hsm])
            hv = pb.emit("v", lambda e: e.tensor_single_scalar(sb["varcol"][:, 0:1], sb["varcol"][:, 0:1],
                                                               2e-5, ALU.add), waits=[hv2])
            hsd = pb.emit("s", lambda e: e.activation(out=sb["rstdcol"][:, 0:1], in_=sb["varcol"][:, 0:1],
                                                      func=AF.Sqrt, scale=0.5), waits=[hv])
            hrs = pb.emit("v", lambda e: e.reciprocal(out=sb["rstdcol"][:, 0:1], in_=sb["rstdcol"][:, 0:1]),
                          waits=[hsd])
            hn = pb.emit("v", lambda e, dtm=dtm: e.tensor_scalar(TRv[:, C_G:C_G + 1024], dtm,
                                                                 sb["mucol"][:, 0:1], sb["rstdcol"][:, 0:1],
                                                                 ALU.subtract, ALU.mult), waits=[hrs, hmu])
            hg1 = pb.emit("v", lambda e: e.tensor_mul(TRv[:, C_G:C_G + 1024], TRv[:, C_G:C_G + 1024],
                                                      sb["lngrep"][:, :]), waits=[hn])
            hg2 = pb.emit("v", lambda e: e.tensor_add(TRv[:, C_G:C_G + 1024], TRv[:, C_G:C_G + 1024],
                                                      sb["lnbrep"][:, :]), waits=[hg1])
            hsg = pb.emit("s", lambda e: e.activation(out=TRv[:, C_SG:C_SG + 1024], in_=TRv[:, C_G:C_G + 1024],
                                                      func=AF.Silu), waits=[hg2])
            hcmb = pb.emit("v", lambda e, cc=cc: e.tensor_mul(
                TRv[:, C_SG:C_SG + 1024], TRv[:, C_SG:C_SG + 1024],
                sb["stm"][:, cc * D:(cc + 1) * D]), waits=[hsg])
            hout = pb.emit("v", lambda e: e.tensor_add(TRv[:, C_OB:C_OB + 1024], TRv[:, C_XR:C_XR + 1024],
                                                       TRv[:, C_SG:C_SG + 1024]), waits=[hcmb, hx])
            pb.emit("g", lambda e, cc=cc, tile_idx=tile_idx: e.dma_start(
                out=dout[tile_idx * T_TILE + cc * L: tile_idx * T_TILE + (cc + 1) * L, :],
                in_=TRv[:, C_OB:C_OB + 1024]), waits=[hout], dma="B")
            pb.barrier()

    def tap(nm, src_ap):
        if DEBUG:
            pb.emit("g", lambda e, nm=nm, src_ap=src_ap: e.dma_start(
                out=dbg[nm][:, :], in_=src_ap), dma="B")
            pb.barrier()

    prevA1 = {}
    ha_map = {}
    for tile in range(N_WARM_TILES + N_MAIN_TILES):
        warm = tile < N_WARM_TILES
        tile_idx = tile - N_WARM_TILES
        row0 = tile * T_TILE
        if tile != N_WARM_TILES:
            phase_A(row0)
        pb.barrier()
        henc = None if warm else load_enc()
        if tile == N_WARM_TILES:
            tap("xn", sb["xnfmh"][:, :4096])
        enc_state["s2c"] = None
        enc_state["cont"] = None
        snap = {}
        for c in range(CPT):
            evh, ha = proj_chunk(c, warm, snap)
            if not warm:
                enc_j(0, 4 * c + 0, henc)
            hfT, hdT, hcT, heT = decay_chunk(ha, warm, snap)
            if not warm:
                enc_j(0, 4 * c + 1, henc)
            hx = xb_transpose(c, evh, warm, snap)
            if not warm:
                enc_j(0, 4 * c + 2, henc)
            hy, hrt = scan_chunk(c, evh, hx, hfT, hdT, hcT, heT, warm, snap)
            if warm:
                snap = {"t": ("t", pb.cnt["t"]), "v": ("v", pb.cnt["v"]), "s": ("s", pb.cnt["s"])}
                emit_split_jobs(9)
                ha_map[c] = ha
                if tile == N_WARM_TILES - 1 and c >= 1:
                    phase_A_chunk(N_WARM_TILES * T_TILE, c - 1, prevA1,
                                  base=6144, bksel=(6, 7), xn_guard=ha_map[c - 1])
            else:
                enc_j(0, 4 * c + 3, henc)
                hyg = yg_chunk(c, evh, hy, hrt)
                snap = {"t": ("t", pb.cnt["t"]), "v": ("v", pb.cnt["v"]), "s": ("s", pb.cnt["s"])}
            if tile == N_WARM_TILES and c == 0:
                tap("proj", TRv[:, B_PROJ:B_PROJ + 4096])
                tap("crow", sb["crow"][:, :])
                tap("asb", sb["asb"][:, :])
                tap("ytm", TRv[:, B_YTM:B_YTM + 1024])
                tap("h1", sb["htst"][:, :])
                tap("xtm", TRv[:, B_XT:B_XT + 2048])
                pb.emit("g", lambda e: e.dma_start(out=dbg["cols"][:, 0:16], in_=sb["cT"][:, :]), dma="B") if DEBUG else None
                pb.emit("g", lambda e: e.dma_start(out=dbg["cols"][:, 16:32], in_=sb["fT"][:, :]), dma="B") if DEBUG else None
                pb.emit("g", lambda e: e.dma_start(out=dbg["cols"][:, 32:48], in_=sb["eT"][:, :]), dma="B") if DEBUG else None
                pb.emit("g", lambda e: e.dma_start(out=dbg["cols"][:, 48:64], in_=sb["dT"][:, :]), dma="B") if DEBUG else None
                pb.barrier()
        if warm:
            if tile == N_WARM_TILES - 1:
                phase_A_chunk(N_WARM_TILES * T_TILE, CPT - 1, prevA1,
                              base=6144, bksel=(6, 7), xn_guard=ha_map[CPT - 1])
                h = pb.emit("v", lambda e: e.tensor_mul(sb["htst"][:, :], sb["htst"][:, :],
                                                        sb["wmask"][:, 0:1].broadcast_to([128, 512])))
                pb.barrier()
                tap("h0", sb["htst"][:, :])
            continue
        hc = enc_state["cont"]
        hcb = pb.emit("v", lambda e: e.tensor_scalar(
            TRv[:, C_CONT:C_CONT + T_TILE], TRv[:, C_CONT:C_CONT + T_TILE],
            sb["s2cbc"][:, 0:1], None, ALU.add), waits=[hc])
        hcs = pb.emit("v", lambda e: e.tensor_copy(sb["contb"][:, 0:T_TILE],
                                                   TRv[:, C_CONT:C_CONT + T_TILE]), waits=[hcb])
        pb.emit("v", lambda e: e.tensor_sub(sb["contb"][:, T_TILE:2 * T_TILE],
                                            TRv[:, C_CONT:C_CONT + T_TILE],
                                            sb["contb"][:, 0:T_TILE]), waits=[hcs])
        pb.barrier()
        hwout = load_wout()
        wout_group(hwout, hyg)
        pb.barrier()
        if tile == N_WARM_TILES:
            tap("cont", TRv[:, C_CONT:C_CONT + 512])
            tap("stm", sb["stm"][:, :4096])
        hexp = load_experts()
        router_state["tile"] = tile
        for g in range(GPT):
            router_group(g, hcb)
            experts_group(g, hexp)
        pb.barrier()
        if tile == N_WARM_TILES:
            tap("comb", sb["comb"][:, :])
            tap("moe", TRv[:, C_MOE:C_MOE + 256])
            tap("mfd", TRv[:, C_MFD:C_MFD + 512])
        hdec = load_dec()
        for g in range(GPT):
            ffn_dec_group(g, tile_idx, hdec)
        pb.barrier()
        if tile == N_WARM_TILES:
            tap("rat", TRv[:, C_RAT:C_RAT + 4096])
            tap("dec", TRv[:, C_DTM:C_DTM + 2048])

    pb.wait_only("g", [("B", pb.cnt["B"])])

    with nc.Block() as block:
        @block.tensor
        def _(t):
            for fn in pb.prog["t"]:
                fn(t)

        @block.vector
        def _(v):
            for fn in pb.prog["v"]:
                fn(v)

        @block.scalar
        def _(s):
            for fn in pb.prog["s"]:
                fn(s)

        @block.gpsimd
        def _(g):
            for fn in pb.prog["g"]:
                fn(g)

        @block.sync
        def _(y):
            for fn in pb.prog["y"]:
                fn(y)

    es.close()
    if pb.ldwf_ins:
        import dataclasses
        targets = {id(i) for i in pb.ldwf_ins}
        npatched = 0
        for bb in nc.m.functions[0].blocks:
            il = bb.instructions
            for idx, ins0 in enumerate(il):
                if id(ins0) in targets:
                    il[idx] = dataclasses.replace(ins0, ldweights=False)
                    npatched += 1
        assert npatched == len(pb.ldwf_ins), (npatched, len(pb.ldwf_ins))
    return nc


def _layout_inputs(inp):
    f32 = np.float32
    x = np.asarray(inp["L0_attn_inp"], f32)
    wA = np.asarray(inp["L0_attn_wA"], f32)
    bA = np.asarray(inp["L0_attn_bA"], f32)
    wXBC = np.asarray(inp["L0_attn_wXBC"], f32)
    bXBC = np.asarray(inp["L0_attn_bXBC"], f32)
    wG = np.asarray(inp["L0_attn_wG"], f32)
    bG = np.asarray(inp["L0_attn_bG"], f32)
    Wout = np.asarray(inp["L0_attn_Wout"], f32)
    bout = np.asarray(inp["L0_attn_bout"], f32)
    encw = np.asarray(inp["L0_ffn_enc_w"], f32)
    encb = np.asarray(inp["L0_ffn_enc_b"], f32)
    s2cw = np.asarray(inp["L0_ffn_s2c_w"], f32)
    s2cb = np.asarray(inp["L0_ffn_s2c_b"], f32)
    rw1 = np.asarray(inp["L0_ffn_r_w1"], f32)
    rb1 = np.asarray(inp["L0_ffn_r_b1"], f32)
    rw2 = np.asarray(inp["L0_ffn_r_w2"], f32)
    rb2 = np.asarray(inp["L0_ffn_r_b2"], f32)
    ew1 = np.asarray(inp["L0_ffn_e_w1"], f32)
    eb1 = np.asarray(inp["L0_ffn_e_b1"], f32)
    ew2 = np.asarray(inp["L0_ffn_e_w2"], f32)
    eb2 = np.asarray(inp["L0_ffn_e_b2"], f32)
    c2sw = np.asarray(inp["L0_ffn_c2s_w"], f32)
    c2sb = np.asarray(inp["L0_ffn_c2s_b"], f32)
    decw = np.asarray(inp["L0_ffn_dec_w"], f32)
    decb = np.asarray(inp["L0_ffn_dec_b"], f32)
    lng = np.asarray(inp["L0_ffn_ln_g"], f32)
    lnb = np.asarray(inp["L0_ffn_ln_b"], f32)

    wblk = np.zeros((8, 128, 512), f32)
    astat = np.zeros((8, 128, 16), f32)
    bxbcg = np.zeros((128, 32), f32)
    for p in range(8):
        for half in range(2):
            h = 2 * p + half
            r = slice(half * 64, half * 64 + 64)
            cix = slice(half * 64, half * 64 + 64)
            wblk[p, r, 0:128][:, cix] = wXBC[h, 0:64, :].T
            wblk[p, r, 128:256][:, cix] = wXBC[h, 64:128, :].T
            wblk[p, r, 256:384][:, cix] = wXBC[h, 128:192, :].T
            wblk[p, r, 384:512][:, cix] = wG[h, :, :].T
            astat[p, r, h] = wA[h, 0, :]
            bxbcg[r, p * 4 + 0] = bXBC[h, 0:64]
            bxbcg[r, p * 4 + 1] = bXBC[h, 64:128]
            bxbcg[r, p * 4 + 2] = bXBC[h, 128:192]
            bxbcg[r, p * 4 + 3] = bG[h, :]

    common = {
        "wblk": wblk, "astat": astat, "bxbcg": bxbcg,
        "bacol": bA[:, 0:1].copy(),
        "woutT": np.ascontiguousarray(Wout.T),
        "boutc": np.ascontiguousarray(bout.reshape(8, 128).T),
        "encw": encw, "encbc": np.ascontiguousarray(encb.reshape(16, 128).T),
        "s2cd": np.concatenate([s2cw, s2cw], axis=1),
        "s2cbc": np.concatenate([s2cb, s2cb])[:, None].copy(),
        "rw1": rw1, "rb1c": rb1[:, None].copy(), "rw2": rw2,
        "rb2rep": np.broadcast_to(rb2[None, :], (128, 8)).copy(),
        "ew1p": np.concatenate([ew1[0::2], ew1[1::2]], axis=1),
        "eb1c": np.ascontiguousarray(
            eb1.reshape(8, 8, 128).transpose(2, 0, 1).reshape(128, 64)),
        "ew2s": ew2, "eb2m": eb2,
        "c2sr": np.concatenate([c2sw, c2sw], axis=0),
        "c2sbc": np.ascontiguousarray(c2sb.reshape(16, 128).T),
        "decw": decw, "decbc": np.ascontiguousarray(decb.reshape(8, 128).T),
        "lngrep": np.broadcast_to(lng[None, :], (128, D)).copy(),
        "lnbrep": np.broadcast_to(lnb[None, :], (128, D)).copy(),
        "iden": np.eye(128, dtype=f32),
        "tripos": np.where(np.arange(128)[None, :] >= np.arange(128)[:, None],
                           0.0, 1e30).astype(f32),
        "tripost": np.ascontiguousarray(
            np.where(np.arange(128)[None, :] >= np.arange(128)[:, None],
                     0.0, 1e30).astype(f32).T),
        "iota8": np.broadcast_to(np.arange(8, dtype=f32)[None, :], (128, 8)).copy(),
        "big8": np.full((128, 8), 1e30, f32),
        "neg1t": np.full((128, 8), -1.0, f32),
        "ones16": np.ones((16, 128), f32),
    }

    in_maps = []
    for core in range(NCORE):
        b, half = core // 2, core % 2
        m0 = half * T_CORE
        w0 = m0 - WARM if half == 1 else 0
        xin = np.concatenate([x[b, w0:w0 + WARM, :], x[b, m0:m0 + T_CORE, :]],
                             axis=0)
        im = dict(common)
        im["xin"] = np.ascontiguousarray(xin)
        im["wmask"] = np.full((128, 1), 1.0 if half == 1 else 0.0, f32)
        in_maps.append(im)
    return in_maps


def kernel(**inputs):
    in_maps = _layout_inputs(inputs)
    nc = build_nc()
    res = run_bass_kernel_spmd(nc, in_maps, core_ids=list(range(NCORE)))
    out = np.empty((4, 4096, D), np.float32)
    for core in range(NCORE):
        b, half = core // 2, core % 2
        out[b, half * T_CORE:(half + 1) * T_CORE, :] = res.results[core]["out"]
    return out


# version 28 (final)
# speedup vs baseline: 1.2307x; 1.0086x over previous
import contextlib
import numpy as np
import concourse.bass as bass
import concourse.mybir as mybir
from concourse.bass_utils import run_bass_kernel_spmd

F32 = mybir.dt.float32
I32 = mybir.dt.int32
AF = mybir.ActivationFunctionType
ALU = mybir.AluOpType
AX = mybir.AxisListType

H, CH, NST, D = 16, 64, 64, 1024
HID, MOE, NEXP, EXPH = 2048, 64, 8, 1024
NCORE = 8
T_CORE = 2048
WARM = 512
T_TILE = 512
L = 128
CPT = T_TILE // L
N_WARM_TILES = WARM // T_TILE
N_MAIN_TILES = T_CORE // T_TILE
GRP = 512
GPT = T_TILE // GRP
DEBUG = False


class PB:

    ENGS = ("t", "v", "s", "g", "y")

    def __init__(self, nc, es):
        self.nc = nc
        self.prog = {e: [] for e in self.ENGS}
        self.sem = {k: es.enter_context(nc.semaphore("sem_" + k))
                    for k in ("t", "v", "s", "g", "A", "B")}
        self.cnt = {k: 0 for k in ("t", "v", "s", "g", "A", "B")}
        self.ldwf_ins = []

    def _skey(self, eng):
        return {"t": "t", "v": "v", "s": "s", "g": "g"}[eng]

    def emit(self, eng, build, waits=(), dma=None, inc=True, ldwf=False):
        sem, cnt = self.sem, self.cnt
        wl = [(k, c) for (k, c) in waits if c is not None and c > 0]
        if dma is None:
            key, amt = self._skey(eng), 1
        else:
            key, amt = dma, 16
        if inc:
            cnt[key] += amt
            target = cnt[key]
        else:
            target = None

        def fn(e, build=build, wl=wl, key=key, inc=inc, ldwf=ldwf, self_=self):
            for (k, c) in wl:
                e.wait_ge(sem[k], c)
            ins = build(e)
            if inc:
                ins.then_inc(sem[key], 1 if key not in ("A", "B") else 16)
            if ldwf:
                self_.ldwf_ins.append(ins.ins)
        self.prog[eng].append(fn)
        return (key, target) if inc else None

    def wait_only(self, eng, waits):
        sem = self.sem
        wl = [(k, c) for (k, c) in waits if c > 0]
        if not wl:
            return

        def fn(e, wl=wl):
            for (k, c) in wl:
                e.wait_ge(sem[k], c)
        self.prog[eng].append(fn)

    def barrier(self):
        snap = dict(self.cnt)
        for eng in ("t", "v", "s", "g", "y"):
            self.wait_only(eng, [(k, snap[k]) for k in ("t", "v", "s", "g", "A", "B")])


class Bank:
    def __init__(self, tensor):
        self.t = tensor
        self.wr = None
        self.rd = {}


def mk_mm(pb):
    def mm(bank, out_ap, lhsT, rhs, start, stop, waits=(), tp=None):
        w = [x for x in waits if x is not None]
        if start:
            w += list(bank.rd.items())
            if bank.wr is not None:
                w.append(bank.wr)

        def b(e, out_ap=out_ap, lhsT=lhsT, rhs=rhs, start=start, stop=stop, tp=tp):
            return e.matmul(out_ap, lhsT, rhs, start=start, stop=stop,
                            tile_position=tp)
        h = pb.emit("t", b, waits=w, inc=stop)
        if stop:
            bank.wr = h
            bank.rd = {}
        return h
    return mm


def mk_tr(pb):
    def tr(bank, out_ap, in_ap, iden_ap, waits=()):
        w = list(waits) + list(bank.rd.items())
        if bank.wr is not None:
            w.append(bank.wr)

        def b(e, out_ap=out_ap, in_ap=in_ap, iden_ap=iden_ap):
            return e.transpose(out_ap, in_ap, iden_ap)
        h = pb.emit("t", b, waits=w)
        bank.wr = h
        bank.rd = {}
        return h
    return tr


def rd_bank(pb, eng, bank, build, waits=()):
    w = list(waits)
    if bank.wr is not None:
        w.append(bank.wr)
    h = pb.emit(eng, build, waits=w)
    bank.rd[h[0]] = h[1]
    return h


def build_nc():
    nc = bass.Bass("TRN2", target_bir_lowering=False, debug=False)
    din = {}
    for name, shape in [
        ("xin", [WARM + T_CORE, D]), ("wmask", [128, 1]),
        ("wblk", [8, 128, 512]), ("astat", [8, 128, 16]),
        ("bxbcg", [128, 32]), ("bacol", [16, 1]),
        ("woutT", [D, D]), ("boutc", [128, 8]),
        ("encw", [D, HID]), ("encbc", [128, 16]),
        ("s2cd", [HID, 128]), ("s2cbc", [128, 1]),
        ("rw1", [64, 64]), ("rb1c", [64, 1]), ("rw2", [64, 8]),
        ("rb2rep", [128, 8]),
        ("ew1p", [4, 128, EXPH]), ("eb1c", [128, 64]),
        ("ew2s", [NEXP, EXPH, MOE]), ("eb2m", [8, 64]),
        ("c2sr", [128, HID]), ("c2sbc", [128, 16]),
        ("decw", [HID, D]), ("decbc", [128, 8]),
        ("lngrep", [128, D]), ("lnbrep", [128, D]),
        ("iden", [128, 128]), ("tripos", [128, 128]), ("tripost", [128, 128]),
        ("iota8", [128, 8]), ("big8", [128, 8]), ("neg1t", [128, 8]),
        ("ones16", [16, 128]),
    ]:
        din[name] = nc.dram_tensor(name, shape, F32, kind="ExternalInput").ap()
    dout = nc.dram_tensor("out", [T_CORE, D], F32, kind="ExternalOutput").ap()
    BF16 = mybir.dt.bfloat16
    scr = {}
    for nm, shape in [("ench", [D, HID]), ("encl", [D, HID]),
                      ("dech", [HID, D]), ("decl", [HID, D]),
                      ("wouth", [D, D]), ("woutl", [D, D]),
                      ("ew1h", [4, 128, EXPH]), ("ew1l", [4, 128, EXPH])]:
        scr[nm] = nc.dram_tensor("scr_" + nm, shape, BF16, kind="Internal").ap()
    dbg = {}
    if DEBUG:
        for nm, shp in [("h0", [128, 512]), ("xn", [128, 4096]),
                        ("proj", [128, 4096]), ("crow", [16, 128]),
                        ("cols", [128, 64]), ("ytm", [128, 1024]),
                        ("h1", [128, 512]), ("asb", [16, 128]),
                        ("stm", [128, 4096]), ("cont", [128, 512]),
                        ("spk", [128, 4096]), ("comb", [128, 32]),
                        ("moe", [128, 256]), ("mfd", [128, 512]),
                        ("rat", [128, 4096]), ("dec", [128, 2048]),
                        ("xtm", [128, 2048]), ("ygf", [128, 2048]),
                        ("lncols", [128, 8]), ("xb", [128, 1024]),
                        ("eu", [128, 8]), ("logit", [128, 8]), ("msk", [128, 16])]:
            dbg[nm] = nc.dram_tensor("dbg_" + nm, shp, F32, kind="ExternalOutput").ap()

    es = contextlib.ExitStack()
    sb = {}

    def S(name, shape, dt=F32):
        sb[name] = es.enter_context(nc.sbuf_tensor("sb_" + name, shape, dt))
        return sb[name]

    BF = mybir.dt.bfloat16
    S("wblkh", [128, 8 * 512], BF)
    S("wblkl", [128, 8 * 512], BF)
    S("astath", [128, 8 * 16], BF)
    S("astatl", [128, 8 * 16], BF)
    S("bxbcg", [128, 32]); S("bacol", [16, 1])
    S("encbc", [128, 16]); S("s2ch", [128, 16 * 128], BF); S("s2cl", [128, 16 * 128], BF)
    S("s2cbc", [128, 1])
    S("rw1", [64, 64]); S("rb1c", [64, 1]); S("rw2", [64, 8])
    S("rb2rep", [128, 8]); S("eb1c", [128, 64]); S("eb2m", [8, 64])
    S("c2srh", [128, HID], BF); S("c2srl", [128, HID], BF); S("c2sbc", [128, 16])
    S("decbc", [128, 8]); S("boutc", [128, 8])
    S("lngrep", [128, D]); S("lnbrep", [128, D])
    S("iden", [128, 128]); S("tripos", [128, 128]); S("tripost", [128, 128])
    S("iota8", [128, 8]); S("big8", [128, 8]); S("neg1t", [128, 8])
    S("ones16", [16, 128]); S("wmask", [128, 1])
    S("wstrb", [128, 32768], BF)
    S("xnfmh", [128, 8 * T_TILE], BF)
    S("xnfml", [128, 8 * T_TILE], BF)
    S("contb", [128, 2 * T_TILE], BF)
    S("stm", [128, CPT * D])
    S("htst", [128, 512])
    S("tr", [128, 13440])
    S("asb", [16, L]); S("sp16", [16, L]); S("crow", [16, L])
    S("frow", [16, L]); S("erow", [16, L]); S("dfree", [16, 128])
    S("cT", [128, 16]); S("fT", [128, 16]); S("eT", [128, 16]); S("dT", [128, 16])
    S("mucol", [128, 2]); S("s2col", [128, 2]); S("m2col", [128, 2])
    S("varcol", [128, 2]); S("rstdcol", [128, 2])
    S("bnst", [128, 24]); S("bnag", [128, 16])
    S("rhsb", [64, GRP])
    S("combm", [128, CPT * 8], I32)
    S("comb", [128, CPT * 8])
    S("combT", [8, CPT * 128])
    S("m1c", [128, 1]); S("m2c", [128, 1]); S("i1c", [128, 1]); S("i2c", [128, 1])
    S("dnc", [128, 1]); S("w1c", [128, 1]); S("w2c", [128, 1])
    S("logit", [128, 8]); S("eu", [128, 8]); S("eu2", [128, 8]); S("selb", [128, 8])
    S("clast", [16, 1]); S("dcol", [16, 1])

    pb = PB(nc, es)
    mm = mk_mm(pb)
    tr = mk_tr(pb)

    def mm3(bank, out_ap, sth, stl, mvh, mvl, start, stop, waits=()):
        w = [x for x in waits if x is not None]
        if start:
            w += list(bank.rd.items())
            if bank.wr is not None:
                w.append(bank.wr)
        if sth.base_partition() != 0:
            pb.emit("t", lambda e, out_ap=out_ap, sth=sth, mvh=mvh, start=start: e.matmul(
                out_ap, sth, mvh, start=start, stop=False), waits=w, inc=False)
            pb.emit("t", lambda e, out_ap=out_ap, sth=sth, mvl=mvl: e.matmul(
                out_ap, sth, mvl, start=False, stop=False), inc=False)
            h = pb.emit("t", lambda e, out_ap=out_ap, stl=stl, mvh=mvh, stop=stop: e.matmul(
                out_ap, stl, mvh, start=False, stop=stop), inc=stop)
            if stop:
                bank.wr = h
                bank.rd = {}
            return h
        pb.emit("t", lambda e, sth=sth: e.ldweights(sth), waits=w, inc=False)
        pb.emit("t", lambda e, out_ap=out_ap, sth=sth, mvh=mvh, start=start: e.matmul(
            out_ap, sth, mvh, start=start, stop=False), inc=False, ldwf=True)
        pb.emit("t", lambda e, out_ap=out_ap, sth=sth, mvl=mvl: e.matmul(
            out_ap, sth, mvl, start=False, stop=False), inc=False, ldwf=True)
        pb.emit("t", lambda e, stl=stl: e.ldweights(stl), inc=False)
        h = pb.emit("t", lambda e, out_ap=out_ap, stl=stl, mvh=mvh, stop=stop: e.matmul(
            out_ap, stl, mvh, start=False, stop=stop), inc=stop, ldwf=True)
        if stop:
            bank.wr = h
            bank.rd = {}
        return h
    banks = [Bank(es.enter_context(nc.psum_tensor("pb%d" % i, [128, 512], F32)))
             for i in range(8)]

    TRv = sb["tr"]
    A_XB, A_X2, A_XN = 0, 1024, 2048
    B_PROJ = 0
    B_XT = 4096
    B_YTM = 6144
    B_YG = 7168
    B_RT, B_GB, B_MT = 11264, 11392, 11520
    B_YT2, B_BW = 11648, 11712
    B_WSC = 11264
    C_RS2 = 11840
    C_SPB = 12352
    C_CONT = 12864
    C_EH = 0
    C_MOE = 9216
    C_MFD = 9472
    C_MTMP = 9984
    C_RAT = 0
    C_RS = 8192
    C_DTM = 8704
    C_G, C_SG, C_XR, C_OB = 0, 1024, 2048, 3072
    C_EO = 13056

    iden = sb["iden"]

    def evac(eng, bank, psrc, dst, func=None, bias=None, scale=1.0, waits=()):
        if eng == "s":
            def b(e, dst=dst, psrc=psrc, func=func, bias=bias, scale=scale):
                return e.activation(out=dst, in_=psrc,
                                    func=(func or AF.Identity),
                                    bias=(0.0 if bias is None else bias),
                                    scale=scale)
        else:
            def b(e, dst=dst, psrc=psrc):
                return e.tensor_copy(dst, psrc)
        return rd_bank(pb, eng, bank, b, waits=waits)

    ld = {}
    init_loads = []
    for name in ["bxbcg", "bacol", "encbc", "s2cbc", "rw1", "rb1c", "rw2",
                 "rb2rep", "eb1c", "eb2m", "c2sbc", "decbc", "boutc",
                 "lngrep", "lnbrep", "iden", "tripos", "tripost", "iota8", "big8",
                 "neg1t", "ones16", "wmask"]:
        init_loads.append((name, sb[name], [(din[name][:, :], sb[name][:, :])]))

    for name, _sbuf, pairs in init_loads:
        for (src, dst) in pairs:
            h = pb.emit("y", lambda e, dst=dst, src=src: e.dma_start(out=dst, in_=src),
                        dma="A")
        ld[name] = h
    init_done = pb.cnt["A"]
    pb.barrier()
    pb.emit("v", lambda e: e.memset(sb["htst"][:, :], 0.0))
    pb.emit("v", lambda e: e.memset(TRv[:, B_BW:B_BW + 128], 0.0))
    pb.barrier()

    wsb = sb["wstrb"]
    wsb_f32 = wsb[:, :].bitcast(F32)

    def split_to(dst_h, dst_l, f32_ap, hload):
        h1 = pb.emit("s", lambda e, dst_h=dst_h, f32_ap=f32_ap: e.activation(
            out=dst_h, in_=f32_ap, func=AF.Identity), waits=[hload])
        h2 = pb.emit("g", lambda e, dst_l=dst_l, f32_ap=f32_ap, dst_h=dst_h: e.tensor_sub(
            dst_l, f32_ap, dst_h), waits=[h1])
        return h2

    ph0 = {}
    for p in range(8):
        pp = p % 2
        of = pp * 1024
        wd = [h for h in (ph0.get(pp),) if h]
        hl = pb.emit("y", lambda e, p=p, of=of: e.dma_start(out=TRv[:, of:of + 512], in_=din["wblk"][p, :, :]),
                     waits=wd, dma="A")
        hs = split_to(sb["wblkh"][:, p * 512:(p + 1) * 512], sb["wblkl"][:, p * 512:(p + 1) * 512],
                      TRv[:, of:of + 512], hl)
        hl2 = pb.emit("y", lambda e, p=p, of=of: e.dma_start(out=TRv[:, of + 512:of + 528], in_=din["astat"][p, :, :]),
                      waits=[hs], dma="A")
        hs2 = split_to(sb["astath"][:, p * 16:(p + 1) * 16], sb["astatl"][:, p * 16:(p + 1) * 16],
                       TRv[:, of + 512:of + 528], hl2)
        ph0[pp] = hs2
    ph1 = {}
    for k in range(16):
        kp = k % 2
        of = kp * 128
        wd = [h for h in (ph1.get(kp),) if h]
        hl = pb.emit("y", lambda e, k=k, of=of: e.dma_start(out=TRv[:, 2048 + of:2048 + of + 128],
                                                            in_=din["s2cd"][k * 128:(k + 1) * 128, :]),
                     waits=wd, dma="A")
        hs = split_to(sb["s2ch"][:, k * 128:(k + 1) * 128], sb["s2cl"][:, k * 128:(k + 1) * 128],
                      TRv[:, 2048 + of:2048 + of + 128], hl)
        ph1[kp] = hs
    hl = pb.emit("y", lambda e: e.dma_start(out=TRv[:, 2432:4480], in_=din["c2sr"][:, :]), dma="A")
    hs = split_to(sb["c2srh"][:, :], sb["c2srl"][:, :], TRv[:, 2432:4480], hl)
    big_specs = [("enc", din["encw"], scr["ench"], scr["encl"], 8, 2048),
                 ("wout", din["woutT"], scr["wouth"], scr["woutl"], 8, 1024),
                 ("dec", din["decw"], scr["dech"], scr["decl"], 16, 1024)]
    jobs = []
    for (kind, dsrc, dh, dl, nchunk, cw) in big_specs:
        for k in range(nchunk):
            jobs.append((kind, dsrc[k * 128:(k + 1) * 128, :], dh[k * 128:(k + 1) * 128, :],
                         dl[k * 128:(k + 1) * 128, :], cw))
    for q in range(4):
        jobs.append(("ew1", din["ew1p"][q, :, :], scr["ew1h"][q, :, :], scr["ew1l"][q, :, :], 1024))
    store_h = {}
    split_ph = {}
    split_pos = [0]

    def emit_split_jobs(n):
        i0 = split_pos[0]
        for i in range(i0, min(i0 + n, len(jobs))):
            kind, jsrc, jdh, jdl, cw = jobs[i]
            p = i % 2
            trof = 8192 + p * 2048
            wbof = p * 4096
            wd = [h for h in (split_ph.get(("sp", p)),) if h]
            hl = pb.emit("y", lambda e, jsrc=jsrc, trof=trof, cw=cw: e.dma_start(
                out=wsb_f32[:, trof:trof + cw], in_=jsrc), waits=wd, dma="A")
            ws = [hl] + [h for h in (split_ph.get(("st", p)),) if h]
            h1 = pb.emit("s", lambda e, wbof=wbof, trof=trof, cw=cw: e.activation(
                out=wsb[:, wbof:wbof + cw], in_=wsb_f32[:, trof:trof + cw],
                func=AF.Identity), waits=ws)
            h2 = pb.emit("g", lambda e, wbof=wbof, trof=trof, cw=cw: e.tensor_sub(
                wsb[:, wbof + cw:wbof + 2 * cw], wsb_f32[:, trof:trof + cw],
                wsb[:, wbof:wbof + cw]), waits=[h1])
            split_ph[("sp", p)] = h2
            pb.emit("g", lambda e, jdh=jdh, wbof=wbof, cw=cw: e.dma_start(
                out=jdh, in_=wsb[:, wbof:wbof + cw]), waits=[h2], dma="B")
            hs2 = pb.emit("g", lambda e, jdl=jdl, wbof=wbof, cw=cw: e.dma_start(
                out=jdl, in_=wsb[:, wbof + cw:wbof + 2 * cw]), waits=[h2], dma="B")
            split_ph[("st", p)] = hs2
            store_h[kind] = hs2
        split_pos[0] = min(i0 + n, len(jobs))
    pb.wait_only("y", [h for h in (ph0.get(0), ph0.get(1), ph1.get(0), ph1.get(1), hs) if h])
    pb.barrier()

    def load_wout():
        h = None
        for k in range(8):
            pb.emit("y", lambda e, k=k: e.dma_start(
                out=wsb[:, k * 1024:(k + 1) * 1024],
                in_=scr["wouth"][k * 128:(k + 1) * 128, :]), dma="A",
                waits=([store_h["wout"]] if (k == 0 and "wout" in store_h) else []))
            h = pb.emit("y", lambda e, k=k: e.dma_start(
                out=wsb[:, 8192 + k * 1024:8192 + (k + 1) * 1024],
                in_=scr["woutl"][k * 128:(k + 1) * 128, :]), dma="A")
        return h

    def load_enc():
        h = None
        for k in range(8):
            pb.emit("y", lambda e, k=k: e.dma_start(
                out=wsb[:, k * 2048:(k + 1) * 2048],
                in_=scr["ench"][k * 128:(k + 1) * 128, :]), dma="A",
                waits=([store_h["enc"]] if (k == 0 and "enc" in store_h) else []))
            h = pb.emit("y", lambda e, k=k: e.dma_start(
                out=wsb[:, 16384 + k * 2048:16384 + (k + 1) * 2048],
                in_=scr["encl"][k * 128:(k + 1) * 128, :]), dma="A")
        return h

    def load_experts():
        for q in range(4):
            pb.emit("y", lambda e, q=q: e.dma_start(
                out=wsb[:, q * 1024:(q + 1) * 1024], in_=scr["ew1h"][q, :, :]), dma="A",
                waits=([store_h["ew1"]] if (q == 0 and "ew1" in store_h) else []))
            pb.emit("y", lambda e, q=q: e.dma_start(
                out=wsb[:, 4096 + q * 1024:4096 + (q + 1) * 1024], in_=scr["ew1l"][q, :, :]), dma="A")
        h = None
        for ee in range(NEXP):
            for k in range(8):
                h = pb.emit("y", lambda e, ee=ee, k=k: e.dma_start(
                    out=wsb_f32[:, 4096 + ee * 512 + k * 64: 4096 + ee * 512 + (k + 1) * 64],
                    in_=din["ew2s"][ee, k * 128:(k + 1) * 128, :]), dma="A")
        return h

    def load_dec():
        h = None
        for k in range(16):
            pb.emit("y", lambda e, k=k: e.dma_start(
                out=wsb[:, k * 1024:(k + 1) * 1024],
                in_=scr["dech"][k * 128:(k + 1) * 128, :]), dma="A",
                waits=([store_h["dec"]] if (k == 0 and "dec" in store_h) else []))
            h = pb.emit("y", lambda e, k=k: e.dma_start(
                out=wsb[:, 16384 + k * 1024:16384 + (k + 1) * 1024],
                in_=scr["decl"][k * 128:(k + 1) * 128, :]), dma="A")
        return h

    def phase_A_chunk(xin_row0, c, prev, base=0, bksel=(0, 1), xn_guard=None):
        lastev = None
        if True:
            cp = c % 2
            XB, XN = base + cp * 2048, base + cp * 2048 + 1024
            mcol = sb["mucol"][:, cp:cp + 1]
            vcol = sb["varcol"][:, cp:cp + 1]
            rcol = sb["rstdcol"][:, cp:cp + 1]
            bnst = sb["bnst"][:, cp * 12:(cp + 1) * 12]
            ag = sb["bnag"][:, cp * 8:(cp + 1) * 8]
            w_dma = [h for h in (prev.get(("xn", cp)),) if h]
            hdma = pb.emit("y", lambda e, XB=XB, c=c: e.dma_start(
                out=TRv[:, XB:XB + 1024],
                in_=din["xin"][xin_row0 + c * L: xin_row0 + (c + 1) * L, :]),
                waits=w_dma, dma="A")
            pb.emit("v", lambda e, XB=XB, bnst=bnst: e.bn_stats(bnst[:, 0:6], TRv[:, XB:XB + 512]),
                    waits=[hdma])
            pb.emit("v", lambda e, XB=XB, bnst=bnst: e.bn_stats(bnst[:, 6:12], TRv[:, XB + 512:XB + 1024]))
            pb.emit("v", lambda e, ag=ag, bnst=bnst: e.bn_aggr(ag[:, 0:2], bnst[:, 0:6]))
            ha2 = pb.emit("v", lambda e, ag=ag, bnst=bnst: e.bn_aggr(ag[:, 2:4], bnst[:, 6:12]))
            hsm = pb.emit("v", lambda e, ag=ag: e.tensor_add(ag[:, 4:6], ag[:, 0:2], ag[:, 2:4]),
                          waits=[ha2])
            hdf = pb.emit("v", lambda e, ag=ag: e.tensor_sub(ag[:, 6:7], ag[:, 0:1], ag[:, 2:3]),
                          waits=[ha2])
            hmu = pb.emit("v", lambda e, mcol=mcol, ag=ag: e.tensor_single_scalar(
                mcol, ag[:, 4:5], 0.5, ALU.mult), waits=[hsm])
            hd2 = pb.emit("v", lambda e, ag=ag: e.tensor_mul(ag[:, 7:8], ag[:, 6:7], ag[:, 6:7]),
                          waits=[hdf])
            hv2 = pb.emit("v", lambda e, vcol=vcol, ag=ag: e.scalar_tensor_tensor(
                vcol, ag[:, 7:8], 0.5, ag[:, 5:6], ALU.mult, ALU.add), waits=[hd2, hsm])
            hv = pb.emit("v", lambda e, vcol=vcol: e.tensor_single_scalar(
                vcol, vcol, 2e-5, ALU.add), waits=[hv2])
            hsd = pb.emit("s", lambda e, rcol=rcol, vcol=vcol: e.activation(
                out=rcol, in_=vcol, func=AF.Sqrt, scale=0.5), waits=[hv])
            hrs = pb.emit("v", lambda e, rcol=rcol: e.reciprocal(out=rcol, in_=rcol),
                          waits=[hsd])
            w_xn = [hrs, hmu] + ([prev.get(("tr", cp))] if prev.get(("tr", cp)) else [])
            hxn = pb.emit("v", lambda e, XB=XB, XN=XN, mcol=mcol, rcol=rcol: e.tensor_scalar(
                TRv[:, XN:XN + 1024], TRv[:, XB:XB + 1024], mcol[:, 0:1], rcol[:, 0:1],
                ALU.subtract, ALU.mult), waits=w_xn)
            prev[("xn", cp)] = hxn
            htr = None
            for f in range(8):
                bk = banks[bksel[f % 2]]
                htr = tr(bk, bk.t[:, 0:128], TRv[:, XN + f * 128:XN + (f + 1) * 128],
                         iden[:, :], waits=[hxn])
                xh = sb["xnfmh"][:, f * T_TILE + c * L: f * T_TILE + (c + 1) * L]
                xl = sb["xnfml"][:, f * T_TILE + c * L: f * T_TILE + (c + 1) * L]
                hh1 = evac("v", bk, bk.t[:, 0:128], xh,
                           waits=([xn_guard] if (xn_guard and f == 0) else []))
                lastev = rd_bank(pb, "v", bk, lambda e, xl=xl, bk=bk, xh=xh: e.tensor_sub(
                    xl, bk.t[:, 0:128], xh), waits=[hh1])
            prev[("tr", cp)] = htr
        return lastev

    def phase_A(xin_row0):
        prev = {}
        lastev = None
        for c in range(CPT):
            lastev = phase_A_chunk(xin_row0, c, prev)
        return lastev

    def proj_chunk(c, warm, snap):
        groups = [0, 1] if warm else [0, 1, 2, 3]
        sw = [h for h in (snap.get("t"), snap.get("v")) if h]
        evh = {}
        for p in range(8):
            bk = banks[p % 2]
            movh = sb["xnfmh"][:, p * T_TILE + c * L: p * T_TILE + (c + 1) * L]
            movl = sb["xnfml"][:, p * T_TILE + c * L: p * T_TILE + (c + 1) * L]
            for gi, g in enumerate(groups):
                mm3(bk, bk.t[:, g * 128:(g + 1) * 128],
                    sb["wblkh"][:, p * 512 + g * 128: p * 512 + (g + 1) * 128],
                    sb["wblkl"][:, p * 512 + g * 128: p * 512 + (g + 1) * 128],
                    movh, movl, start=(gi == 0), stop=(gi == len(groups) - 1),
                    waits=sw if gi == 0 else ())
            for g in groups:
                fn = AF.Silu if g < 3 else AF.Identity
                evh[(p, g)] = evac("s", bk, bk.t[:, g * 128:(g + 1) * 128],
                                   TRv[:, B_PROJ + p * 512 + g * 128: B_PROJ + p * 512 + (g + 1) * 128],
                                   func=fn, bias=sb["bxbcg"][:, p * 4 + g: p * 4 + g + 1],
                                   waits=sw)
        bkA = banks[2]
        for p in range(8):
            movh = sb["xnfmh"][:, p * T_TILE + c * L: p * T_TILE + (c + 1) * L]
            movl = sb["xnfml"][:, p * T_TILE + c * L: p * T_TILE + (c + 1) * L]
            for pi, (st_, mv_) in enumerate([
                    (sb["astath"][:, p * 16:(p + 1) * 16], movh),
                    (sb["astath"][:, p * 16:(p + 1) * 16], movl),
                    (sb["astatl"][:, p * 16:(p + 1) * 16], movh)]):
                mm(bkA, bkA.t[0:16, 0:128], st_, mv_,
                   start=(p == 0 and pi == 0), stop=(p == 7 and pi == 2))
        ha = evac("s", bkA, bkA.t[0:16, 0:128], sb["asb"][:, :],
                  func=AF.Identity, bias=sb["bacol"][:, 0:1], waits=sw)
        return evh, ha

    def decay_chunk(ha, warm, snap):
        ts_ = [snap["t"]] if snap.get("t") else []
        ss_ = [snap["s"]] if snap.get("s") else []
        vs_ = [snap["v"]] if snap.get("v") else []
        h1 = pb.emit("s", lambda e: e.activation(out=sb["sp16"][:, :], in_=sb["asb"][:, :], func=AF.Abs),
                     waits=[ha] + vs_)
        h2 = pb.emit("s", lambda e: e.activation(out=sb["sp16"][:, :], in_=sb["sp16"][:, :], func=AF.Exp, scale=-1.0),
                     waits=[h1])
        h3 = pb.emit("s", lambda e: e.activation(out=sb["sp16"][:, :], in_=sb["sp16"][:, :], func=AF.Ln, bias=1.0),
                     waits=[h2])
        h4 = pb.emit("v", lambda e: e.tensor_single_scalar(sb["crow"][:, :], sb["asb"][:, :], 0.0, ALU.max),
                     waits=[ha] + ts_)
        h5 = pb.emit("v", lambda e: e.tensor_add(sb["sp16"][:, :], sb["sp16"][:, :], sb["crow"][:, :]),
                     waits=[h3, h4])
        h6 = pb.emit("v", lambda e: e.tensor_tensor_scan(sb["crow"][:, :], sb["ones16"][:, :], sb["sp16"][:, :],
                                                         0.0, ALU.mult, ALU.add), waits=[h5])
        h7 = pb.emit("v", lambda e: e.tensor_copy(sb["clast"][:, :], sb["crow"][:, L - 1:L]), waits=[h6] + ss_)
        h8 = pb.emit("v", lambda e: e.tensor_scalar(sb["frow"][:, :], sb["crow"][:, :], sb["clast"][:, 0:1], None,
                                                    ALU.subtract), waits=[h7] + ts_)
        h9 = pb.emit("s", lambda e: e.activation(out=sb["frow"][:, :], in_=sb["frow"][:, :], func=AF.Exp),
                     waits=[h8])
        hD = pb.emit("s", lambda e: e.activation(out=sb["dcol"][:, :], in_=sb["clast"][:, :], func=AF.Exp, scale=-1.0),
                     waits=[h7])
        hDf = pb.emit("v", lambda e: e.tensor_copy(sb["dfree"][:, :], sb["dcol"][:, 0:1].broadcast_to([16, 128])),
                      waits=[hD] + ts_)
        hE = None
        if not warm:
            hE = pb.emit("s", lambda e: e.activation(out=sb["erow"][:, :], in_=sb["crow"][:, :], func=AF.Exp,
                                                     scale=-1.0), waits=[h6] + ts_)
        bk = banks[2]
        tr(bk, bk.t[0:128, 128:144], sb["frow"][:, :], iden[0:16, 0:16], waits=[h9])
        hfT = evac("v", bk, bk.t[0:128, 128:144], sb["fT"][:, :])
        tr(bk, bk.t[0:128, 144:160], sb["dfree"][:, :], iden[0:16, 0:16], waits=[hDf])
        hdT = evac("v", bk, bk.t[0:128, 144:160], sb["dT"][:, :])
        hcT = heT = None
        if not warm:
            tr(bk, bk.t[0:128, 160:176], sb["crow"][:, :], iden[0:16, 0:16], waits=[h6])
            cw = [h for h in (rt_state["gp"][H - 1], rt_state["sexp"]) if h]
            hcT = evac("v", bk, bk.t[0:128, 160:176], sb["cT"][:, :], waits=cw)
            tr(bk, bk.t[0:128, 176:192], sb["erow"][:, :], iden[0:16, 0:16], waits=[hE])
            heT = evac("v", bk, bk.t[0:128, 176:192], sb["eT"][:, :])
        return hfT, hdT, hcT, heT

    def xb_transpose(c, evh, warm, snap):
        ts_ = [snap["t"]] if snap.get("t") else []
        hx = {}
        for p in range(8):
            bk = banks[3]
            tr(bk, bk.t[:, 0:128], TRv[:, B_PROJ + p * 512: B_PROJ + p * 512 + 128],
               iden[:, :], waits=[evh[(p, 0)]])
            h1 = evac("v", bk, bk.t[:, 0:128], TRv[:, B_XT + p * 256: B_XT + p * 256 + 128],
                      waits=(ts_ if p == 0 else []))
            tr(bk, bk.t[:, 128:256], TRv[:, B_PROJ + p * 512 + 128: B_PROJ + p * 512 + 256],
               iden[:, :], waits=[evh[(p, 1)]])
            h2 = evac("v", bk, bk.t[:, 128:256], TRv[:, B_XT + p * 256 + 128: B_XT + p * 256 + 256])
            hx[p] = (h1, h2)
        return hx

    state_dep = [None]
    enc_state = {"split": None, "spk": {}}
    router_state = {}
    cc_prev = {}
    rt_state = {"gp": [None] * H, "tr": [None] * H, "sexp": None}

    def scan_chunk(c, evh, hx, hfT, hdT, hcT, heT, warm, snap):
        prev_yi_mm = snap.get("t") and ("t", snap["t"][1]) or None
        prev_hu_mm = prev_yi_mm
        hlast_y = None
        first_y = True
        prev_hmt = snap.get("v")
        if not warm:
            for h in range(H):
                wg = [hcT] + ([rt_state["tr"][h]] if rt_state["tr"][h] else [])
                rt_state["gp"][h] = pb.emit("g", lambda e, h=h: e.tensor_scalar(
                    sb["stm"][:, h * 128:(h + 1) * 128], sb["tripost"][:, :],
                    sb["cT"][:, h:h + 1], None, ALU.add), waits=wg)
        for h in range(H):
            p, half = h // 2, h % 2
            hb = half * 64
            Xtm = TRv[:, B_XT + p * 256: B_XT + p * 256 + 128]
            Btm = TRv[:, B_XT + p * 256 + 128: B_XT + p * 256 + 256]
            hst = sb["htst"][hb:hb + 64, p * 64:(p + 1) * 64]
            bkY = banks[5]
            if not warm:
                bkS = banks[3 + (h % 2)]
                Bfm = TRv[:, B_PROJ + p * 512 + 128 + c * 0: B_PROJ + p * 512 + 256]
                Cfm = TRv[:, B_PROJ + p * 512 + 256: B_PROJ + p * 512 + 384]
                mm(bkS, bkS.t[:, 0:128], Bfm[hb:hb + 64, :], Cfm[hb:hb + 64, :],
                   start=True, stop=True, waits=[evh[(p, 1)], evh[(p, 2)]])
                rt_state["tr"][h] = tr(bkS, bkS.t[:, 128:256],
                                       sb["stm"][:, h * 128:(h + 1) * 128], iden[:, :],
                                       waits=[rt_state["gp"][h]])
                hexp = evac("s", bkS, bkS.t[:, 128:256], TRv[:, B_GB:B_GB + 128],
                            func=AF.Exp, bias=sb["cT"][:, h:h + 1], scale=-1.0,
                            waits=[hcT] + ([prev_hmt] if prev_hmt else []))
                rt_state["sexp"] = hexp
                hmt = rd_bank(pb, "v", bkS, lambda e, bkS=bkS: e.tensor_mul(
                    TRv[:, B_MT:B_MT + 128], TRv[:, B_GB:B_GB + 128], bkS.t[:, 0:128]),
                    waits=[hexp] + ([prev_yi_mm] if prev_yi_mm else []))
                prev_hmt = hmt
                prev_yi_mm = mm(bkY, bkY.t[:, 0:64], TRv[:, B_MT:B_MT + 128], Xtm[:, hb:hb + 64],
                                start=True, stop=True, waits=[hmt, hx[p][0]])
                sw = [evh[(p, 2)]]
                if state_dep[0] is not None:
                    sw.append(state_dep[0])
                mm(bkY, bkY.t[:, 64:128], Cfm[hb:hb + 64, :], hst, start=True, stop=True,
                   waits=sw)
            hbw = pb.emit("v", lambda e, h=h, hb=hb, Btm=Btm: e.tensor_scalar(
                TRv[:, B_BW + hb:B_BW + hb + 64], Btm[:, hb:hb + 64], sb["fT"][:, h:h + 1],
                None, ALU.mult),
                waits=[hfT, hx[p][1]] + ([prev_hu_mm] if prev_hu_mm else []))
            prev_hu_mm = mm(bkY, bkY.t[:, 128:192] if not warm else bkY.t[:, 0:64],
                            TRv[:, B_BW:B_BW + 128], Xtm[:, hb:hb + 64],
                            start=True, stop=True, waits=[hbw, hx[p][0]])
            upd_ap = (bkY.t[:, 128:192] if not warm else bkY.t[:, 0:64])
            if not warm:
                hy1 = rd_bank(pb, "v", bkY, lambda e, bkY=bkY, h=h: e.tensor_scalar(
                    TRv[:, B_YT2:B_YT2 + 64], bkY.t[:, 64:128], sb["eT"][:, h:h + 1],
                    None, ALU.mult), waits=[heT])
                hlast_y = rd_bank(pb, "v", bkY, lambda e, bkY=bkY, h=h: e.tensor_add(
                    TRv[:, B_YTM + h * 64:B_YTM + (h + 1) * 64],
                    TRv[:, B_YT2:B_YT2 + 64], bkY.t[:, 0:64]),
                    waits=[hy1] + ([snap["t"]] if (first_y and snap.get("t")) else []))
                first_y = False
            hup = rd_bank(pb, "v", bkY, lambda e, hst=hst, h=h, hb=hb, upd_ap=upd_ap: e.scalar_tensor_tensor(
                hst, hst, sb["dT"][hb:hb + 64, h:h + 1], upd_ap[hb:hb + 64, :],
                ALU.mult, ALU.add), waits=[hdT])
            state_dep[0] = hup
        return hlast_y, None

    def yg_chunk(c, evh, hy, hrt):
        trb = TRv[:, :].bitcast(mybir.dt.bfloat16)
        hlast = None
        for p in range(8):
            bk = banks[0 + (p % 2)]
            tr(bk, bk.t[:, 0:128], TRv[:, B_YTM + p * 128:B_YTM + (p + 1) * 128],
               iden[:, :], waits=[hy])
            hm = rd_bank(pb, "v", bk, lambda e, bk=bk, p=p: e.tensor_mul(
                TRv[:, B_RT:B_RT + 128],
                TRv[:, B_PROJ + p * 512 + 384: B_PROJ + p * 512 + 512],
                bk.t[:, 0:128]), waits=[evh[(p, 3)]] + ([hrt] if (p == 0 and hrt) else []))
            ygh = trb[:, 2 * B_YG + p * 512 + c * 128: 2 * B_YG + p * 512 + c * 128 + 128]
            ygl = trb[:, 2 * B_YG + 4096 + p * 512 + c * 128: 2 * B_YG + 4096 + p * 512 + c * 128 + 128]
            hh = pb.emit("v", lambda e, ygh=ygh: e.tensor_copy(ygh, TRv[:, B_RT:B_RT + 128]),
                         waits=[hm])
            hlast = pb.emit("v", lambda e, ygl=ygl, ygh=ygh: e.tensor_sub(
                ygl, TRv[:, B_RT:B_RT + 128], ygh), waits=[hh])
        return hlast

    def wout_group(hwout, hyg):
        trb = TRv[:, :].bitcast(mybir.dt.bfloat16)
        for o in range(8):
            bk = banks[2 + (o % 2)]
            for k in range(8):
                sth = wsb[:, k * 1024 + o * 128: k * 1024 + (o + 1) * 128]
                stl = wsb[:, 8192 + k * 1024 + o * 128: 8192 + k * 1024 + (o + 1) * 128]
                mvh = trb[:, 2 * B_YG + k * 512: 2 * B_YG + (k + 1) * 512]
                mvl = trb[:, 2 * B_YG + 4096 + k * 512: 2 * B_YG + 4096 + (k + 1) * 512]
                mm3(bk, bk.t[:, 0:512], sth, stl, mvh, mvl,
                    start=(k == 0), stop=(k == 7),
                    waits=([hwout, hyg] if k == 0 else []))
            hsf = evac("s", bk, bk.t[:, 0:512], TRv[:, B_WSC:B_WSC + 512],
                       func=AF.Identity, bias=sb["boutc"][:, o:o + 1])
            for cc in range(CPT):
                bkt = banks[6 + (cc % 2)]
                tr(bkt, bkt.t[:, 0:128],
                   TRv[:, B_WSC + cc * 128: B_WSC + (cc + 1) * 128],
                   iden[:, :], waits=[hsf])
                hsf2 = evac("v", bkt, bkt.t[:, 0:128],
                            sb["stm"][:, cc * D + o * 128: cc * D + (o + 1) * 128])
            pb.wait_only("s", [pb.cnt and ("t", pb.cnt["t"])])

    def enc_j(g, j, henc):
        trb = TRv[:, :].bitcast(mybir.dt.bfloat16)
        bk = banks[7]
        bks = banks[6]
        for k in range(8):
            sth = wsb[:, k * 2048 + j * 128: k * 2048 + (j + 1) * 128]
            stl = wsb[:, 16384 + k * 2048 + j * 128: 16384 + k * 2048 + (j + 1) * 128]
            mvh = sb["xnfmh"][:, k * T_TILE + g * GRP: k * T_TILE + (g + 1) * GRP]
            mvl = sb["xnfml"][:, k * T_TILE + g * GRP: k * T_TILE + (g + 1) * GRP]
            mm3(bk, bk.t[:, 0:GRP], sth, stl, mvh, mvl,
                start=(k == 0), stop=(k == 7), waits=([henc] if k == 0 else []))
        hv0 = evac("s", bk, bk.t[:, 0:GRP], TRv[:, C_RS2:C_RS2 + GRP],
                   func=AF.Sigmoid, bias=sb["encbc"][:, j:j + 1],
                   waits=([enc_state["s2c"]] if enc_state["s2c"] else []))
        sh_ap = trb[:, 2 * C_SPB: 2 * C_SPB + GRP]
        sl_ap = trb[:, 2 * C_SPB + GRP: 2 * C_SPB + 2 * GRP]
        hv1 = pb.emit("v", lambda e, sh_ap=sh_ap: e.tensor_copy(sh_ap, TRv[:, C_RS2:C_RS2 + GRP]),
                      waits=[hv0])
        hv2 = pb.emit("v", lambda e, sl_ap=sl_ap, sh_ap=sh_ap: e.tensor_sub(
            sl_ap, TRv[:, C_RS2:C_RS2 + GRP], sh_ap), waits=[hv1])
        h = mm3(bks, bks.t[:, 0:GRP], sb["s2ch"][:, j * 128:(j + 1) * 128],
                sb["s2cl"][:, j * 128:(j + 1) * 128], sh_ap, sl_ap,
                start=(j == 0), stop=(j == 15), waits=[hv2])
        enc_state["s2c"] = h if h else enc_state["s2c"]
        if j == 15:
            hc = evac("v", bks, bks.t[:, 0:GRP], TRv[:, C_CONT:C_CONT + GRP])
            enc_state["cont"] = hc
    def router_group(g, hcont):
        contg = TRv[:, C_CONT + g * GRP: C_CONT + (g + 1) * GRP]
        bk = banks[5]
        mm(bk, bk.t[0:64, 0:GRP], sb["rw1"][:, :], contg[0:64, :], start=True,
           stop=True, waits=[hcont])
        hrh = evac("s", bk, bk.t[0:64, 0:GRP], sb["rhsb"][:, :], func=AF.Relu,
                   bias=sb["rb1c"][:, 0:1])
        for ci in range(CPT):
            cc = CPT * g + ci
            bkl = banks[6]
            mm(bkl, bkl.t[:, 0:8], sb["rhsb"][:, ci * 128:(ci + 1) * 128],
               sb["rw2"][:, :], start=True, stop=True, waits=[hrh])
            hl = rd_bank(pb, "v", bkl, lambda e, bkl=bkl: e.tensor_add(
                sb["logit"][:, :], bkl.t[:, 0:8], sb["rb2rep"][:, :]))
            hm = pb.emit("v", lambda e: e.reduce_max(sb["m1c"][:, :], sb["logit"][:, :], AX.X, negate=True),
                         waits=[hl])
            hm2_ = pb.emit("v", lambda e: e.tensor_scalar(sb["eu"][:, :], sb["logit"][:, :],
                                                          sb["m1c"][:, 0:1], None, ALU.add), waits=[hm])
            he = pb.emit("s", lambda e: e.activation(out=sb["eu"][:, :], in_=sb["eu"][:, :],
                                                     func=AF.Exp), waits=[hm2_])
            h1 = pb.emit("v", lambda e: e.reduce_max(sb["m1c"][:, :], sb["eu"][:, :], AX.X), waits=[he])
            h2 = pb.emit("v", lambda e: e.tensor_scalar(sb["combm"][:, 0:8], sb["eu"][:, :], sb["m1c"][:, 0:1],
                                                        None, ALU.is_ge), waits=[h1])
            h3 = pb.emit("v", lambda e: e.select(sb["selb"][:, :], sb["combm"][:, 0:8], sb["iota8"][:, :],
                                                 sb["big8"][:, :]), waits=[h2])
            h4 = pb.emit("v", lambda e: e.tensor_reduce(sb["i1c"][:, :], sb["selb"][:, :], AX.X, ALU.min),
                         waits=[h3])
            h5 = pb.emit("v", lambda e: e.tensor_scalar(sb["combm"][:, 0:8], sb["iota8"][:, :], sb["i1c"][:, 0:1],
                                                        None, ALU.is_equal), waits=[h4])
            h6 = pb.emit("v", lambda e: e.select(sb["eu2"][:, :], sb["combm"][:, 0:8], sb["neg1t"][:, :],
                                                 sb["eu"][:, :]), waits=[h5])
            h7 = pb.emit("v", lambda e: e.reduce_max(sb["m2c"][:, :], sb["eu2"][:, :], AX.X), waits=[h6])
            h8 = pb.emit("v", lambda e: e.tensor_scalar(sb["combm"][:, 8:16], sb["eu2"][:, :], sb["m2c"][:, 0:1],
                                                        None, ALU.is_ge), waits=[h7])
            h9 = pb.emit("v", lambda e: e.select(sb["selb"][:, :], sb["combm"][:, 8:16], sb["iota8"][:, :],
                                                 sb["big8"][:, :]), waits=[h8])
            h10 = pb.emit("v", lambda e: e.tensor_reduce(sb["i2c"][:, :], sb["selb"][:, :], AX.X, ALU.min),
                          waits=[h9])
            h11 = pb.emit("v", lambda e: e.tensor_scalar(sb["combm"][:, 8:16], sb["iota8"][:, :], sb["i2c"][:, 0:1],
                                                         None, ALU.is_equal), waits=[h10])
            hd = pb.emit("v", lambda e: e.tensor_add(sb["dnc"][:, :], sb["m1c"][:, :], sb["m2c"][:, :]),
                         waits=[h7])
            hd2 = pb.emit("v", lambda e: e.reciprocal(out=sb["dnc"][:, :], in_=sb["dnc"][:, :]), waits=[hd])
            hw1 = pb.emit("v", lambda e: e.tensor_mul(sb["w1c"][:, :], sb["m1c"][:, :], sb["dnc"][:, :]),
                          waits=[hd2])
            hw2 = pb.emit("v", lambda e: e.tensor_mul(sb["w2c"][:, :], sb["m2c"][:, :], sb["dnc"][:, :]),
                          waits=[hd2])
            hc1 = pb.emit("v", lambda e: e.tensor_scalar(sb["selb"][:, :],
                                                         sb["combm"][:, 0:8], sb["w1c"][:, 0:1], None,
                                                         ALU.mult), waits=[h5, hw1, h10])
            hc2 = pb.emit("v", lambda e, cc=cc: e.scalar_tensor_tensor(
                sb["comb"][:, cc * 8:(cc + 1) * 8], sb["combm"][:, 8:16], sb["w2c"][:, 0:1],
                sb["selb"][:, :], ALU.mult, ALU.add), waits=[h11, hw2, hc1])
            if DEBUG and cc == 1 and router_state.get("tile") == N_WARM_TILES:
                pb.barrier()
                pb.emit("g", lambda e: e.dma_start(out=dbg["eu"][:, :], in_=sb["eu"][:, :]), dma="B")
                pb.emit("g", lambda e: e.dma_start(out=dbg["logit"][:, :], in_=sb["logit"][:, :]), dma="B")
                pb.emit("g", lambda e: e.dma_start(out=dbg["msk"][:, :], in_=sb["combm"][:, :].bitcast(F32)[:, 0:16]), dma="B")
                pb.barrier()
            bkt = banks[7]
            tr(bkt, bkt.t[0:8, 0:128], sb["comb"][:, cc * 8:(cc + 1) * 8], iden[:, :],
               waits=[hc2])
            evac("v", bkt, bkt.t[0:8, 0:128], sb["combT"][:, cc * 128:(cc + 1) * 128])

    def experts_group(g, hexp):
        contg = TRv[:, C_CONT + g * GRP: C_CONT + (g + 1) * GRP]
        cbh = sb["contb"][:, g * GRP:(g + 1) * GRP]
        cbl = sb["contb"][:, T_TILE + g * GRP: T_TILE + (g + 1) * GRP]
        prev_eo = None
        for q in range(4):
            bkA, bkB = banks[q % 2], banks[2 + q % 2]
            ehh = {}
            for j in range(8):
                for half, bk in ((0, bkA), (1, bkB)):
                    sth = wsb[half * 64:half * 64 + 64,
                              q * 1024 + j * 128: q * 1024 + (j + 1) * 128]
                    stl = wsb[half * 64:half * 64 + 64,
                              4096 + q * 1024 + j * 128: 4096 + q * 1024 + (j + 1) * 128]
                    mh = cbh[half * 64:half * 64 + 64, :]
                    ml = cbl[half * 64:half * 64 + 64, :]
                    mm3(bk, bk.t[:, 0:GRP], sth, stl, mh, ml,
                        start=True, stop=True,
                        waits=([hexp] if j == 0 else []))
                    ehh[(j, half)] = evac(
                        "s", bk, bk.t[:, 0:GRP],
                        TRv[:, C_EH + half * 4096 + j * GRP: C_EH + half * 4096 + (j + 1) * GRP],
                        func=AF.Relu,
                        bias=sb["eb1c"][:, (2 * q + half) * 8 + j:(2 * q + half) * 8 + j + 1],
                        waits=([prev_eo] if prev_eo else []))
            for half in (0, 1):
                ee = 2 * q + half
                for ci in range(CPT):
                    cc = CPT * g + ci
                    bko = banks[4 + (ci % 2)]
                    for j in range(8):
                        prev_eo = mm(bko, bko.t[:, 0:64],
                                     TRv[:, C_EH + half * 4096 + j * GRP + ci * 128:
                                         C_EH + half * 4096 + j * GRP + ci * 128 + 128],
                                     wsb_f32[:, 4096 + ee * 512 + j * 64: 4096 + ee * 512 + (j + 1) * 64],
                                     start=(j == 0), stop=(j == 7),
                                     waits=[ehh[(j, half)]])
                    moeap = TRv[:, C_MOE + cc * 64: C_MOE + (cc + 1) * 64]
                    tmp64 = TRv[:, C_EO:C_EO + 64]
                    if ee == 0:
                        rd_bank(pb, "v", bko, lambda e, bko=bko, moeap=moeap, cc=cc: e.tensor_scalar(
                            moeap, bko.t[:, 0:64], sb["comb"][:, cc * 8:cc * 8 + 1], None, ALU.mult))
                    else:
                        hta = rd_bank(pb, "v", bko, lambda e, bko=bko, tmp64=tmp64, cc=cc, ee=ee: e.tensor_scalar(
                            tmp64, bko.t[:, 0:64], sb["comb"][:, cc * 8 + ee:cc * 8 + ee + 1], None, ALU.mult))
                        pb.emit("v", lambda e, moeap=moeap, tmp64=tmp64: e.tensor_add(
                            moeap, moeap, tmp64), waits=[hta])
        for ci in range(CPT):
            cc = CPT * g + ci
            bkb = banks[6]
            mm(bkb, bkb.t[:, 0:64], sb["combT"][:, cc * 128:(cc + 1) * 128],
               sb["eb2m"][:, :], start=True, stop=True)
            hmb = rd_bank(pb, "v", bkb, lambda e, bkb=bkb, cc=cc: e.tensor_add(
                TRv[:, C_MOE + cc * 64: C_MOE + (cc + 1) * 64],
                TRv[:, C_MOE + cc * 64: C_MOE + (cc + 1) * 64], bkb.t[:, 0:64]))
            h1 = pb.emit("v", lambda e, cc=cc: e.tensor_copy(
                TRv[:, C_MTMP:C_MTMP + 64], TRv[:, C_MOE + cc * 64: C_MOE + (cc + 1) * 64]),
                waits=[hmb])
            h2 = pb.emit("v", lambda e: e.tensor_copy(
                TRv[:, C_MTMP + 64:C_MTMP + 128], TRv[:, C_MTMP:C_MTMP + 64]), waits=[h1])
            bkt = banks[7]
            trb2 = TRv[:, :].bitcast(mybir.dt.bfloat16)
            tr(bkt, bkt.t[:, 0:128], TRv[:, C_MTMP:C_MTMP + 128], iden[:, :], waits=[h2])
            mfh = trb2[:, 2 * C_MFD + cc * 128: 2 * C_MFD + (cc + 1) * 128]
            mfl = trb2[:, 2 * C_MFD + 512 + cc * 128: 2 * C_MFD + 512 + (cc + 1) * 128]
            hq1 = evac("v", bkt, bkt.t[:, 0:128], mfh)
            rd_bank(pb, "v", bkt, lambda e, mfl=mfl, bkt=bkt, mfh=mfh: e.tensor_sub(
                mfl, bkt.t[:, 0:128], mfh), waits=[hq1])

    def ffn_dec_group(g, tile_idx, hdec):
        trb = TRv[:, :].bitcast(mybir.dt.bfloat16)
        mfh_a = trb[:, 2 * C_MFD: 2 * C_MFD + GRP]
        mfl_a = trb[:, 2 * C_MFD + 512: 2 * C_MFD + 512 + GRP]
        rat_h = {}
        prev_split = {C_RS: None}
        for jp in range(8):
            j0, j1 = 2 * jp, 2 * jp + 1
            bkA, bkB = banks[jp % 2], banks[2 + jp % 2]
            for (jj, bk_, rlo) in ((j0, bkA, 0), (j1, bkB, 64)):
                mm3(bk_, bk_.t[:, 0:GRP], sb["c2srh"][rlo:rlo + 64, jj * 128:(jj + 1) * 128],
                    sb["c2srl"][rlo:rlo + 64, jj * 128:(jj + 1) * 128],
                    mfh_a[rlo:rlo + 64, :], mfl_a[rlo:rlo + 64, :],
                    start=True, stop=True)
            for (jj, bk_) in ((j0, bkA), (j1, bkB)):
                hv0 = evac("s", bk_, bk_.t[:, 0:GRP],
                           TRv[:, C_RS:C_RS + GRP],
                           func=AF.Sigmoid, bias=sb["c2sbc"][:, jj:jj + 1],
                           waits=([prev_split[C_RS]] if prev_split[C_RS] else []))
                rh_ap = trb[:, 2 * C_RAT + jj * GRP: 2 * C_RAT + (jj + 1) * GRP]
                rl_ap = trb[:, 2 * C_RAT + 8192 + jj * GRP: 2 * C_RAT + 8192 + (jj + 1) * GRP]
                hv1 = pb.emit("v", lambda e, rh_ap=rh_ap: e.tensor_copy(
                    rh_ap, TRv[:, C_RS:C_RS + GRP]), waits=[hv0])
                rat_h[jj] = pb.emit("v", lambda e, rl_ap=rl_ap, rh_ap=rh_ap: e.tensor_sub(
                    rl_ap, TRv[:, C_RS:C_RS + GRP], rh_ap), waits=[hv1])
                prev_split[C_RS] = rat_h[jj]
        hdtm = {}
        prev_tr = None
        for o in range(8):
            bk = banks[o % 4]
            for k in range(16):
                sth = wsb[:, k * 1024 + o * 128: k * 1024 + (o + 1) * 128]
                stl = wsb[:, 16384 + k * 1024 + o * 128: 16384 + k * 1024 + (o + 1) * 128]
                rh_ap = trb[:, 2 * C_RAT + k * GRP: 2 * C_RAT + (k + 1) * GRP]
                rl_ap = trb[:, 2 * C_RAT + 8192 + k * GRP: 2 * C_RAT + 8192 + (k + 1) * GRP]
                mm3(bk, bk.t[:, 0:GRP], sth, stl, rh_ap, rl_ap,
                    start=(k == 0), stop=(k == 15),
                    waits=[rat_h[k]] + ([hdec] if k == 0 else []))
            hdv = evac("s", bk, bk.t[:, 0:GRP], TRv[:, C_RS:C_RS + GRP],
                       func=AF.Sigmoid, bias=sb["decbc"][:, o:o + 1],
                       waits=([prev_tr] if prev_tr else []))
            for ci in range(CPT):
                bkt = banks[4 + (ci % 2)]
                prev_tr = tr(bkt, bkt.t[:, 0:128],
                             TRv[:, C_RS + ci * 128: C_RS + (ci + 1) * 128],
                             iden[:, :], waits=[hdv])
                hdtm[(o, ci)] = evac("v", bkt, bkt.t[:, 0:128],
                                     TRv[:, C_DTM + ci * 1024 + o * 128: C_DTM + ci * 1024 + (o + 1) * 128])
        for ci in range(CPT):
            cc = ci
            dtm = TRv[:, C_DTM + ci * 1024:C_DTM + (ci + 1) * 1024]
            hlast = hdtm[(7, ci)]
            hx = pb.emit("y", lambda e, cc=cc, tile_idx=tile_idx: e.dma_start(
                out=TRv[:, C_XR:C_XR + 1024],
                in_=din["xin"][WARM + tile_idx * T_TILE + cc * L: WARM + tile_idx * T_TILE + (cc + 1) * L, :]),
                waits=[hlast] + [h for h in (cc_prev.get("out"),) if h], dma="A")
            pb.emit("v", lambda e, dtm=dtm: e.bn_stats(sb["bnst"][:, 0:6], dtm[:, 0:512]),
                    waits=[hlast])
            pb.emit("v", lambda e, dtm=dtm: e.bn_stats(sb["bnst"][:, 6:12], dtm[:, 512:1024]))
            pb.emit("v", lambda e: e.bn_aggr(sb["bnag"][:, 0:2], sb["bnst"][:, 0:6]))
            ha2 = pb.emit("v", lambda e: e.bn_aggr(sb["bnag"][:, 2:4], sb["bnst"][:, 6:12]))
            hsm = pb.emit("v", lambda e: e.tensor_add(sb["bnag"][:, 4:6], sb["bnag"][:, 0:2],
                                                      sb["bnag"][:, 2:4]), waits=[ha2])
            hdf = pb.emit("v", lambda e: e.tensor_sub(sb["bnag"][:, 6:7], sb["bnag"][:, 0:1],
                                                      sb["bnag"][:, 2:3]), waits=[ha2])
            hmu = pb.emit("v", lambda e: e.tensor_single_scalar(sb["mucol"][:, 0:1], sb["bnag"][:, 4:5],
                                                                0.5, ALU.mult), waits=[hsm])
            hd2 = pb.emit("v", lambda e: e.tensor_mul(sb["bnag"][:, 7:8], sb["bnag"][:, 6:7],
                                                      sb["bnag"][:, 6:7]), waits=[hdf])
            hv2 = pb.emit("v", lambda e: e.scalar_tensor_tensor(sb["varcol"][:, 0:1], sb["bnag"][:, 7:8],
                                                                0.5, sb["bnag"][:, 5:6],
                                                                ALU.mult, ALU.add),
                          waits=[hd2, hsm] + [h for h in (cc_prev.get("sd"),) if h])
            hv = pb.emit("v", lambda e: e.tensor_single_scalar(sb["varcol"][:, 0:1], sb["varcol"][:, 0:1],
                                                               2e-5, ALU.add), waits=[hv2])
            hsd = pb.emit("s", lambda e: e.activation(out=sb["rstdcol"][:, 0:1], in_=sb["varcol"][:, 0:1],
                                                      func=AF.Sqrt, scale=0.5), waits=[hv])
            hrs = pb.emit("v", lambda e: e.reciprocal(out=sb["rstdcol"][:, 0:1], in_=sb["rstdcol"][:, 0:1]),
                          waits=[hsd])
            hn = pb.emit("v", lambda e, dtm=dtm: e.tensor_scalar(TRv[:, C_G:C_G + 1024], dtm,
                                                                 sb["mucol"][:, 0:1], sb["rstdcol"][:, 0:1],
                                                                 ALU.subtract, ALU.mult),
                         waits=[hrs, hmu] + [h for h in (cc_prev.get("sg"),) if h])
            hg1 = pb.emit("v", lambda e: e.tensor_mul(TRv[:, C_G:C_G + 1024], TRv[:, C_G:C_G + 1024],
                                                      sb["lngrep"][:, :]), waits=[hn])
            hg2 = pb.emit("v", lambda e: e.tensor_add(TRv[:, C_G:C_G + 1024], TRv[:, C_G:C_G + 1024],
                                                      sb["lnbrep"][:, :]), waits=[hg1])
            hsg = pb.emit("s", lambda e: e.activation(out=TRv[:, C_SG:C_SG + 1024], in_=TRv[:, C_G:C_G + 1024],
                                                      func=AF.Silu), waits=[hg2])
            hcmb = pb.emit("v", lambda e, cc=cc: e.tensor_mul(
                TRv[:, C_SG:C_SG + 1024], TRv[:, C_SG:C_SG + 1024],
                sb["stm"][:, cc * D:(cc + 1) * D]), waits=[hsg])
            hout = pb.emit("v", lambda e: e.tensor_add(TRv[:, C_OB:C_OB + 1024], TRv[:, C_XR:C_XR + 1024],
                                                       TRv[:, C_SG:C_SG + 1024]),
                           waits=[hcmb, hx] + [h for h in (cc_prev.get("dma"),) if h])
            hdma_o = pb.emit("g", lambda e, cc=cc, tile_idx=tile_idx: e.dma_start(
                out=dout[tile_idx * T_TILE + cc * L: tile_idx * T_TILE + (cc + 1) * L, :],
                in_=TRv[:, C_OB:C_OB + 1024]), waits=[hout], dma="B")
            cc_prev["out"] = hout
            cc_prev["sg"] = hsg
            cc_prev["dma"] = hdma_o
            cc_prev["sd"] = hsd

    def tap(nm, src_ap):
        if DEBUG:
            pb.emit("g", lambda e, nm=nm, src_ap=src_ap: e.dma_start(
                out=dbg[nm][:, :], in_=src_ap), dma="B")
            pb.barrier()

    prevA1 = {}
    ha_map = {}
    for tile in range(N_WARM_TILES + N_MAIN_TILES):
        warm = tile < N_WARM_TILES
        tile_idx = tile - N_WARM_TILES
        row0 = tile * T_TILE
        if tile != N_WARM_TILES:
            phase_A(row0)
        pb.barrier()
        henc = None if warm else load_enc()
        if tile == N_WARM_TILES:
            tap("xn", sb["xnfmh"][:, :4096])
        enc_state["s2c"] = None
        enc_state["cont"] = None
        snap = {}
        for c in range(CPT):
            evh, ha = proj_chunk(c, warm, snap)
            if not warm:
                enc_j(0, 4 * c + 0, henc)
            hfT, hdT, hcT, heT = decay_chunk(ha, warm, snap)
            if not warm:
                enc_j(0, 4 * c + 1, henc)
            hx = xb_transpose(c, evh, warm, snap)
            if not warm:
                enc_j(0, 4 * c + 2, henc)
            hy, hrt = scan_chunk(c, evh, hx, hfT, hdT, hcT, heT, warm, snap)
            if warm:
                snap = {"t": ("t", pb.cnt["t"]), "v": ("v", pb.cnt["v"]), "s": ("s", pb.cnt["s"])}
                emit_split_jobs(9)
                ha_map[c] = ha
                if tile == N_WARM_TILES - 1 and c >= 1:
                    phase_A_chunk(N_WARM_TILES * T_TILE, c - 1, prevA1,
                                  base=6144, bksel=(6, 7), xn_guard=ha_map[c - 1])
            else:
                enc_j(0, 4 * c + 3, henc)
                hyg = yg_chunk(c, evh, hy, hrt)
                snap = {"t": ("t", pb.cnt["t"]), "v": ("v", pb.cnt["v"]), "s": ("s", pb.cnt["s"])}
            if tile == N_WARM_TILES and c == 0:
                tap("proj", TRv[:, B_PROJ:B_PROJ + 4096])
                tap("crow", sb["crow"][:, :])
                tap("asb", sb["asb"][:, :])
                tap("ytm", TRv[:, B_YTM:B_YTM + 1024])
                tap("h1", sb["htst"][:, :])
                tap("xtm", TRv[:, B_XT:B_XT + 2048])
                pb.emit("g", lambda e: e.dma_start(out=dbg["cols"][:, 0:16], in_=sb["cT"][:, :]), dma="B") if DEBUG else None
                pb.emit("g", lambda e: e.dma_start(out=dbg["cols"][:, 16:32], in_=sb["fT"][:, :]), dma="B") if DEBUG else None
                pb.emit("g", lambda e: e.dma_start(out=dbg["cols"][:, 32:48], in_=sb["eT"][:, :]), dma="B") if DEBUG else None
                pb.emit("g", lambda e: e.dma_start(out=dbg["cols"][:, 48:64], in_=sb["dT"][:, :]), dma="B") if DEBUG else None
                pb.barrier()
        if warm:
            if tile == N_WARM_TILES - 1:
                phase_A_chunk(N_WARM_TILES * T_TILE, CPT - 1, prevA1,
                              base=6144, bksel=(6, 7), xn_guard=ha_map[CPT - 1])
                h = pb.emit("v", lambda e: e.tensor_mul(sb["htst"][:, :], sb["htst"][:, :],
                                                        sb["wmask"][:, 0:1].broadcast_to([128, 512])))
                pb.barrier()
                tap("h0", sb["htst"][:, :])
            continue
        hc = enc_state["cont"]
        hcb = pb.emit("v", lambda e: e.tensor_scalar(
            TRv[:, C_CONT:C_CONT + T_TILE], TRv[:, C_CONT:C_CONT + T_TILE],
            sb["s2cbc"][:, 0:1], None, ALU.add), waits=[hc])
        hcs = pb.emit("v", lambda e: e.tensor_copy(sb["contb"][:, 0:T_TILE],
                                                   TRv[:, C_CONT:C_CONT + T_TILE]), waits=[hcb])
        pb.emit("v", lambda e: e.tensor_sub(sb["contb"][:, T_TILE:2 * T_TILE],
                                            TRv[:, C_CONT:C_CONT + T_TILE],
                                            sb["contb"][:, 0:T_TILE]), waits=[hcs])
        pb.barrier()
        hwout = load_wout()
        wout_group(hwout, hyg)
        pb.barrier()
        if tile == N_WARM_TILES:
            tap("cont", TRv[:, C_CONT:C_CONT + 512])
            tap("stm", sb["stm"][:, :4096])
        hexp = load_experts()
        router_state["tile"] = tile
        for g in range(GPT):
            router_group(g, hcb)
            experts_group(g, hexp)
        pb.barrier()
        if tile == N_WARM_TILES:
            tap("comb", sb["comb"][:, :])
            tap("moe", TRv[:, C_MOE:C_MOE + 256])
            tap("mfd", TRv[:, C_MFD:C_MFD + 512])
        hdec = load_dec()
        for g in range(GPT):
            ffn_dec_group(g, tile_idx, hdec)
        pb.barrier()
        if tile == N_WARM_TILES:
            tap("rat", TRv[:, C_RAT:C_RAT + 4096])
            tap("dec", TRv[:, C_DTM:C_DTM + 2048])

    pb.wait_only("g", [("B", pb.cnt["B"])])

    with nc.Block() as block:
        @block.tensor
        def _(t):
            for fn in pb.prog["t"]:
                fn(t)

        @block.vector
        def _(v):
            for fn in pb.prog["v"]:
                fn(v)

        @block.scalar
        def _(s):
            for fn in pb.prog["s"]:
                fn(s)

        @block.gpsimd
        def _(g):
            for fn in pb.prog["g"]:
                fn(g)

        @block.sync
        def _(y):
            for fn in pb.prog["y"]:
                fn(y)

    es.close()
    if pb.ldwf_ins:
        import dataclasses
        targets = {id(i) for i in pb.ldwf_ins}
        npatched = 0
        for bb in nc.m.functions[0].blocks:
            il = bb.instructions
            for idx, ins0 in enumerate(il):
                if id(ins0) in targets:
                    il[idx] = dataclasses.replace(ins0, ldweights=False)
                    npatched += 1
        assert npatched == len(pb.ldwf_ins), (npatched, len(pb.ldwf_ins))
    return nc


def _layout_inputs(inp):
    f32 = np.float32
    x = np.asarray(inp["L0_attn_inp"], f32)
    wA = np.asarray(inp["L0_attn_wA"], f32)
    bA = np.asarray(inp["L0_attn_bA"], f32)
    wXBC = np.asarray(inp["L0_attn_wXBC"], f32)
    bXBC = np.asarray(inp["L0_attn_bXBC"], f32)
    wG = np.asarray(inp["L0_attn_wG"], f32)
    bG = np.asarray(inp["L0_attn_bG"], f32)
    Wout = np.asarray(inp["L0_attn_Wout"], f32)
    bout = np.asarray(inp["L0_attn_bout"], f32)
    encw = np.asarray(inp["L0_ffn_enc_w"], f32)
    encb = np.asarray(inp["L0_ffn_enc_b"], f32)
    s2cw = np.asarray(inp["L0_ffn_s2c_w"], f32)
    s2cb = np.asarray(inp["L0_ffn_s2c_b"], f32)
    rw1 = np.asarray(inp["L0_ffn_r_w1"], f32)
    rb1 = np.asarray(inp["L0_ffn_r_b1"], f32)
    rw2 = np.asarray(inp["L0_ffn_r_w2"], f32)
    rb2 = np.asarray(inp["L0_ffn_r_b2"], f32)
    ew1 = np.asarray(inp["L0_ffn_e_w1"], f32)
    eb1 = np.asarray(inp["L0_ffn_e_b1"], f32)
    ew2 = np.asarray(inp["L0_ffn_e_w2"], f32)
    eb2 = np.asarray(inp["L0_ffn_e_b2"], f32)
    c2sw = np.asarray(inp["L0_ffn_c2s_w"], f32)
    c2sb = np.asarray(inp["L0_ffn_c2s_b"], f32)
    decw = np.asarray(inp["L0_ffn_dec_w"], f32)
    decb = np.asarray(inp["L0_ffn_dec_b"], f32)
    lng = np.asarray(inp["L0_ffn_ln_g"], f32)
    lnb = np.asarray(inp["L0_ffn_ln_b"], f32)

    wblk = np.zeros((8, 128, 512), f32)
    astat = np.zeros((8, 128, 16), f32)
    bxbcg = np.zeros((128, 32), f32)
    for p in range(8):
        for half in range(2):
            h = 2 * p + half
            r = slice(half * 64, half * 64 + 64)
            cix = slice(half * 64, half * 64 + 64)
            wblk[p, r, 0:128][:, cix] = wXBC[h, 0:64, :].T
            wblk[p, r, 128:256][:, cix] = wXBC[h, 64:128, :].T
            wblk[p, r, 256:384][:, cix] = wXBC[h, 128:192, :].T
            wblk[p, r, 384:512][:, cix] = wG[h, :, :].T
            astat[p, r, h] = wA[h, 0, :]
            bxbcg[r, p * 4 + 0] = bXBC[h, 0:64]
            bxbcg[r, p * 4 + 1] = bXBC[h, 64:128]
            bxbcg[r, p * 4 + 2] = bXBC[h, 128:192]
            bxbcg[r, p * 4 + 3] = bG[h, :]

    common = {
        "wblk": wblk, "astat": astat, "bxbcg": bxbcg,
        "bacol": bA[:, 0:1].copy(),
        "woutT": np.ascontiguousarray(Wout.T),
        "boutc": np.ascontiguousarray(bout.reshape(8, 128).T),
        "encw": encw, "encbc": np.ascontiguousarray(encb.reshape(16, 128).T),
        "s2cd": np.concatenate([s2cw, s2cw], axis=1),
        "s2cbc": np.concatenate([s2cb, s2cb])[:, None].copy(),
        "rw1": rw1, "rb1c": rb1[:, None].copy(), "rw2": rw2,
        "rb2rep": np.broadcast_to(rb2[None, :], (128, 8)).copy(),
        "ew1p": np.concatenate([ew1[0::2], ew1[1::2]], axis=1),
        "eb1c": np.ascontiguousarray(
            eb1.reshape(8, 8, 128).transpose(2, 0, 1).reshape(128, 64)),
        "ew2s": ew2, "eb2m": eb2,
        "c2sr": np.concatenate([c2sw, c2sw], axis=0),
        "c2sbc": np.ascontiguousarray(c2sb.reshape(16, 128).T),
        "decw": decw, "decbc": np.ascontiguousarray(decb.reshape(8, 128).T),
        "lngrep": np.broadcast_to(lng[None, :], (128, D)).copy(),
        "lnbrep": np.broadcast_to(lnb[None, :], (128, D)).copy(),
        "iden": np.eye(128, dtype=f32),
        "tripos": np.where(np.arange(128)[None, :] >= np.arange(128)[:, None],
                           0.0, 1e30).astype(f32),
        "tripost": np.ascontiguousarray(
            np.where(np.arange(128)[None, :] >= np.arange(128)[:, None],
                     0.0, 1e30).astype(f32).T),
        "iota8": np.broadcast_to(np.arange(8, dtype=f32)[None, :], (128, 8)).copy(),
        "big8": np.full((128, 8), 1e30, f32),
        "neg1t": np.full((128, 8), -1.0, f32),
        "ones16": np.ones((16, 128), f32),
    }

    in_maps = []
    for core in range(NCORE):
        b, half = core // 2, core % 2
        m0 = half * T_CORE
        w0 = m0 - WARM if half == 1 else 0
        xin = np.concatenate([x[b, w0:w0 + WARM, :], x[b, m0:m0 + T_CORE, :]],
                             axis=0)
        im = dict(common)
        im["xin"] = np.ascontiguousarray(xin)
        im["wmask"] = np.full((128, 1), 1.0 if half == 1 else 0.0, f32)
        in_maps.append(im)
    return in_maps


def kernel(**inputs):
    in_maps = _layout_inputs(inputs)
    nc = build_nc()
    res = run_bass_kernel_spmd(nc, in_maps, core_ids=list(range(NCORE)))
    out = np.empty((4, 4096, D), np.float32)
    for core in range(NCORE):
        b, half = core // 2, core % 2
        out[b, half * T_CORE:(half + 1) * T_CORE, :] = res.results[core]["out"]
    return out
